# version 05
# baseline (speedup 1.0000x reference)
import numpy as np
from contextlib import ExitStack
import concourse.bass as bass
import concourse.mybir as mybir
from concourse.bass_utils import run_bass_kernel_spmd

F32 = mybir.dt.float32
BF16 = mybir.dt.bfloat16
I32 = mybir.dt.int32
OP = mybir.AluOpType
ACT = mybir.ActivationFunctionType
AX = mybir.AxisListType

B, S, D, H, HD = 2, 4096, 2048, 16, 128
INTER = 5504
NT = S // 128
NDC = D // 128
NQ = 4
TQ = S // NQ
NTQ = TQ // 128
JSL = 1408
NJT = JSL // 128
EH = 4 * 3 * HD
NORM32 = float(np.float32(np.sqrt(np.float32(128.0))))
C1 = float(np.float32(2 * np.pi))
C2 = float(np.float32(2 * np.pi - np.float64(np.float32(2 * np.pi))))
EPS = 1e-5
INV_D = 1.0 / float(D)

_CACHE = {}


def _host_tables():
    if "t" in _CACHE:
        return _CACHE["t"]
    import jax
    import jax.numpy as jnp
    with jax.default_device(jax.devices("cpu")[0]):
        inv_freq = 1.0 / 10000.0 ** (jnp.arange(0, HD, 2, dtype=jnp.float32) / HD)
        t = jnp.arange(S, dtype=jnp.float32)
        freqs = jnp.einsum("i,j->ij", t, inv_freq)
        emb32 = np.asarray(freqs)
    emb64 = emb32.astype(np.float64)
    ks = np.round(emb64 / (2 * np.pi))
    kc = np.round(emb64 / (2 * np.pi) + 0.25) - 0.25
    assert np.all(np.abs(ks) < 2 ** 11) and np.all(np.abs(kc * 4) % 1 == 0)

    def dev_layout(a):
        return np.ascontiguousarray(
            a.reshape(NT, 128, 64).transpose(1, 0, 2).reshape(128, NT * 64)
        ).astype(np.float32)

    ident = np.eye(128, dtype=np.float32)
    maskA = np.zeros((128, 4 * 512), dtype=np.float32)
    for r in range(4):
        q_i = np.arange(128)[:, None]
        k_i = np.arange(512)[None, :]
        maskA[:, 512 * r:512 * (r + 1)] = np.where(k_i <= 128 * r + q_i,
                                                   0.0, -1.0e30)
    _CACHE["t"] = dict(
        embT=dev_layout(emb32), ksT=dev_layout(ks), kcT=dev_layout(kc),
        ident=ident, maskA=maskA,
    )
    return _CACHE["t"]


def _jslices():
    out = []
    for g in range(NQ):
        o0 = 1408 * g
        osz = 1408 if g < 3 else INTER - 1408 * 3
        out.append((o0, osz))
    return out


class Chan:

    def __init__(self, nc, stack, name):
        self.h = stack.enter_context(nc.semaphore(name))
        self.n = 0

    def inc(self, instr, k=1):
        instr.then_inc(self.h, k)
        self.n += k
        return (self.h, self.n)

    def ev(self):
        return (self.h, self.n)


def _wait(eng, ev):
    if ev is not None:
        eng.wait_ge(ev[0], ev[1])



def emit_mod_phase(nc, block, st, ch, cCol, wmod_dram, mod_out, rowM, psM,
                   ev_pre=None):
    T, V, SC = ch["T"], ch["V"], ch["SC"]
    scS = st.enter_context(nc.sbuf_tensor("scS", [128, 16], F32))
    wmW = st.enter_context(nc.sbuf_tensor("wmW", [128, 2 * 16 * 512], F32))

    def sc_fn(scalar):
        _wait(scalar, ch["LD0"].ev())
        SC.inc(scalar.activation(scS[:], cCol[:], ACT.Silu))
    block.scalar(sc_fn)
    ev_silu = SC.ev()

    njc = 12288 // 512
    evs_v = []
    evs_st = []
    for jc in range(njc):
        par = jc % 2

        def ld_fn(sync, jc=jc, par=par):
            if jc >= 2:
                _wait(sync, evs_v[jc - 2])
            else:
                _wait(sync, ev_pre)
            for dc in range(NDC):
                ch["LW%d" % par].inc(
                    sync.dma_start(
                        out=wmW[:, (par * 16 + dc) * 512:(par * 16 + dc) * 512 + 512],
                        in_=wmod_dram[128 * dc:128 * dc + 128, 512 * jc:512 * jc + 512],
                    ), 16)
        block.sync(ld_fn)
        ev_ld = ch["LW%d" % par].ev()

        def t_fn(tensor, jc=jc, par=par, ev_ld=ev_ld):
            _wait(tensor, ev_ld)
            _wait(tensor, ev_silu)
            if jc >= 1:
                _wait(tensor, evs_v[jc - 1])
            last = None
            for dc in range(NDC):
                last = tensor.matmul(psM[0:1, 0:512], scS[:, dc:dc + 1],
                                     wmW[:, (par * 16 + dc) * 512:(par * 16 + dc) * 512 + 512],
                                     start=(dc == 0), stop=(dc == NDC - 1))
            T.inc(last)
        block.tensor(t_fn)
        ev_mm = T.ev()

        def v_fn(vector, jc=jc, par=par, ev_mm=ev_mm):
            _wait(vector, ev_mm)
            if jc >= 2:
                _wait(vector, evs_st[jc - 2])
            V.inc(vector.tensor_copy(rowM[0:1, 512 * par:512 * par + 512],
                                     psM[0:1, 0:512]))
        block.vector(v_fn)
        evs_v.append(V.ev())

        def g_fn(gpsimd, jc=jc, par=par, ev_v=evs_v[-1]):
            _wait(gpsimd, ev_v)
            ch["SO"].inc(gpsimd.dma_start(
                out=mod_out[0:1, 512 * jc:512 * jc + 512],
                in_=rowM[0:1, 512 * par:512 * par + 512]), 16)
        block.gpsimd(g_fn)
        evs_st.append(ch["SO"].ev())
    return evs_st[-1]


def emit_clamp_row(nc, block, ch, modrow, off, outR, tmpR):
    V = ch["V"]

    def v_fn(vector):
        g = modrow[0:1, off:off + D]
        vector.tensor_scalar(tmpR[0:1, 0:D], g, 0.0, 2.0, OP.is_ge, OP.mult)
        vector.tensor_scalar_add(tmpR[0:1, 0:D], tmpR[0:1, 0:D], -1.0)
        vector.tensor_scalar_mul(outR[0:1, 0:D], g, -1.0)
        vector.tensor_max(outR[0:1, 0:D], g, outR[0:1, 0:D])
        vector.tensor_scalar_max(outR[0:1, 0:D], outR[0:1, 0:D], 0.25)
        V.inc(vector.tensor_mul(outR[0:1, 0:D], outR[0:1, 0:D], tmpR[0:1, 0:D]))
    block.vector(v_fn)
    return V.ev()


def emit_bcast_row(nc, block, ch, onesR, srcR, s_off, dstB, width, psB, ev_pre=None):
    T, V = ch["T"], ch["V"]
    nch = (width + 511) // 512
    for k in range(nch):
        w = min(512, width - 512 * k)
        ev_vpre = ch["V"].ev()

        def t_fn(tensor, k=k, w=w, ev_vpre=ev_vpre):
            _wait(tensor, ev_pre)
            _wait(tensor, ev_vpre)
            T.inc(tensor.matmul(psB[:, 0:w], onesR[0:1, 0:128],
                                srcR[0:1, s_off + 512 * k:s_off + 512 * k + w],
                                start=True, stop=True))
        block.tensor(t_fn)
        ev_mm = T.ev()

        def v_fn(vector, k=k, w=w, ev_mm=ev_mm):
            _wait(vector, ev_mm)
            V.inc(vector.tensor_copy(dstB[:, 512 * k:512 * k + w], psB[:, 0:w]))
        block.vector(v_fn)
    return V.ev()


def emit_ln_h(nc, block, ch, xbuf, stats, xc, dump, hout, scaleB, shiftB, sc_ch,
              ev_sq=None):
    V, SC = ch["V"], ch["SC"]
    ev_xc = ch["SC"].ev()

    def v1(vector):
        _wait(vector, ev_xc)
        V.inc(vector.reduce_sum(stats[:, 0:1], xbuf[:], axis=AX.X))
        _wait(vector, V.ev())
        V.inc(vector.tensor_scalar_mul(stats[:, 1:2], stats[:, 0:1], INV_D))
        _wait(vector, V.ev())
        V.inc(vector.tensor_scalar(xc[:], xbuf[:], stats[:, 1:2], None, OP.subtract))
    block.vector(v1)
    ev1 = V.ev()

    def s1(scalar):
        _wait(scalar, ev1)
        _wait(scalar, ev_sq)
        SC.inc(scalar.activation(dump[:], xc[:], ACT.Square))
    block.scalar(s1)
    ev2 = SC.ev()

    def v2(vector):
        _wait(vector, ev2)
        V.inc(vector.reduce_sum(stats[:, 2:3], dump[:], axis=AX.X))
        _wait(vector, V.ev())
        V.inc(vector.tensor_scalar_mul(stats[:, 3:4], stats[:, 2:3], INV_D))
    block.vector(v2)
    ev3 = V.ev()

    def s2(scalar):
        _wait(scalar, ev3)
        SC.inc(scalar.activation(stats[:, 4:5], stats[:, 3:4], ACT.Sqrt,
                                 bias=stats[:, 15:16]))
    block.scalar(s2)
    ev4 = SC.ev()

    def v3(vector):
        _wait(vector, ev4)
        V.inc(vector.reciprocal(stats[:, 5:6], stats[:, 4:5]))
        _wait(vector, V.ev())
        vector.scalar_tensor_tensor(hout[:], xc[:], stats[:, 5:6], scaleB[:],
                                    OP.mult, OP.mult)
        V.inc(vector.tensor_add(hout[:], hout[:], shiftB[:]))
    block.vector(v3)
    return V.ev()


def emit_transpose_split(nc, block, ch, src_f32, ncols, ident, psT, tmpf,
                         dst_hi, dst_lo, dst_off, ev_pre=None, scale_ap=None):
    T, V = ch["T"], ch["V"]
    nt = ncols // 128
    for g0 in range(0, nt, 4):
        gn = min(4, nt - g0)

        def t_fn(tensor, g0=g0, gn=gn):
            _wait(tensor, ev_pre)
            if g0 > 0:
                _wait(tensor, ch["V"].ev() if False else evs[-1])
            last = None
            for k in range(gn):
                last = tensor.transpose(psT[:, 128 * k:128 * k + 128],
                                        src_f32[:, (g0 + k) * 128:(g0 + k) * 128 + 128],
                                        ident[:])
            T.inc(last)
        if g0 == 0:
            evs = []
        block.tensor(t_fn)
        ev_t = T.ev()

        def v_fn(vector, g0=g0, gn=gn, ev_t=ev_t):
            _wait(vector, ev_t)
            w = 128 * gn
            o = dst_off + 128 * g0
            if scale_ap is not None:
                vector.tensor_scalar(tmpf[:, 0:w], psT[:, 0:w], scale_ap, None, OP.mult)
                vector.tensor_copy(dst_hi[:, o:o + w], tmpf[:, 0:w])
                vector.tensor_tensor(tmpf[:, 0:w], tmpf[:, 0:w], dst_hi[:, o:o + w],
                                     OP.subtract)
            else:
                vector.tensor_copy(dst_hi[:, o:o + w], psT[:, 0:w])
                vector.tensor_tensor(tmpf[:, 0:w], psT[:, 0:w], dst_hi[:, o:o + w],
                                     OP.subtract)
            V.inc(vector.tensor_copy(dst_lo[:, o:o + w], tmpf[:, 0:w]))
        block.vector(v_fn)
        evs.append(V.ev())
    return V.ev()


def emit_split_hilo(nc, block, ch, src, dst_hi, dst_lo, tmpf, w, ev_pre=None):
    V = ch["V"]

    def v_fn(vector):
        _wait(vector, ev_pre)
        vector.tensor_copy(dst_hi[:, 0:w], src[:, 0:w])
        vector.tensor_tensor(tmpf[:, 0:w], src[:, 0:w], dst_hi[:, 0:w], OP.subtract)
        V.inc(vector.tensor_copy(dst_lo[:, 0:w], tmpf[:, 0:w]))
    block.vector(v_fn)
    return V.ev()



def build_L1(dbg=False):
    nc = bass.Bass("TRN2", target_bir_lowering=False)
    xb = nc.dram_tensor("xb", [S, D], F32, kind="ExternalInput")
    wq = nc.dram_tensor("wq", [EH, D], F32, kind="ExternalInput")
    bq = nc.dram_tensor("bq", [1, EH], F32, kind="ExternalInput")
    cCol_d = nc.dram_tensor("cCol", [128, 16], F32, kind="ExternalInput")
    wmod = nc.dram_tensor("wmod", [D, 6 * D], F32, kind="ExternalInput")
    embT_d = nc.dram_tensor("embT", [128, NT * 64], F32, kind="ExternalInput")
    ksT_d = nc.dram_tensor("ksT", [128, NT * 64], F32, kind="ExternalInput")
    kcT_d = nc.dram_tensor("kcT", [128, NT * 64], F32, kind="ExternalInput")
    ident_d = nc.dram_tensor("ident", [128, 128], F32, kind="ExternalInput")
    maskA_d = nc.dram_tensor("maskA", [128, 4 * 512], F32, kind="ExternalInput")

    O_out = nc.dram_tensor("O_out", [S, 512], F32, kind="ExternalOutput")
    mod_out = nc.dram_tensor("mod_out", [1, 6 * D], F32, kind="ExternalOutput")
    kw = dict(kind="ExternalOutput") if dbg else {}
    h1T_hi_d = nc.dram_tensor("h1T_hi", [NT, 128, D], BF16, **kw)
    h1T_lo_d = nc.dram_tensor("h1T_lo", [NT, 128, D], BF16, **kw)
    if dbg:
        qT_dbg = nc.dram_tensor("qT_dbg", [128, S], BF16, **kw)
        kT_dbg = nc.dram_tensor("kT_dbg", [128, S], BF16, **kw)
        v_dbg = nc.dram_tensor("v_dbg", [128, S], BF16, **kw)

    with ExitStack() as st:
        E = st.enter_context
        ch = {}
        for name in ["T", "V", "SC", "LD0", "LW0", "LW1", "LX0", "LX1",
                     "LH0", "LH1", "SO", "SH0", "SH1", "G"]:
            ch[name] = Chan(nc, st, "c_" + name)
        ch["LD"] = ch["LD0"]
        T, V, SC = ch["T"], ch["V"], ch["SC"]
        block = E(nc.Block())

        cCol = E(nc.sbuf_tensor("cCols", [128, 16], F32))
        ident = E(nc.sbuf_tensor("idents", [128, 128], F32))
        maskA = E(nc.sbuf_tensor("maskAs", [128, 4 * 512], F32))
        cosT = E(nc.sbuf_tensor("cosT", [128, NT * 64], F32))
        sinT = E(nc.sbuf_tensor("sinT", [128, NT * 64], F32))
        rowA = E(nc.sbuf_tensor("rowA", [1, D], F32))
        rowS = E(nc.sbuf_tensor("rowS", [1, D], F32))
        rowM = E(nc.sbuf_tensor("rowM", [1, 1024], F32))
        onesR = E(nc.sbuf_tensor("onesR", [1, 128], F32))
        caB = E(nc.sbuf_tensor("caB", [128, D], F32))
        saB = E(nc.sbuf_tensor("saB", [128, D], F32))
        bqB = E(nc.sbuf_tensor("bqB", [128, EH], F32))
        invB = E(nc.sbuf_tensor("invB", [128, 1], F32))
        nrm = E(nc.sbuf_tensor("nrm", [1, 1], F32))
        stats = E(nc.sbuf_tensor("stats", [128, 16], F32))
        tmpf = E(nc.sbuf_tensor("tmpf", [128, 512], F32))
        dump = E(nc.sbuf_tensor("dump", [128, 512], F32))

        psA = E(nc.psum_tensor("psA", [128, 512], F32))
        psO = E(nc.psum_tensor("psO", [128, 512], F32))
        psT_ = E(nc.psum_tensor("psT", [128, 512], F32))
        psQ = E(nc.psum_tensor("psQ", [128, 512], F32))
        psM = E(nc.psum_tensor("psM", [128, 512], F32))
        psB = E(nc.psum_tensor("psB", [128, 512], F32))

        with ExitStack() as st_rope:
            ER = st_rope.enter_context
            embT = ER(nc.sbuf_tensor("embTs", [128, NT * 64], F32))
            ksT = ER(nc.sbuf_tensor("ksTs", [128, NT * 64], F32))
            kcT = ER(nc.sbuf_tensor("kcTs", [128, NT * 64], F32))
            rtmp = ER(nc.sbuf_tensor("rtmp", [128, NT * 64], F32))

            def ld0(sync):
                for dst, src in [(cCol, cCol_d), (ident, ident_d), (maskA, maskA_d),
                                 (embT, embT_d), (ksT, ksT_d), (kcT, kcT_d)]:
                    ch["LD0"].inc(sync.dma_start(out=dst[:], in_=src[:]), 16)
                ch["LD0"].inc(sync.dma_start(out=bqB[0:1, :], in_=bq[:]), 16)
            block.sync(ld0)
            ev_ld0 = ch["LD0"].ev()

            def v0(vector):
                vector.memset(onesR[:], 1.0)
                vector.memset(stats[:, 15:16], EPS)
                V.inc(vector.memset(nrm[:], NORM32))
                _wait(vector, V.ev())
                V.inc(vector.reciprocal(nrm[0:1, 0:1], nrm[0:1, 0:1]))
            block.vector(v0)
            ev_ones = V.ev()

            def v_rope(vector):
                _wait(vector, ev_ld0)
                vector.tensor_scalar(rtmp[:], ksT[:], C1, None, OP.mult)
                vector.tensor_tensor(cosT[:], embT[:], rtmp[:], OP.subtract)
                vector.tensor_scalar(rtmp[:], ksT[:], C2, None, OP.mult)
                vector.tensor_tensor(cosT[:], cosT[:], rtmp[:], OP.subtract)
                vector.tensor_scalar(rtmp[:], kcT[:], C1, None, OP.mult)
                vector.tensor_tensor(sinT[:], embT[:], rtmp[:], OP.subtract)
                vector.tensor_scalar(rtmp[:], kcT[:], C2, None, OP.mult)
                V.inc(vector.tensor_tensor(sinT[:], sinT[:], rtmp[:], OP.subtract))
            block.vector(v_rope)
            ev_r = V.ev()

            def s_rope(scalar):
                _wait(scalar, ev_r)
                scalar.activation(cosT[:], cosT[:], ACT.Sin)
                SC.inc(scalar.activation(sinT[:], sinT[:], ACT.Sin))
            block.scalar(s_rope)
            ev_tables = SC.ev()
        sinTab, cosTab = cosT, sinT

        def t_inv(tensor):
            _wait(tensor, ev_ones)
            T.inc(tensor.matmul(psB[:, 0:1], onesR[0:1, 0:128], nrm[0:1, 0:1],
                                start=True, stop=True))
        block.tensor(t_inv)
        ev_ti = T.ev()

        def v_inv(vector):
            _wait(vector, ev_ti)
            V.inc(vector.tensor_copy(invB[:], psB[:, 0:1]))
        block.vector(v_inv)

        with ExitStack() as st_mod:
            ev_mod = emit_mod_phase(nc, block, st_mod, ch, cCol, wmod, mod_out,
                                    rowM, psM, ev_pre=ev_tables)

        def ld_rows(sync):
            _wait(sync, ev_mod)
            ch["LD0"].inc(sync.dma_start(out=rowA[0:1, 0:D],
                                         in_=mod_out[0:1, D:2 * D]), 16)
            ch["LD0"].inc(sync.dma_start(out=rowS[0:1, 0:D],
                                         in_=mod_out[0:1, 0:D]), 16)
        block.sync(ld_rows)
        ev_rows = ch["LD0"].ev()

        def v_rows(vector):
            _wait(vector, ev_rows)
            V.inc(vector.tensor_scalar_add(rowA[0:1, 0:D], rowA[0:1, 0:D], 1.0))
        block.vector(v_rows)
        ev_ca1 = V.ev()
        ev_caB = emit_bcast_row(nc, block, ch, onesR, rowA, 0, caB, D, psB, ev_ca1)
        ev_saB = emit_bcast_row(nc, block, ch, onesR, rowS, 0, saB, D, psB, ev_rows)
        ev_bqB = emit_bcast_row(nc, block, ch, onesR, bqB, 0, bqB, EH, psB, ev_ld0)

        with ExitStack() as st_ln:
            E2 = st_ln.enter_context
            xbuf = [E2(nc.sbuf_tensor("xb%d" % i, [128, D], F32)) for i in range(2)]
            xc = E2(nc.sbuf_tensor("xc", [128, D], F32))
            hbuf = E2(nc.sbuf_tensor("hbuf", [128, D], F32))
            hTs = [(E2(nc.sbuf_tensor("hTh%d" % i, [128, D], BF16)),
                    E2(nc.sbuf_tensor("hTl%d" % i, [128, D], BF16))) for i in range(2)]
            evs_cons = []
            evs_hT = []
            for tt in range(NT):
                par = tt % 2

                def ldx(sync, tt=tt, par=par):
                    if tt >= 2:
                        _wait(sync, evs_cons[tt - 2])
                    else:
                        _wait(sync, ev_caB)
                    ch["LX%d" % par].inc(
                        sync.dma_start(out=xbuf[par][:],
                                       in_=xb[128 * tt:128 * tt + 128, :]), 16)
                block.sync(ldx)
                ev_x = ch["LX%d" % par].ev()

                def vw(vector, par=par, ev_x=ev_x, tt=tt):
                    _wait(vector, ev_x)
                    if tt == 0:
                        _wait(vector, ev_caB)
                        _wait(vector, ev_saB)
                block.vector(vw)
                ev_h = emit_ln_h(nc, block, ch, xbuf[par], stats, xc, hbuf, hbuf,
                                 caB, saB, SC, ev_sq=ch["T"].ev())
                evs_cons.append(ev_h)
                def w_guard(vector, tt=tt, par=par):
                    if tt >= 2:
                        _wait(vector, evs_st[tt - 2])
                if tt >= 2:
                    block.vector(lambda vector, tt=tt, par=par: _wait(vector, evs_st[tt - 2]))
                else:
                    evs_st = []
                ev_sp = emit_transpose_split(nc, block, ch, hbuf, D, ident, psT_,
                                             tmpf, hTs[par][0], hTs[par][1], 0,
                                             ev_pre=ev_h)
                def sh(gpsimd, tt=tt, par=par, ev_sp=ev_sp):
                    _wait(gpsimd, ev_sp)
                    ch["SH%d" % par].inc(
                        gpsimd.dma_start(out=h1T_hi_d[tt, :, :], in_=hTs[par][0][:]), 16)
                    ch["SH%d" % par].inc(
                        gpsimd.dma_start(out=h1T_lo_d[tt, :, :], in_=hTs[par][1][:]), 16)
                block.gpsimd(sh)
                evs_st.append(ch["SH%d" % par].ev())

        ev_allst = [ch["SH0"].ev(), ch["SH1"].ev()]

        with ExitStack() as st_at:
            E3 = st_at.enter_context
            wqNat = E3(nc.sbuf_tensor("wqNat", [128, D], F32))
            wqT_hi = E3(nc.sbuf_tensor("wqT_hi", [128, NDC * 384], BF16))
            wqT_lo = E3(nc.sbuf_tensor("wqT_lo", [128, NDC * 384], BF16))
            hh = [E3(nc.sbuf_tensor("hh%d" % i, [128, D], BF16)) for i in range(2)]
            hl = [E3(nc.sbuf_tensor("hl%d" % i, [128, D], BF16)) for i in range(2)]
            qkvS = E3(nc.sbuf_tensor("qkvS", [128, 384], F32))
            qrot = E3(nc.sbuf_tensor("qrot", [128, 256], F32))
            qT_hi = E3(nc.sbuf_tensor("qT_hi", [128, S], BF16))
            qT_lo = E3(nc.sbuf_tensor("qT_lo", [128, S], BF16))
            kT_hi = E3(nc.sbuf_tensor("kT_hi", [128, S], BF16))
            kT_lo = E3(nc.sbuf_tensor("kT_lo", [128, S], BF16))
            v_hi = E3(nc.sbuf_tensor("v_hi", [128, S], BF16))
            v_lo = E3(nc.sbuf_tensor("v_lo", [128, S], BF16))
            sc_buf = E3(nc.sbuf_tensor("sc_buf", [128, S], F32))
            p32e = E3(nc.sbuf_tensor("p32e", [128, 512], F32))
            PT_hi = E3(nc.sbuf_tensor("PT_hi", [128, 512], BF16))
            PT_lo = E3(nc.sbuf_tensor("PT_lo", [128, 512], BF16))
            mxc = E3(nc.sbuf_tensor("mxc", [128, 16], F32))
            smc = E3(nc.sbuf_tensor("smc", [128, 16], F32))
            mneg = E3(nc.sbuf_tensor("mneg", [128, 1], F32))
            rT = E3(nc.sbuf_tensor("rT", [128, 1], F32))
            OTs = E3(nc.sbuf_tensor("OTs", [128, 128], F32))
            Ofin = E3(nc.sbuf_tensor("Ofin", [128, 128], F32))

            ev_headdone = None
            evs_evc = []
            for hd_i in range(4):
                for et in range(3):
                    def ldw(sync, hd_i=hd_i, et=et):
                        if et == 0 and hd_i == 0:
                            _wait(sync, ev_allst[0])
                            _wait(sync, ev_allst[1])
                        elif et == 0:
                            _wait(sync, ev_headdone)
                        else:
                            _wait(sync, evs_wt[-1])
                        ch["LD0"].inc(sync.dma_start(
                            out=wqNat[:],
                            in_=wq[384 * hd_i + 128 * et:
                                   384 * hd_i + 128 * et + 128, :]), 16)
                    block.sync(ldw)
                    ev_wq = ch["LD0"].ev()
                    for g0 in range(0, NDC, 4):
                        def t_tw(tensor, g0=g0, ev_wq=ev_wq, first=(et == 0 and
                                 g0 == 0 and hd_i == 0)):
                            _wait(tensor, ev_wq)
                            if not first:
                                _wait(tensor, evs_wt[-1])
                            last = None
                            for k in range(4):
                                last = tensor.transpose(
                                    psT_[:, 128 * k:128 * k + 128],
                                    wqNat[:, (g0 + k) * 128:(g0 + k) * 128 + 128],
                                    ident[:])
                            T.inc(last)
                        if et == 0 and g0 == 0 and hd_i == 0:
                            evs_wt = []
                        block.tensor(t_tw)
                        ev_t = T.ev()

                        def v_tw(vector, et=et, g0=g0, ev_t=ev_t):
                            _wait(vector, ev_t)
                            for k in range(4):
                                dc = g0 + k
                                o = 384 * dc + 128 * et
                                vector.tensor_copy(wqT_hi[:, o:o + 128],
                                                   psT_[:, 128 * k:128 * k + 128])
                                vector.tensor_tensor(tmpf[:, 0:128],
                                                     psT_[:, 128 * k:128 * k + 128],
                                                     wqT_hi[:, o:o + 128], OP.subtract)
                                vector.tensor_copy(wqT_lo[:, o:o + 128], tmpf[:, 0:128])
                            V.inc(vector.engine_nop())
                        block.vector(v_tw)
                        evs_wt.append(V.ev())
                ev_wqT = V.ev()

                evs_qkv_done = []
                for tt in range(NT):
                    par = tt % 2

                    def ldh(sync, tt=tt, par=par, hd_i=hd_i):
                        if tt >= 2:
                            _wait(sync, evs_qkv_done[tt - 2])
                        elif hd_i == 0:
                            _wait(sync, ev_allst[0])
                            _wait(sync, ev_allst[1])
                        ch["LH%d" % par].inc(sync.dma_start(
                            out=hh[par][:], in_=h1T_hi_d[tt, :, :]), 16)
                        ch["LH%d" % par].inc(sync.dma_start(
                            out=hl[par][:], in_=h1T_lo_d[tt, :, :]), 16)
                    block.sync(ldh)
                    ev_h1 = ch["LH%d" % par].ev()

                    def t_qkv(tensor, tt=tt, par=par, ev_h1=ev_h1):
                        _wait(tensor, ev_h1)
                        _wait(tensor, ev_wqT)
                        if tt >= 1:
                            _wait(tensor, evs_evac[-1])
                        last = None
                        for dc in range(NDC):
                            a = hh[par][:, 128 * dc:128 * dc + 128]
                            b_ = hl[par][:, 128 * dc:128 * dc + 128]
                            wh = wqT_hi[:, 384 * dc:384 * dc + 384]
                            wl = wqT_lo[:, 384 * dc:384 * dc + 384]
                            last = tensor.matmul(psQ[:, 0:384], a, wh,
                                                 start=(dc == 0), stop=False)
                            last = tensor.matmul(psQ[:, 0:384], a, wl,
                                                 start=False, stop=False)
                            last = tensor.matmul(psQ[:, 0:384], b_, wh,
                                                 start=False,
                                                 stop=(dc == NDC - 1))
                        T.inc(last)
                    if tt == 0:
                        evs_evac = []
                    block.tensor(t_qkv)
                    ev_mm = T.ev()

                    def v_ev(vector, tt=tt, hd_i=hd_i, ev_mm=ev_mm):
                        _wait(vector, ev_mm)
                        vector.tensor_tensor(qkvS[:], psQ[:, 0:384],
                                             bqB[:, 384 * hd_i:384 * hd_i + 384], OP.add)
                        cosL = cosTab[:, 64 * tt:64 * tt + 64]
                        sinL = sinTab[:, 64 * tt:64 * tt + 64]
                        for base in (0, 128):
                            src = qkvS[:, base:base + 128]
                            vector.tensor_mul(dump[:, 0:64], src[:, 0:64], cosL)
                            vector.tensor_mul(dump[:, 64:128], src[:, 64:128], cosL)
                            vector.tensor_mul(dump[:, 128:192], src[:, 64:128], sinL)
                            vector.tensor_mul(dump[:, 192:256], src[:, 0:64], sinL)
                            vector.tensor_tensor(qrot[:, base:base + 64],
                                                 dump[:, 0:64], dump[:, 128:192],
                                                 OP.subtract)
                            vector.tensor_tensor(qrot[:, base + 64:base + 128],
                                                 dump[:, 64:128], dump[:, 192:256],
                                                 OP.add)
                        vector.tensor_copy(v_hi[:, 128 * tt:128 * tt + 128],
                                           qkvS[:, 256:384])
                        vector.tensor_tensor(tmpf[:, 0:128], qkvS[:, 256:384],
                                             v_hi[:, 128 * tt:128 * tt + 128],
                                             OP.subtract)
                        V.inc(vector.tensor_copy(v_lo[:, 128 * tt:128 * tt + 128],
                                                 tmpf[:, 0:128]))
                    block.vector(v_ev)
                    ev_rope = V.ev()
                    evs_evac.append(ev_rope)

                    def t_tq(tensor, tt=tt, ev_rope=ev_rope):
                        _wait(tensor, ev_rope)
                        if tt >= 1:
                            _wait(tensor, evs_qkv_done[-1])
                        tensor.transpose(psT_[:, 0:128], qrot[:, 0:128], ident[:])
                        T.inc(tensor.transpose(psT_[:, 128:256], qrot[:, 128:256],
                                               ident[:]))
                    block.tensor(t_tq)
                    ev_tq = T.ev()

                    def v_tq(vector, tt=tt, ev_tq=ev_tq):
                        _wait(vector, ev_tq)
                        c = 128 * tt
                        vector.tensor_scalar(tmpf[:, 0:128], psT_[:, 0:128], invB[:, 0:1],
                                             None, OP.mult)
                        vector.tensor_copy(qT_hi[:, c:c + 128], tmpf[:, 0:128])
                        vector.tensor_tensor(tmpf[:, 128:256], tmpf[:, 0:128],
                                             qT_hi[:, c:c + 128], OP.subtract)
                        vector.tensor_copy(qT_lo[:, c:c + 128], tmpf[:, 128:256])
                        vector.tensor_copy(kT_hi[:, c:c + 128], psT_[:, 128:256])
                        vector.tensor_tensor(tmpf[:, 128:256], psT_[:, 128:256],
                                             kT_hi[:, c:c + 128], OP.subtract)
                        V.inc(vector.tensor_copy(kT_lo[:, c:c + 128], tmpf[:, 128:256]))
                    block.vector(v_tq)
                    evs_qkv_done.append(V.ev())
                ev_qkvall = V.ev()

                if dbg and hd_i == 0:
                    def g_dbg(gpsimd, ev_qkvall=ev_qkvall):
                        _wait(gpsimd, ev_qkvall)
                        ch["SO"].inc(gpsimd.dma_start(out=qT_dbg[:], in_=qT_hi[:]), 16)
                        ch["SO"].inc(gpsimd.dma_start(out=kT_dbg[:], in_=kT_hi[:]), 16)
                        ch["SO"].inc(gpsimd.dma_start(out=v_dbg[:], in_=v_hi[:]), 16)
                    block.gpsimd(g_dbg)

                for qt in range(NT):
                    nkc = qt // 4 + 1
                    r = qt % 4
                    ev_sc_free = ch["SC"].ev()
                    for kc in range(nkc):
                        diag = (kc == nkc - 1)

                        def t_sc(tensor, qt=qt, kc=kc, ev_qkvall=ev_qkvall):
                            _wait(tensor, ev_qkvall)
                            if evs_evc:
                                _wait(tensor, evs_evc[-1])
                            q0 = 128 * qt
                            c = 512 * kc
                            tensor.matmul(psA[:], qT_hi[:, q0:q0 + 128],
                                          kT_hi[:, c:c + 512], start=True, stop=False)
                            tensor.matmul(psA[:], qT_lo[:, q0:q0 + 128],
                                          kT_hi[:, c:c + 512], start=False, stop=False)
                            T.inc(tensor.matmul(psA[:], qT_hi[:, q0:q0 + 128],
                                                kT_lo[:, c:c + 512], start=False,
                                                stop=True))
                        block.tensor(t_sc)
                        ev_s = T.ev()

                        def v_evac(vector, kc=kc, diag=diag, r=r, ev_s=ev_s,
                                   ev_sc_free=ev_sc_free):
                            _wait(vector, ev_s)
                            if kc == 0:
                                _wait(vector, ev_sc_free)
                            if diag:
                                vector.tensor_tensor(psA[:], psA[:],
                                                     maskA[:, 512 * r:512 * r + 512],
                                                     OP.add)
                            V.inc(vector.tensor_scalar(
                                sc_buf[:, 512 * kc:512 * kc + 512], psA[:], 1.0,
                                -3.0e38, OP.mult, OP.max,
                                accum_out=mxc[:, kc:kc + 1]))
                        block.vector(v_evac)
                        evs_evc.append(V.ev())

                    def v_mx(vector, nkc=nkc, ev_last=evs_evc[-1]):
                        _wait(vector, ev_last)
                        V.inc(vector.reduce_max(mxc[:, 8:9], mxc[:, 0:nkc], axis=AX.X))
                        _wait(vector, V.ev())
                        V.inc(vector.tensor_scalar_mul(mneg[:], mxc[:, 8:9], -1.0))
                    block.vector(v_mx)
                    ev_mx = V.ev()

                    for kc in range(nkc):
                        def s_exp(scalar, kc=kc, ev_mx=ev_mx, ev_ptr=ch["T"].ev()):
                            _wait(scalar, ev_mx)
                            _wait(scalar, ev_ptr)
                            SC.inc(scalar.activation(
                                p32e[:], sc_buf[:, 512 * kc:512 * kc + 512],
                                ACT.Exp, bias=mneg[:, 0:1],
                                accum_out=smc[:, kc:kc + 1]))
                        block.scalar(s_exp)
                        ev_e = SC.ev()

                        def t_tr(tensor, ev_e=ev_e, ev_vfree=ch["V"].ev()):
                            _wait(tensor, ev_e)
                            _wait(tensor, ev_vfree)
                            last = None
                            for i in range(4):
                                last = tensor.transpose(
                                    psT_[:, 128 * i:128 * i + 128],
                                    p32e[:, 128 * i:128 * i + 128], ident[:])
                            T.inc(last)
                        block.tensor(t_tr)
                        ev_tr = T.ev()

                        def s_hi(scalar, ev_tr=ev_tr):
                            _wait(scalar, ev_tr)
                            SC.inc(scalar.activation(PT_hi[:], psT_[:, 0:512],
                                                     ACT.Copy, bias=0.0))
                        block.scalar(s_hi)
                        ev_hi = SC.ev()

                        def v_plo(vector, ev_hi=ev_hi):
                            _wait(vector, ev_hi)
                            V.inc(vector.tensor_tensor(PT_lo[:], psT_[:, 0:512],
                                                       PT_hi[:], OP.subtract))
                        block.vector(v_plo)
                        ev_lo = V.ev()

                        def t_pv(tensor, kc=kc, nkc=nkc, ev_lo=ev_lo):
                            _wait(tensor, ev_lo)
                            last = None
                            for i in range(4):
                                kt = 4 * kc + i
                                first = (kc == 0 and i == 0)
                                fin = (kc == nkc - 1 and i == 3)
                                c = 128 * kt
                                tensor.matmul(psO[:, 0:128], v_hi[:, c:c + 128],
                                              PT_hi[:, 128 * i:128 * i + 128],
                                              start=first, stop=False)
                                tensor.matmul(psO[:, 0:128], v_lo[:, c:c + 128],
                                              PT_hi[:, 128 * i:128 * i + 128],
                                              start=False, stop=False)
                                last = tensor.matmul(psO[:, 0:128], v_hi[:, c:c + 128],
                                                     PT_lo[:, 128 * i:128 * i + 128],
                                                     start=False, stop=fin)
                            T.inc(last)
                        block.tensor(t_pv)
                    ev_pv_last = T.ev()

                    def v_fin(vector, nkc=nkc, ev_sc=ch["SC"].ev(),
                              ev_pv=ev_pv_last):
                        _wait(vector, ev_sc)
                        V.inc(vector.reduce_sum(smc[:, 8:9], smc[:, 0:nkc], axis=AX.X))
                        _wait(vector, V.ev())
                        V.inc(vector.reciprocal(rT[:], smc[:, 8:9]))
                        _wait(vector, V.ev())
                        _wait(vector, ev_pv)
                        V.inc(vector.tensor_copy(OTs[:], psO[:, 0:128]))
                    block.vector(v_fin)
                    ev_ots = V.ev()

                    def t_to(tensor, ev_ots=ev_ots):
                        _wait(tensor, ev_ots)
                        T.inc(tensor.transpose(psB[:, 0:128], OTs[:], ident[:]))
                    block.tensor(t_to)
                    ev_to = T.ev()

                    def v_of(vector, ev_to=ev_to, ev_so=ch["SO"].ev()):
                        _wait(vector, ev_to)
                        _wait(vector, ev_so)
                        V.inc(vector.tensor_scalar(Ofin[:], psB[:, 0:128], rT[:, 0:1],
                                                   None, OP.mult))
                    block.vector(v_of)
                    ev_of = V.ev()

                    def g_of(gpsimd, qt=qt, hd_i=hd_i, ev_of=ev_of):
                        _wait(gpsimd, ev_of)
                        ch["SO"].inc(gpsimd.dma_start(
                            out=O_out[128 * qt:128 * qt + 128,
                                      128 * hd_i:128 * hd_i + 128],
                            in_=Ofin[:]), 16)
                    block.gpsimd(g_of)
                ev_headdone = ch["SO"].ev()

        def fin(gpsimd):
            _wait(gpsimd, ch["SO"].ev())
        block.gpsimd(fin)
    return nc



def build_L2():
    nc = bass.Bass("TRN2", target_bir_lowering=False)
    xq = nc.dram_tensor("xq", [TQ, D], F32, kind="ExternalInput")
    O4 = nc.dram_tensor("O4", [TQ, D], F32, kind="ExternalInput")
    modr = nc.dram_tensor("modr", [1, 3 * D], F32, kind="ExternalInput")
    ident_d = nc.dram_tensor("ident", [128, 128], F32, kind="ExternalInput")
    x1_out = nc.dram_tensor("x1_out", [TQ, D], F32, kind="ExternalOutput")
    h2T_hi_o = nc.dram_tensor("h2T_hi", [NDC, 128, TQ], BF16, kind="ExternalOutput")
    h2T_lo_o = nc.dram_tensor("h2T_lo", [NDC, 128, TQ], BF16, kind="ExternalOutput")

    with ExitStack() as st:
        E = st.enter_context
        ch = {}
        for name in ["T", "V", "SC", "LD0", "LX0", "LX1", "SO"]:
            ch[name] = Chan(nc, st, "c_" + name)
        T, V, SC = ch["T"], ch["V"], ch["SC"]
        block = E(nc.Block())

        ident = E(nc.sbuf_tensor("idents", [128, 128], F32))
        modrow = E(nc.sbuf_tensor("modrow", [1, 3 * D], F32))
        rowtmp = E(nc.sbuf_tensor("rowtmp", [1, D], F32))
        rowtmp2 = E(nc.sbuf_tensor("rowtmp2", [1, D], F32))
        onesR = E(nc.sbuf_tensor("onesR", [1, 128], F32))
        gaB = E(nc.sbuf_tensor("gaB", [128, D], F32))
        cbB = E(nc.sbuf_tensor("cbB", [128, D], F32))
        sbB = E(nc.sbuf_tensor("sbB", [128, D], F32))
        stats = E(nc.sbuf_tensor("stats", [128, 16], F32))
        tmpf = E(nc.sbuf_tensor("tmpf", [128, 2048], F32))
        xb0 = E(nc.sbuf_tensor("xb0", [128, D], F32))
        ob0 = E(nc.sbuf_tensor("ob0", [128, D], F32))
        x1b = E(nc.sbuf_tensor("x1b", [128, D], F32))
        xc = E(nc.sbuf_tensor("xc", [128, D], F32))
        hbuf = E(nc.sbuf_tensor("hbuf", [128, D], F32))
        h2T_hi = E(nc.sbuf_tensor("h2T_hi", [128, NDC * TQ // 16], BF16)) if False else None
        h2Th = E(nc.sbuf_tensor("h2Th", [128, NDC * 1024], BF16))
        h2Tl = E(nc.sbuf_tensor("h2Tl", [128, NDC * 1024], BF16))

        psT_ = E(nc.psum_tensor("psT", [128, 512], F32))
        psB = E(nc.psum_tensor("psB", [128, 512], F32))

        def ld0(sync):
            ch["LD0"].inc(sync.dma_start(out=ident[:], in_=ident_d[:]), 16)
            ch["LD0"].inc(sync.dma_start(out=modrow[:], in_=modr[:]), 16)
        block.sync(ld0)
        ev_ld0 = ch["LD0"].ev()

        def v0(vector):
            _wait(vector, ev_ld0)
            vector.memset(stats[:, 15:16], EPS)
            V.inc(vector.memset(onesR[:], 1.0))
        block.vector(v0)

        ev_ga = emit_clamp_row(nc, block, ch, modrow, 0, rowtmp, rowtmp2)
        ev_gaB = emit_bcast_row(nc, block, ch, onesR, rowtmp, 0, gaB, D, psB, ev_ga)

        def v_cb(vector):
            _wait(vector, ev_gaB)
            V.inc(vector.tensor_scalar_add(rowtmp[0:1, 0:D], modrow[0:1, 2 * D:3 * D], 1.0))
        block.vector(v_cb)
        ev_cb1 = V.ev()
        ev_cbB = emit_bcast_row(nc, block, ch, onesR, rowtmp, 0, cbB, D, psB, ev_cb1)
        ev_sbB = emit_bcast_row(nc, block, ch, onesR, modrow, D, sbB, D, psB, ev_ld0)

        evs_done = []
        for tt in range(NTQ):
            par = tt % 2

            def ldx(sync, tt=tt, par=par):
                if tt >= 1:
                    _wait(sync, evs_done[tt - 1])
                else:
                    _wait(sync, ev_sbB)
                ch["LX%d" % par].inc(sync.dma_start(
                    out=xb0[:], in_=xq[128 * tt:128 * tt + 128, :]), 16)
                ch["LX%d" % par].inc(sync.dma_start(
                    out=ob0[:], in_=O4[128 * tt:128 * tt + 128, :]), 16)
            block.sync(ldx)
            ev_x = ch["LX%d" % par].ev()

            def v_x1(vector, ev_x=ev_x, ev_so=ch["SO"].ev()):
                _wait(vector, ev_x)
                _wait(vector, ev_cbB)
                _wait(vector, ev_so)
                vector.tensor_tensor(tmpf[:], ob0[:], gaB[:], OP.mult)
                V.inc(vector.tensor_add(x1b[:], tmpf[:], xb0[:]))
            block.vector(v_x1)
            ev_x1 = V.ev()

            def g_x1(gpsimd, tt=tt, ev_x1=ev_x1):
                _wait(gpsimd, ev_x1)
                ch["SO"].inc(gpsimd.dma_start(
                    out=x1_out[128 * tt:128 * tt + 128, :], in_=x1b[:]), 16)
            block.gpsimd(g_x1)

            ev_h = emit_ln_h(nc, block, ch, x1b, stats, xc, hbuf, hbuf, cbB, sbB,
                             SC, ev_sq=ch["T"].ev())
            for g0 in range(0, NDC, 4):
                def t_tr(tensor, g0=g0, ev_h=ev_h, tt=tt):
                    _wait(tensor, ev_h)
                    if not (g0 == 0):
                        _wait(tensor, evs_tr[-1])
                    elif tt > 0:
                        _wait(tensor, evs_done[-1])
                    last = None
                    for k in range(4):
                        last = tensor.transpose(psT_[:, 128 * k:128 * k + 128],
                                                hbuf[:, (g0 + k) * 128:(g0 + k) * 128 + 128],
                                                ident[:])
                    T.inc(last)
                if g0 == 0:
                    evs_tr = []
                block.tensor(t_tr)
                ev_t = T.ev()

                def v_tr(vector, g0=g0, tt=tt, ev_t=ev_t):
                    _wait(vector, ev_t)
                    for k in range(4):
                        dc = g0 + k
                        o = 1024 * dc + 128 * tt
                        vector.tensor_copy(h2Th[:, o:o + 128], psT_[:, 128 * k:128 * k + 128])
                        vector.tensor_tensor(tmpf[:, 0:128], psT_[:, 128 * k:128 * k + 128],
                                             h2Th[:, o:o + 128], OP.subtract)
                        vector.tensor_copy(h2Tl[:, o:o + 128], tmpf[:, 0:128])
                    V.inc(vector.engine_nop())
                block.vector(v_tr)
                evs_tr.append(V.ev())
            evs_done.append(V.ev())

        def g_out(gpsimd):
            _wait(gpsimd, evs_done[-1])
            for dc in range(NDC):
                ch["SO"].inc(gpsimd.dma_start(
                    out=h2T_hi_o[dc, :, :], in_=h2Th[:, 1024 * dc:1024 * dc + 1024]), 16)
                ch["SO"].inc(gpsimd.dma_start(
                    out=h2T_lo_o[dc, :, :], in_=h2Tl[:, 1024 * dc:1024 * dc + 1024]), 16)
            _wait(gpsimd, ch["SO"].ev())
        block.gpsimd(g_out)
    return nc



def build_L3(dbg=False):
    nc = bass.Bass("TRN2", target_bir_lowering=False)
    h2hi = nc.dram_tensor("h2hi", [NDC, 128, S], BF16, kind="ExternalInput")
    h2lo = nc.dram_tensor("h2lo", [NDC, 128, S], BF16, kind="ExternalInput")
    gidx = nc.dram_tensor("gidx", [JSL, D], I32, kind="ExternalInput")
    uidx = nc.dram_tensor("uidx", [JSL, D], I32, kind="ExternalInput")
    didx = nc.dram_tensor("didx", [D, JSL], I32, kind="ExternalInput")
    lutr = nc.dram_tensor("lutr", [1, 16], F32, kind="ExternalInput")
    maskj = nc.dram_tensor("maskj", [128, NJT], F32, kind="ExternalInput")
    ident_d = nc.dram_tensor("ident", [128, 128], F32, kind="ExternalInput")
    part_out = nc.dram_tensor("part_out", [S, D], F32, kind="ExternalOutput")

    kw3 = dict(kind="ExternalOutput") if dbg else {}
    wgh_d = nc.dram_tensor("wgh", [NJT, 128, D], BF16, **kw3)
    wgl_d = nc.dram_tensor("wgl", [NJT, 128, D], BF16, **kw3)
    wuh_d = nc.dram_tensor("wuh", [NJT, 128, D], BF16, **kw3)
    wul_d = nc.dram_tensor("wul", [NJT, 128, D], BF16, **kw3)
    wdh_d = nc.dram_tensor("wdh", [NDC, 128, JSL], BF16, **kw3)
    wdl_d = nc.dram_tensor("wdl", [NDC, 128, JSL], BF16, **kw3)

    with ExitStack() as st:
        E = st.enter_context
        ch = {}
        for name in ["T", "V", "SC", "LD0", "LI0", "LI1", "LH0", "LW0", "LW1",
                     "SO", "SW0", "SW1"]:
            ch[name] = Chan(nc, st, "c_" + name)
        T, V, SC = ch["T"], ch["V"], ch["SC"]
        block = E(nc.Block())

        ident = E(nc.sbuf_tensor("idents", [128, 128], F32))
        lutB = E(nc.sbuf_tensor("lutB", [128, 16], F32))
        mskj = E(nc.sbuf_tensor("mskj", [128, NJT], F32))
        onesR = E(nc.sbuf_tensor("onesR", [1, 128], F32))
        tmpf = E(nc.sbuf_tensor("tmpf", [128, 2048], F32))

        psG = E(nc.psum_tensor("psG", [128, 512], F32))
        psU = E(nc.psum_tensor("psU", [128, 512], F32))
        psD = [E(nc.psum_tensor("psD%d" % i, [128, 512], F32)) for i in range(4)]
        psT_ = E(nc.psum_tensor("psT", [128, 512], F32))
        psB = E(nc.psum_tensor("psB", [128, 512], F32))

        def ld0(sync):
            ch["LD0"].inc(sync.dma_start(out=ident[:], in_=ident_d[:]), 16)
            ch["LD0"].inc(sync.dma_start(out=lutB[0:1, :], in_=lutr[:]), 16)
            ch["LD0"].inc(sync.dma_start(out=mskj[:], in_=maskj[:]), 16)
        block.sync(ld0)
        ev_ld0 = ch["LD0"].ev()

        def v0(vector):
            _wait(vector, ev_ld0)
            V.inc(vector.memset(onesR[:], 1.0))
        block.vector(v0)
        ev_lutB = emit_bcast_row(nc, block, ch, onesR, lutB, 0, lutB, 16, psB, V.ev())

        with ExitStack() as stg:
            E2 = stg.enter_context
            idxb = [E2(nc.sbuf_tensor("idxb%d" % i, [128, 2048], I32)) for i in range(2)]
            acc = E2(nc.sbuf_tensor("acc", [128, 2048], F32))
            gtmp = E2(nc.sbuf_tensor("gtmp", [128, 2048], F32))
            whi = [E2(nc.sbuf_tensor("whi%d" % i, [128, 2048], BF16)) for i in range(2)]
            wlo = [E2(nc.sbuf_tensor("wlo%d" % i, [128, 2048], BF16)) for i in range(2)]

            jobs = []
            for jt in range(NJT):
                jobs.append((gidx, 128 * jt, D, wgh_d, wgl_d, jt, jt))
            for jt in range(NJT):
                jobs.append((uidx, 128 * jt, D, wuh_d, wul_d, jt, jt))
            for ot in range(NDC):
                jobs.append((didx, 128 * ot, JSL, wdh_d, wdl_d, ot, None))

            evs_jdone = []
            for ji, (srcd, r0, wdt, dsth, dstl, blk, mcol) in enumerate(jobs):
                par = ji % 2

                def ldi(sync, ji=ji, par=par, srcd=srcd, r0=r0, wdt=wdt):
                    if ji >= 2:
                        _wait(sync, evs_jdone[ji - 2])
                    else:
                        _wait(sync, ev_lutB)
                    ch["LI%d" % par].inc(sync.dma_start(
                        out=idxb[par][:, 0:wdt], in_=srcd[r0:r0 + 128, 0:wdt]), 16)
                block.sync(ldi)
                ev_i = ch["LI%d" % par].ev()

                def v_g(vector, ji=ji, par=par, wdt=wdt, mcol=mcol, ev_i=ev_i,
                        ev_tprev=ch["T"].ev()):
                    _wait(vector, ev_i)
                    _wait(vector, ev_tprev)
                    if ji < 2:
                        _wait(vector, ev_lutB)
                    src = idxb[par][:, 0:wdt]
                    vector.tensor_scalar(acc[:, 0:wdt], src, 0.0, lutB[:, 0:1],
                                         OP.is_equal, OP.mult)
                    for v in range(1, 16):
                        vector.tensor_scalar(gtmp[:, 0:wdt], src, float(v),
                                             lutB[:, v:v + 1], OP.is_equal, OP.mult)
                        vector.tensor_tensor(acc[:, 0:wdt], acc[:, 0:wdt],
                                             gtmp[:, 0:wdt], OP.add)
                    if mcol is not None:
                        vector.tensor_scalar(acc[:, 0:wdt], acc[:, 0:wdt],
                                             mskj[:, mcol:mcol + 1], None, OP.mult)
                    V.inc(vector.engine_nop())
                block.vector(v_g)
                ev_acc = V.ev()

                nt_ = wdt // 128
                for g0 in range(0, nt_, 4):
                    gn = min(4, nt_ - g0)

                    def t_g(tensor, g0=g0, gn=gn, ev_acc=ev_acc, ji=ji):
                        _wait(tensor, ev_acc)
                        if g0 > 0:
                            _wait(tensor, evs_gt[-1])
                        elif ji >= 2:
                            _wait(tensor, evs_jdone[ji - 2])
                        last = None
                        for k in range(gn):
                            last = tensor.transpose(
                                psT_[:, 128 * k:128 * k + 128],
                                acc[:, (g0 + k) * 128:(g0 + k) * 128 + 128], ident[:])
                        T.inc(last)
                    if g0 == 0:
                        evs_gt = []
                    block.tensor(t_g)
                    ev_t = T.ev()

                    def v_t(vector, g0=g0, gn=gn, par=par, ev_t=ev_t):
                        _wait(vector, ev_t)
                        w = 128 * gn
                        o = 128 * g0
                        vector.tensor_copy(whi[par][:, o:o + w], psT_[:, 0:w])
                        vector.tensor_tensor(tmpf[:, 0:w], psT_[:, 0:w],
                                             whi[par][:, o:o + w], OP.subtract)
                        V.inc(vector.tensor_copy(wlo[par][:, o:o + w], tmpf[:, 0:w]))
                    block.vector(v_t)
                    evs_gt.append(V.ev())
                ev_sp = V.ev()

                def g_w(gpsimd, ji=ji, par=par, dsth=dsth, dstl=dstl, blk=blk,
                        wdt=wdt, ev_sp=ev_sp):
                    _wait(gpsimd, ev_sp)
                    ch["SW%d" % par].inc(gpsimd.dma_start(
                        out=dsth[blk, :, 0:wdt], in_=whi[par][:, 0:wdt]), 16)
                    ch["SW%d" % par].inc(gpsimd.dma_start(
                        out=dstl[blk, :, 0:wdt], in_=wlo[par][:, 0:wdt]), 16)
                block.gpsimd(g_w)
                evs_jdone.append(ch["SW%d" % par].ev())

        ev_gall = [ch["SW0"].ev(), ch["SW1"].ev()]


        TBW = 512
        NTB = S // TBW
        NTTL = TBW // 128
        with ExitStack() as stf:
            E3 = stf.enter_context
            h2h = E3(nc.sbuf_tensor("h2h", [128, NDC * TBW], BF16))
            h2l = E3(nc.sbuf_tensor("h2l", [128, NDC * TBW], BF16))
            wb_h = [E3(nc.sbuf_tensor("wbh%d" % i, [128, D], BF16)) for i in range(2)]
            wb_l = [E3(nc.sbuf_tensor("wbl%d" % i, [128, D], BF16)) for i in range(2)]
            wub_h = [E3(nc.sbuf_tensor("wubh%d" % i, [128, D], BF16)) for i in range(2)]
            wub_l = [E3(nc.sbuf_tensor("wubl%d" % i, [128, D], BF16)) for i in range(2)]
            gsil = E3(nc.sbuf_tensor("gsil", [128, 512], F32))
            hfh = E3(nc.sbuf_tensor("hfh", [128, NJT * TBW], BF16))
            hfl = E3(nc.sbuf_tensor("hfl", [128, NJT * TBW], BF16))
            wdhs = E3(nc.sbuf_tensor("wdhs", [128, NDC * JSL], BF16))
            wdls = E3(nc.sbuf_tensor("wdls", [128, NDC * JSL], BF16))
            partS = E3(nc.sbuf_tensor("partS", [128, D], F32))

            def ldwd(sync):
                _wait(sync, ev_gall[0])
                _wait(sync, ev_gall[1])
                for ot in range(NDC):
                    ch["LH0"].inc(sync.dma_start(
                        out=wdhs[:, JSL * ot:JSL * ot + JSL],
                        in_=wdh_d[ot, :, 0:JSL]), 16)
                    ch["LH0"].inc(sync.dma_start(
                        out=wdls[:, JSL * ot:JSL * ot + JSL],
                        in_=wdl_d[ot, :, 0:JSL]), 16)
            block.sync(ldwd)
            ev_wd = ch["LH0"].ev()

            for tb in range(NTB):
                def ldh2(sync, tb=tb):
                    if tb == 0:
                        _wait(sync, ev_gall[0])
                        _wait(sync, ev_gall[1])
                    else:
                        _wait(sync, evs_tb[-1])
                    for dc in range(NDC):
                        ch["LH0"].inc(sync.dma_start(
                            out=h2h[:, TBW * dc:TBW * dc + TBW],
                            in_=h2hi[dc, :, TBW * tb:TBW * tb + TBW]), 16)
                        ch["LH0"].inc(sync.dma_start(
                            out=h2l[:, TBW * dc:TBW * dc + TBW],
                            in_=h2lo[dc, :, TBW * tb:TBW * tb + TBW]), 16)
                if tb == 0:
                    evs_tb = []
                block.sync(ldh2)
                ev_h2 = ch["LH0"].ev()

                evs_hf = []
                for jt in range(NJT):
                    par = jt % 2

                    def ldwg(sync, jt=jt, par=par, tb=tb):
                        if jt >= 2:
                            _wait(sync, evs_hf[jt - 2])
                        elif tb > 0:
                            _wait(sync, evs_hf_prev[-1])
                        ch["LW%d" % par].inc(sync.dma_start(
                            out=wb_h[par][:], in_=wgh_d[jt, :, :]), 16)
                        ch["LW%d" % par].inc(sync.dma_start(
                            out=wb_l[par][:], in_=wgl_d[jt, :, :]), 16)
                        ch["LW%d" % par].inc(sync.dma_start(
                            out=wub_h[par][:], in_=wuh_d[jt, :, :]), 16)
                        ch["LW%d" % par].inc(sync.dma_start(
                            out=wub_l[par][:], in_=wul_d[jt, :, :]), 16)
                    block.sync(ldwg)
                    ev_w = ch["LW%d" % par].ev()

                    def t_gu(tensor, jt=jt, par=par, ev_w=ev_w, ev_h2=ev_h2, tb=tb):
                        _wait(tensor, ev_w)
                        _wait(tensor, ev_h2)
                        if jt > 0:
                            _wait(tensor, evs_guev[-1])
                        elif tb > 0:
                            _wait(tensor, evs_guev_prev[-1])
                        last = None
                        for dc in range(NDC):
                            rh = h2h[:, TBW * dc:TBW * dc + TBW]
                            rl = h2l[:, TBW * dc:TBW * dc + TBW]
                            for (ps, wh_, wl_) in ((psG, wb_h[par], wb_l[par]),
                                                   (psU, wub_h[par], wub_l[par])):
                                a = wh_[:, 128 * dc:128 * dc + 128]
                                bb = wl_[:, 128 * dc:128 * dc + 128]
                                tensor.matmul(ps[:], a, rh, start=(dc == 0), stop=False)
                                tensor.matmul(ps[:], a, rl, start=False, stop=False)
                                last = tensor.matmul(ps[:], bb, rh, start=False,
                                                     stop=(dc == NDC - 1))
                        T.inc(last)
                    if jt == 0:
                        evs_guev_prev = evs_guev if tb > 0 else None
                        evs_guev = []
                    block.tensor(t_gu)
                    ev_mm = T.ev()

                    def s_si(scalar, ev_mm=ev_mm):
                        _wait(scalar, ev_mm)
                        SC.inc(scalar.activation(gsil[:], psG[:], ACT.Silu))
                    block.scalar(s_si)
                    ev_si = SC.ev()

                    def v_hf(vector, jt=jt, ev_si=ev_si, ev_tdn=ch["T"].ev()):
                        _wait(vector, ev_si)
                        if jt == 0:
                            _wait(vector, ev_tdn)
                        o = TBW * jt
                        vector.tensor_tensor(tmpf[:, 0:512], psU[:], gsil[:], OP.mult)
                        vector.tensor_copy(hfh[:, o:o + 512], tmpf[:, 0:512])
                        vector.tensor_tensor(tmpf[:, 512:1024], tmpf[:, 0:512],
                                             hfh[:, o:o + 512], OP.subtract)
                        V.inc(vector.tensor_copy(hfl[:, o:o + 512], tmpf[:, 512:1024]))
                    block.vector(v_hf)
                    evs_guev.append(V.ev())
                    evs_hf.append(V.ev())
                evs_hf_prev = evs_hf
                ev_hf_all = V.ev()

                for ttl in range(NTTL):
                    def t_dn(tensor, ttl=ttl, tb=tb, ev_hf_all=ev_hf_all):
                        _wait(tensor, ev_hf_all)
                        _wait(tensor, ev_wd)
                        if ttl > 0:
                            _wait(tensor, evs_dn[-1])
                        last = None
                        for ot in range(NDC):
                            ps = psD[ot // 4]
                            oc = 128 * (ot % 4)
                            for jt in range(NJT):
                                sh = hfh[:, TBW * jt + 128 * ttl:
                                          TBW * jt + 128 * ttl + 128]
                                sl = hfl[:, TBW * jt + 128 * ttl:
                                          TBW * jt + 128 * ttl + 128]
                                rh = wdhs[:, JSL * ot + 128 * jt:JSL * ot + 128 * jt + 128]
                                rl = wdls[:, JSL * ot + 128 * jt:JSL * ot + 128 * jt + 128]
                                first = (jt == 0)
                                lastf = (jt == NJT - 1)
                                tensor.matmul(ps[:, oc:oc + 128], sh, rh,
                                              start=first, stop=False)
                                tensor.matmul(ps[:, oc:oc + 128], sh, rl,
                                              start=False, stop=False)
                                last = tensor.matmul(ps[:, oc:oc + 128], sl, rh,
                                                     start=False, stop=lastf)
                        T.inc(last)
                    if ttl == 0:
                        evs_dn = []
                    block.tensor(t_dn)
                    ev_dm = T.ev()

                    def v_dn(vector, ttl=ttl, ev_dm=ev_dm):
                        _wait(vector, ev_dm)
                        _wait(vector, ch["SO"].ev())
                        for i in range(4):
                            vector.tensor_copy(partS[:, 512 * i:512 * i + 512], psD[i][:])
                        V.inc(vector.engine_nop())
                    block.vector(v_dn)
                    evs_dn.append(V.ev())

                    def g_dn(gpsimd, ttl=ttl, tb=tb):
                        _wait(gpsimd, evs_dn[-1])
                        r = TBW * tb + 128 * ttl
                        ch["SO"].inc(gpsimd.dma_start(
                            out=part_out[r:r + 128, :], in_=partS[:]), 16)
                    block.gpsimd(g_dn)
                evs_tb.append(V.ev())

        def fin(gpsimd):
            _wait(gpsimd, ch["SO"].ev())
        block.gpsimd(fin)
    return nc



def build_L4():
    nc = bass.Bass("TRN2", target_bir_lowering=False)
    x1 = nc.dram_tensor("x1", [TQ, D], F32, kind="ExternalInput")
    p_in = [nc.dram_tensor("p%d" % i, [TQ, D], F32, kind="ExternalInput")
            for i in range(4)]
    modr = nc.dram_tensor("modr", [1, D], F32, kind="ExternalInput")
    y_out = nc.dram_tensor("y_out", [TQ, D], F32, kind="ExternalOutput")

    with ExitStack() as st:
        E = st.enter_context
        ch = {}
        for name in ["T", "V", "SC", "LD0", "LX0", "LX1", "SO"]:
            ch[name] = Chan(nc, st, "c_" + name)
        T, V = ch["T"], ch["V"]
        block = E(nc.Block())
        modrow = E(nc.sbuf_tensor("modrow", [1, D], F32))
        rowtmp = E(nc.sbuf_tensor("rowtmp", [1, D], F32))
        rowtmp2 = E(nc.sbuf_tensor("rowtmp2", [1, D], F32))
        onesR = E(nc.sbuf_tensor("onesR", [1, 128], F32))
        gbB = E(nc.sbuf_tensor("gbB", [128, D], F32))
        bufs = [E(nc.sbuf_tensor("b%d" % i, [128, D], F32)) for i in range(6)]
        psB = E(nc.psum_tensor("psB", [128, 512], F32))

        def ld0(sync):
            ch["LD0"].inc(sync.dma_start(out=modrow[:], in_=modr[:]), 16)
        block.sync(ld0)
        ev0 = ch["LD0"].ev()

        def v0(vector):
            _wait(vector, ev0)
            V.inc(vector.memset(onesR[:], 1.0))
        block.vector(v0)
        ev_gb = emit_clamp_row(nc, block, ch, modrow, 0, rowtmp, rowtmp2)
        ev_gbB = emit_bcast_row(nc, block, ch, onesR, rowtmp, 0, gbB, D, psB, ev_gb)

        evs = []
        for tt in range(NTQ):
            par = tt % 2

            def ldx(sync, tt=tt, par=par):
                if tt >= 1:
                    _wait(sync, evs[tt - 1])
                else:
                    _wait(sync, ev_gbB)
                ch["LX%d" % par].inc(sync.dma_start(
                    out=bufs[0][:], in_=x1[128 * tt:128 * tt + 128, :]), 16)
                for i in range(4):
                    ch["LX%d" % par].inc(sync.dma_start(
                        out=bufs[1 + i][:], in_=p_in[i][128 * tt:128 * tt + 128, :]), 16)
            block.sync(ldx)
            ev_x = ch["LX%d" % par].ev()

            def v_s(vector, tt=tt, ev_x=ev_x):
                _wait(vector, ev_x)
                if tt > 0:
                    _wait(vector, ch["SO"].ev())
                vector.tensor_add(bufs[5][:], bufs[1][:], bufs[2][:])
                vector.tensor_add(bufs[5][:], bufs[5][:], bufs[3][:])
                vector.tensor_add(bufs[5][:], bufs[5][:], bufs[4][:])
                vector.tensor_tensor(bufs[5][:], bufs[5][:], gbB[:], OP.mult)
                V.inc(vector.tensor_add(bufs[5][:], bufs[5][:], bufs[0][:]))
            block.vector(v_s)
            evs.append(V.ev())

            def g_o(gpsimd, tt=tt):
                _wait(gpsimd, evs[-1])
                ch["SO"].inc(gpsimd.dma_start(
                    out=y_out[128 * tt:128 * tt + 128, :], in_=bufs[5][:]), 16)
            block.gpsimd(g_o)

        def fin(gpsimd):
            _wait(gpsimd, ch["SO"].ev())
        block.gpsimd(fin)
    return nc



def kernel_multilaunch(L0_attn_hidden_states, L0_attn_w_qkv, L0_attn_b_qkv, L0_ffn_lut,
           L0_ffn_gate_idx, L0_ffn_up_idx, L0_ffn_down_idx, L0_c, L0_wmod):
    x = np.ascontiguousarray(L0_attn_hidden_states)
    wqkv = np.ascontiguousarray(L0_attn_w_qkv)
    bqkv = np.ascontiguousarray(L0_attn_b_qkv)
    lut = np.ascontiguousarray(L0_ffn_lut)
    gi = np.ascontiguousarray(L0_ffn_gate_idx)
    ui = np.ascontiguousarray(L0_ffn_up_idx)
    di = np.ascontiguousarray(L0_ffn_down_idx)
    c = np.ascontiguousarray(L0_c)
    wmod = np.ascontiguousarray(L0_wmod)
    tb = _host_tables()
    cores = list(range(8))
    jsl = _jslices()

    nc1 = build_L1()
    maps1 = []
    for cc in cores:
        b, g = cc // 4, cc % 4
        maps1.append(dict(
            xb=x[b], wq=wqkv[EH * g:EH * (g + 1)], bq=bqkv.reshape(1, EH * 4)[:, EH * g:EH * (g + 1)],
            cCol=np.ascontiguousarray(c[b].reshape(16, 128).T),
            wmod=wmod, embT=tb["embT"], ksT=tb["ksT"], kcT=tb["kcT"],
            ident=tb["ident"], maskA=tb["maskA"],
        ))
    r1 = run_bass_kernel_spmd(nc1, maps1, core_ids=cores)
    O = [r["O_out"] for r in r1.results]
    mods = [r["mod_out"] for r in r1.results]

    nc2 = build_L2()
    maps2 = []
    for cc in cores:
        b, g = cc // 4, cc % 4
        O4 = np.concatenate([O[4 * b + gg][TQ * g:TQ * (g + 1)] for gg in range(4)],
                            axis=1)
        maps2.append(dict(
            xq=x[b][TQ * g:TQ * (g + 1)], O4=np.ascontiguousarray(O4),
            modr=np.ascontiguousarray(mods[4 * b][:, 2 * D:5 * D]),
            ident=tb["ident"],
        ))
    r2 = run_bass_kernel_spmd(nc2, maps2, core_ids=cores)
    x1 = [r["x1_out"] for r in r2.results]
    h2hi = [r["h2T_hi"] for r in r2.results]
    h2lo = [r["h2T_lo"] for r in r2.results]

    nc3 = build_L3()
    maps3 = []
    for cc in cores:
        b, g = cc // 4, cc % 4
        hh = np.concatenate([h2hi[4 * b + gg] for gg in range(4)], axis=2)
        hl = np.concatenate([h2lo[4 * b + gg] for gg in range(4)], axis=2)
        o0, osz = jsl[g]
        gip = np.zeros((JSL, D), dtype=np.int32)
        uip = np.zeros((JSL, D), dtype=np.int32)
        dip = np.zeros((D, JSL), dtype=np.int32)
        gip[:osz] = gi[o0:o0 + osz]
        uip[:osz] = ui[o0:o0 + osz]
        dip[:, :osz] = di[:, o0:o0 + osz]
        mj = np.zeros((128, NJT), dtype=np.float32)
        mjf = np.zeros(JSL, dtype=np.float32)
        mjf[:osz] = 1.0
        mj[:, :] = mjf.reshape(NJT, 128).T
        maps3.append(dict(
            h2hi=np.ascontiguousarray(hh), h2lo=np.ascontiguousarray(hl),
            gidx=gip, uidx=uip, didx=dip,
            lutr=lut.reshape(1, 16), maskj=mj, ident=tb["ident"],
        ))
    r3 = run_bass_kernel_spmd(nc3, maps3, core_ids=cores)
    parts = [r["part_out"] for r in r3.results]

    nc4 = build_L4()
    maps4 = []
    for cc in cores:
        b, g = cc // 4, cc % 4
        m = dict(x1=x1[cc], modr=np.ascontiguousarray(mods[4 * b][:, 5 * D:6 * D]))
        for i in range(4):
            m["p%d" % i] = parts[4 * b + i][TQ * g:TQ * (g + 1)]
        maps4.append(m)
    r4 = run_bass_kernel_spmd(nc4, maps4, core_ids=cores)
    y = np.empty((B, S, D), dtype=np.float32)
    for cc in cores:
        b, g = cc // 4, cc % 4
        y[b, TQ * g:TQ * (g + 1)] = r4.results[cc]["y_out"]
    return y



CC_GROUPS = [[0, 1, 2, 3], [4, 5, 6, 7]]


def build_fused(dbg=False):
    nc = bass.Bass("TRN2", target_bir_lowering=False)
    xb = nc.dram_tensor("xb", [S, D], F32, kind="ExternalInput")
    wq = nc.dram_tensor("wq", [EH, D], F32, kind="ExternalInput")
    bq = nc.dram_tensor("bq", [1, EH], F32, kind="ExternalInput")
    cCol_d = nc.dram_tensor("cCol", [128, 16], F32, kind="ExternalInput")
    wmod = nc.dram_tensor("wmod", [D, 6 * D], F32, kind="ExternalInput")
    embT_d = nc.dram_tensor("embT", [128, NT * 64], F32, kind="ExternalInput")
    ksT_d = nc.dram_tensor("ksT", [128, NT * 64], F32, kind="ExternalInput")
    kcT_d = nc.dram_tensor("kcT", [128, NT * 64], F32, kind="ExternalInput")
    ident_d = nc.dram_tensor("ident", [128, 128], F32, kind="ExternalInput")
    maskA_d = nc.dram_tensor("maskA", [128, 4 * 512], F32, kind="ExternalInput")
    gidx = nc.dram_tensor("gidx", [JSL, D], I32, kind="ExternalInput")
    uidx = nc.dram_tensor("uidx", [JSL, D], I32, kind="ExternalInput")
    didx = nc.dram_tensor("didx", [D, JSL], I32, kind="ExternalInput")
    lutr = nc.dram_tensor("lutr", [1, 16], F32, kind="ExternalInput")
    maskj = nc.dram_tensor("maskj", [128, NJT], F32, kind="ExternalInput")
    y_out = nc.dram_tensor("y_out", [TQ, D], F32, kind="ExternalOutput")
    O_out = nc.dram_tensor("O_own", [S, 512], F32)
    mod_out = nc.dram_tensor("mod_own", [1, 6 * D], F32)
    h1T_hi_d = nc.dram_tensor("h1T_hi", [NT, 128, D], BF16)
    h1T_lo_d = nc.dram_tensor("h1T_lo", [NT, 128, D], BF16)
    O_all = nc.dram_tensor("O_all", [4, S, 512], F32)
    x1_own = nc.dram_tensor("x1_own", [S, D], F32)
    h2hi_own = nc.dram_tensor("h2hi", [NDC, 128, S], BF16)
    h2lo_own = nc.dram_tensor("h2lo", [NDC, 128, S], BF16)
    wgh_d = nc.dram_tensor("wgh", [NJT, 128, D], BF16)
    wgl_d = nc.dram_tensor("wgl", [NJT, 128, D], BF16)
    wuh_d = nc.dram_tensor("wuh", [NJT, 128, D], BF16)
    wul_d = nc.dram_tensor("wul", [NJT, 128, D], BF16)
    wdh_d = nc.dram_tensor("wdh", [NDC, 128, JSL], BF16)
    wdl_d = nc.dram_tensor("wdl", [NDC, 128, JSL], BF16)
    c_own = nc.dram_tensor("c_own", [S, D], F32)
    y_q = nc.dram_tensor("y_q", [TQ, D], F32)

    with ExitStack() as st:
        E0 = st.enter_context
        ch = {}
        for name in ["T", "V", "SC", "LD0", "LW0", "LW1", "LX0", "LX1",
                     "LH0", "LH1", "SO", "SH0", "SH1", "LI0", "LI1",
                     "SW0", "SW1", "CC"]:
            ch[name] = Chan(nc, st, "c_" + name)
        ch["LD"] = ch["LD0"]
        T, V, SC, CC = ch["T"], ch["V"], ch["SC"], ch["CC"]
        block = E0(nc.Block())

        ident = E0(nc.sbuf_tensor("idents", [128, 128], F32))
        onesR = E0(nc.sbuf_tensor("onesR", [1, 128], F32))
        invB = E0(nc.sbuf_tensor("invB", [128, 1], F32))
        nrm = E0(nc.sbuf_tensor("nrm", [1, 1], F32))
        stats = E0(nc.sbuf_tensor("stats", [128, 16], F32))
        tmpf = E0(nc.sbuf_tensor("tmpf", [128, 512], F32))
        dump = E0(nc.sbuf_tensor("dump", [128, 512], F32))
        psB = E0(nc.psum_tensor("psB", [128, 512], F32))

        with ExitStack() as st1:
            E = st1.enter_context
            cCol = E(nc.sbuf_tensor("cCols", [128, 16], F32))
            maskA = E(nc.sbuf_tensor("maskAs", [128, 4 * 512], F32))
            cosT = E(nc.sbuf_tensor("cosT", [128, NT * 64], F32))
            sinT = E(nc.sbuf_tensor("sinT", [128, NT * 64], F32))
            rowA = E(nc.sbuf_tensor("rowA", [1, D], F32))
            rowS = E(nc.sbuf_tensor("rowS", [1, D], F32))
            rowM = E(nc.sbuf_tensor("rowM", [1, 1024], F32))
            caB = E(nc.sbuf_tensor("caB", [128, D], F32))
            saB = E(nc.sbuf_tensor("saB", [128, D], F32))
            bqB = E(nc.sbuf_tensor("bqB", [128, EH], F32))
            psA = E(nc.psum_tensor("psA", [128, 512], F32))
            psO = E(nc.psum_tensor("psO", [128, 512], F32))
            psT_ = E(nc.psum_tensor("psT", [128, 512], F32))
            psQ = E(nc.psum_tensor("psQ", [128, 512], F32))
            psM = E(nc.psum_tensor("psM", [128, 512], F32))
            with ExitStack() as st_rope:
                ER = st_rope.enter_context
                embT = ER(nc.sbuf_tensor("embTs", [128, NT * 64], F32))
                ksT = ER(nc.sbuf_tensor("ksTs", [128, NT * 64], F32))
                kcT = ER(nc.sbuf_tensor("kcTs", [128, NT * 64], F32))
                rtmp = ER(nc.sbuf_tensor("rtmp", [128, NT * 64], F32))

                def ld0(sync):
                    for dst, src in [(cCol, cCol_d), (ident, ident_d), (maskA, maskA_d),
                                     (embT, embT_d), (ksT, ksT_d), (kcT, kcT_d)]:
                        ch["LD0"].inc(sync.dma_start(out=dst[:], in_=src[:]), 16)
                    ch["LD0"].inc(sync.dma_start(out=bqB[0:1, :], in_=bq[:]), 16)
                block.sync(ld0)
                ev_ld0 = ch["LD0"].ev()

                def v0(vector):
                    vector.memset(onesR[:], 1.0)
                    vector.memset(stats[:, 15:16], EPS)
                    V.inc(vector.memset(nrm[:], NORM32))
                    _wait(vector, V.ev())
                    V.inc(vector.reciprocal(nrm[0:1, 0:1], nrm[0:1, 0:1]))
                block.vector(v0)
                ev_ones = V.ev()

                def v_rope(vector):
                    _wait(vector, ev_ld0)
                    vector.tensor_scalar(rtmp[:], ksT[:], C1, None, OP.mult)
                    vector.tensor_tensor(cosT[:], embT[:], rtmp[:], OP.subtract)
                    vector.tensor_scalar(rtmp[:], ksT[:], C2, None, OP.mult)
                    vector.tensor_tensor(cosT[:], cosT[:], rtmp[:], OP.subtract)
                    vector.tensor_scalar(rtmp[:], kcT[:], C1, None, OP.mult)
                    vector.tensor_tensor(sinT[:], embT[:], rtmp[:], OP.subtract)
                    vector.tensor_scalar(rtmp[:], kcT[:], C2, None, OP.mult)
                    V.inc(vector.tensor_tensor(sinT[:], sinT[:], rtmp[:], OP.subtract))
                block.vector(v_rope)
                ev_r = V.ev()

                def s_rope(scalar):
                    _wait(scalar, ev_r)
                    scalar.activation(cosT[:], cosT[:], ACT.Sin)
                    SC.inc(scalar.activation(sinT[:], sinT[:], ACT.Sin))
                block.scalar(s_rope)
                ev_tables = SC.ev()
            sinTab, cosTab = cosT, sinT

            def t_inv(tensor):
                _wait(tensor, ev_ones)
                T.inc(tensor.matmul(psB[:, 0:1], onesR[0:1, 0:128], nrm[0:1, 0:1],
                                    start=True, stop=True))
            block.tensor(t_inv)
            ev_ti = T.ev()

            def v_inv(vector):
                _wait(vector, ev_ti)
                V.inc(vector.tensor_copy(invB[:], psB[:, 0:1]))
            block.vector(v_inv)

            with ExitStack() as st_mod:
                ev_mod = emit_mod_phase(nc, block, st_mod, ch, cCol, wmod, mod_out,
                                        rowM, psM, ev_pre=ev_tables)

            def ld_rows(sync):
                _wait(sync, ev_mod)
                ch["LD0"].inc(sync.dma_start(out=rowA[0:1, 0:D],
                                             in_=mod_out[0:1, D:2 * D]), 16)
                ch["LD0"].inc(sync.dma_start(out=rowS[0:1, 0:D],
                                             in_=mod_out[0:1, 0:D]), 16)
            block.sync(ld_rows)
            ev_rows = ch["LD0"].ev()

            def v_rows(vector):
                _wait(vector, ev_rows)
                V.inc(vector.tensor_scalar_add(rowA[0:1, 0:D], rowA[0:1, 0:D], 1.0))
            block.vector(v_rows)
            ev_ca1 = V.ev()
            ev_caB = emit_bcast_row(nc, block, ch, onesR, rowA, 0, caB, D, psB, ev_ca1)
            ev_saB = emit_bcast_row(nc, block, ch, onesR, rowS, 0, saB, D, psB, ev_rows)
            ev_bqB = emit_bcast_row(nc, block, ch, onesR, bqB, 0, bqB, EH, psB, ev_ld0)

            with ExitStack() as st_ln:
                E2 = st_ln.enter_context
                xbuf = [E2(nc.sbuf_tensor("xb%d" % i, [128, D], F32)) for i in range(2)]
                xc = E2(nc.sbuf_tensor("xc", [128, D], F32))
                hbuf = E2(nc.sbuf_tensor("hbuf", [128, D], F32))
                hTs = [(E2(nc.sbuf_tensor("hTh%d" % i, [128, D], BF16)),
                        E2(nc.sbuf_tensor("hTl%d" % i, [128, D], BF16))) for i in range(2)]
                evs_cons = []
                evs_hT = []
                for tt in range(NT):
                    par = tt % 2

                    def ldx(sync, tt=tt, par=par):
                        if tt >= 2:
                            _wait(sync, evs_cons[tt - 2])
                        else:
                            _wait(sync, ev_caB)
                        ch["LX%d" % par].inc(
                            sync.dma_start(out=xbuf[par][:],
                                           in_=xb[128 * tt:128 * tt + 128, :]), 16)
                    block.sync(ldx)
                    ev_x = ch["LX%d" % par].ev()

                    def vw(vector, par=par, ev_x=ev_x, tt=tt):
                        _wait(vector, ev_x)
                        if tt == 0:
                            _wait(vector, ev_caB)
                            _wait(vector, ev_saB)
                    block.vector(vw)
                    ev_h = emit_ln_h(nc, block, ch, xbuf[par], stats, xc, hbuf, hbuf,
                                     caB, saB, SC, ev_sq=ch["T"].ev())
                    evs_cons.append(ev_h)
                    def w_guard(vector, tt=tt, par=par):
                        if tt >= 2:
                            _wait(vector, evs_st[tt - 2])
                    if tt >= 2:
                        block.vector(lambda vector, tt=tt, par=par: _wait(vector, evs_st[tt - 2]))
                    else:
                        evs_st = []
                    ev_sp = emit_transpose_split(nc, block, ch, hbuf, D, ident, psT_,
                                                 tmpf, hTs[par][0], hTs[par][1], 0,
                                                 ev_pre=ev_h)
                    def sh(gpsimd, tt=tt, par=par, ev_sp=ev_sp):
                        _wait(gpsimd, ev_sp)
                        ch["SH%d" % par].inc(
                            gpsimd.dma_start(out=h1T_hi_d[tt, :, :], in_=hTs[par][0][:]), 16)
                        ch["SH%d" % par].inc(
                            gpsimd.dma_start(out=h1T_lo_d[tt, :, :], in_=hTs[par][1][:]), 16)
                    block.gpsimd(sh)
                    evs_st.append(ch["SH%d" % par].ev())

            ev_allst = [ch["SH0"].ev(), ch["SH1"].ev()]

            with ExitStack() as st_at:
                E3 = st_at.enter_context
                wqNat = E3(nc.sbuf_tensor("wqNat", [128, D], F32))
                wqT_hi = E3(nc.sbuf_tensor("wqT_hi", [128, NDC * 384], BF16))
                wqT_lo = E3(nc.sbuf_tensor("wqT_lo", [128, NDC * 384], BF16))
                hh = [E3(nc.sbuf_tensor("hh%d" % i, [128, D], BF16)) for i in range(2)]
                hl = [E3(nc.sbuf_tensor("hl%d" % i, [128, D], BF16)) for i in range(2)]
                qkvS = E3(nc.sbuf_tensor("qkvS", [128, 384], F32))
                qrot = E3(nc.sbuf_tensor("qrot", [128, 256], F32))
                qT_hi = E3(nc.sbuf_tensor("qT_hi", [128, S], BF16))
                qT_lo = E3(nc.sbuf_tensor("qT_lo", [128, S], BF16))
                kT_hi = E3(nc.sbuf_tensor("kT_hi", [128, S], BF16))
                kT_lo = E3(nc.sbuf_tensor("kT_lo", [128, S], BF16))
                v_hi = E3(nc.sbuf_tensor("v_hi", [128, S], BF16))
                v_lo = E3(nc.sbuf_tensor("v_lo", [128, S], BF16))
                sc_buf = E3(nc.sbuf_tensor("sc_buf", [128, S], F32))
                p32e = E3(nc.sbuf_tensor("p32e", [128, 512], F32))
                PT_hi = E3(nc.sbuf_tensor("PT_hi", [128, 512], BF16))
                PT_lo = E3(nc.sbuf_tensor("PT_lo", [128, 512], BF16))
                mxc = E3(nc.sbuf_tensor("mxc", [128, 16], F32))
                smc = E3(nc.sbuf_tensor("smc", [128, 16], F32))
                mneg = E3(nc.sbuf_tensor("mneg", [128, 1], F32))
                rT = E3(nc.sbuf_tensor("rT", [128, 1], F32))
                OTs = E3(nc.sbuf_tensor("OTs", [128, 128], F32))
                Ofin = E3(nc.sbuf_tensor("Ofin", [128, 128], F32))

                ev_headdone = None
                evs_evc = []
                for hd_i in range(4):
                    for et in range(3):
                        def ldw(sync, hd_i=hd_i, et=et):
                            if et == 0 and hd_i == 0:
                                _wait(sync, ev_allst[0])
                                _wait(sync, ev_allst[1])
                            elif et == 0:
                                _wait(sync, ev_headdone)
                            else:
                                _wait(sync, evs_wt[-1])
                            ch["LD0"].inc(sync.dma_start(
                                out=wqNat[:],
                                in_=wq[384 * hd_i + 128 * et:
                                       384 * hd_i + 128 * et + 128, :]), 16)
                        block.sync(ldw)
                        ev_wq = ch["LD0"].ev()
                        for g0 in range(0, NDC, 4):
                            def t_tw(tensor, g0=g0, ev_wq=ev_wq, first=(et == 0 and
                                     g0 == 0 and hd_i == 0)):
                                _wait(tensor, ev_wq)
                                if not first:
                                    _wait(tensor, evs_wt[-1])
                                last = None
                                for k in range(4):
                                    last = tensor.transpose(
                                        psT_[:, 128 * k:128 * k + 128],
                                        wqNat[:, (g0 + k) * 128:(g0 + k) * 128 + 128],
                                        ident[:])
                                T.inc(last)
                            if et == 0 and g0 == 0 and hd_i == 0:
                                evs_wt = []
                            block.tensor(t_tw)
                            ev_t = T.ev()

                            def v_tw(vector, et=et, g0=g0, ev_t=ev_t):
                                _wait(vector, ev_t)
                                for k in range(4):
                                    dc = g0 + k
                                    o = 384 * dc + 128 * et
                                    vector.tensor_copy(wqT_hi[:, o:o + 128],
                                                       psT_[:, 128 * k:128 * k + 128])
                                    vector.tensor_tensor(tmpf[:, 0:128],
                                                         psT_[:, 128 * k:128 * k + 128],
                                                         wqT_hi[:, o:o + 128], OP.subtract)
                                    vector.tensor_copy(wqT_lo[:, o:o + 128], tmpf[:, 0:128])
                                V.inc(vector.engine_nop())
                            block.vector(v_tw)
                            evs_wt.append(V.ev())
                    ev_wqT = V.ev()

                    evs_qkv_done = []
                    for tt in range(NT):
                        par = tt % 2

                        def ldh(sync, tt=tt, par=par, hd_i=hd_i):
                            if tt >= 2:
                                _wait(sync, evs_qkv_done[tt - 2])
                            elif hd_i == 0:
                                _wait(sync, ev_allst[0])
                                _wait(sync, ev_allst[1])
                            ch["LH%d" % par].inc(sync.dma_start(
                                out=hh[par][:], in_=h1T_hi_d[tt, :, :]), 16)
                            ch["LH%d" % par].inc(sync.dma_start(
                                out=hl[par][:], in_=h1T_lo_d[tt, :, :]), 16)
                        block.sync(ldh)
                        ev_h1 = ch["LH%d" % par].ev()

                        def t_qkv(tensor, tt=tt, par=par, ev_h1=ev_h1):
                            _wait(tensor, ev_h1)
                            _wait(tensor, ev_wqT)
                            if tt >= 1:
                                _wait(tensor, evs_evac[-1])
                            last = None
                            for dc in range(NDC):
                                a = hh[par][:, 128 * dc:128 * dc + 128]
                                b_ = hl[par][:, 128 * dc:128 * dc + 128]
                                wh = wqT_hi[:, 384 * dc:384 * dc + 384]
                                wl = wqT_lo[:, 384 * dc:384 * dc + 384]
                                last = tensor.matmul(psQ[:, 0:384], a, wh,
                                                     start=(dc == 0), stop=False)
                                last = tensor.matmul(psQ[:, 0:384], a, wl,
                                                     start=False, stop=False)
                                last = tensor.matmul(psQ[:, 0:384], b_, wh,
                                                     start=False,
                                                     stop=(dc == NDC - 1))
                            T.inc(last)
                        if tt == 0:
                            evs_evac = []
                        block.tensor(t_qkv)
                        ev_mm = T.ev()

                        def v_ev(vector, tt=tt, hd_i=hd_i, ev_mm=ev_mm):
                            _wait(vector, ev_mm)
                            vector.tensor_tensor(qkvS[:], psQ[:, 0:384],
                                                 bqB[:, 384 * hd_i:384 * hd_i + 384], OP.add)
                            cosL = cosTab[:, 64 * tt:64 * tt + 64]
                            sinL = sinTab[:, 64 * tt:64 * tt + 64]
                            for base in (0, 128):
                                src = qkvS[:, base:base + 128]
                                vector.tensor_mul(dump[:, 0:64], src[:, 0:64], cosL)
                                vector.tensor_mul(dump[:, 64:128], src[:, 64:128], cosL)
                                vector.tensor_mul(dump[:, 128:192], src[:, 64:128], sinL)
                                vector.tensor_mul(dump[:, 192:256], src[:, 0:64], sinL)
                                vector.tensor_tensor(qrot[:, base:base + 64],
                                                     dump[:, 0:64], dump[:, 128:192],
                                                     OP.subtract)
                                vector.tensor_tensor(qrot[:, base + 64:base + 128],
                                                     dump[:, 64:128], dump[:, 192:256],
                                                     OP.add)
                            vector.tensor_copy(v_hi[:, 128 * tt:128 * tt + 128],
                                               qkvS[:, 256:384])
                            vector.tensor_tensor(tmpf[:, 0:128], qkvS[:, 256:384],
                                                 v_hi[:, 128 * tt:128 * tt + 128],
                                                 OP.subtract)
                            V.inc(vector.tensor_copy(v_lo[:, 128 * tt:128 * tt + 128],
                                                     tmpf[:, 0:128]))
                        block.vector(v_ev)
                        ev_rope = V.ev()
                        evs_evac.append(ev_rope)

                        def t_tq(tensor, tt=tt, ev_rope=ev_rope):
                            _wait(tensor, ev_rope)
                            if tt >= 1:
                                _wait(tensor, evs_qkv_done[-1])
                            tensor.transpose(psT_[:, 0:128], qrot[:, 0:128], ident[:])
                            T.inc(tensor.transpose(psT_[:, 128:256], qrot[:, 128:256],
                                                   ident[:]))
                        block.tensor(t_tq)
                        ev_tq = T.ev()

                        def v_tq(vector, tt=tt, ev_tq=ev_tq):
                            _wait(vector, ev_tq)
                            c = 128 * tt
                            vector.tensor_scalar(tmpf[:, 0:128], psT_[:, 0:128], invB[:, 0:1],
                                                 None, OP.mult)
                            vector.tensor_copy(qT_hi[:, c:c + 128], tmpf[:, 0:128])
                            vector.tensor_tensor(tmpf[:, 128:256], tmpf[:, 0:128],
                                                 qT_hi[:, c:c + 128], OP.subtract)
                            vector.tensor_copy(qT_lo[:, c:c + 128], tmpf[:, 128:256])
                            vector.tensor_copy(kT_hi[:, c:c + 128], psT_[:, 128:256])
                            vector.tensor_tensor(tmpf[:, 128:256], psT_[:, 128:256],
                                                 kT_hi[:, c:c + 128], OP.subtract)
                            V.inc(vector.tensor_copy(kT_lo[:, c:c + 128], tmpf[:, 128:256]))
                        block.vector(v_tq)
                        evs_qkv_done.append(V.ev())
                    ev_qkvall = V.ev()

                    if dbg and hd_i == 0:
                        def g_dbg(gpsimd, ev_qkvall=ev_qkvall):
                            _wait(gpsimd, ev_qkvall)
                            ch["SO"].inc(gpsimd.dma_start(out=qT_dbg[:], in_=qT_hi[:]), 16)
                            ch["SO"].inc(gpsimd.dma_start(out=kT_dbg[:], in_=kT_hi[:]), 16)
                            ch["SO"].inc(gpsimd.dma_start(out=v_dbg[:], in_=v_hi[:]), 16)
                        block.gpsimd(g_dbg)

                    for qt in range(NT):
                        nkc = qt // 4 + 1
                        r = qt % 4
                        ev_sc_free = ch["SC"].ev()
                        for kc in range(nkc):
                            diag = (kc == nkc - 1)

                            def t_sc(tensor, qt=qt, kc=kc, ev_qkvall=ev_qkvall):
                                _wait(tensor, ev_qkvall)
                                if evs_evc:
                                    _wait(tensor, evs_evc[-1])
                                q0 = 128 * qt
                                c = 512 * kc
                                tensor.matmul(psA[:], qT_hi[:, q0:q0 + 128],
                                              kT_hi[:, c:c + 512], start=True, stop=False)
                                tensor.matmul(psA[:], qT_lo[:, q0:q0 + 128],
                                              kT_hi[:, c:c + 512], start=False, stop=False)
                                T.inc(tensor.matmul(psA[:], qT_hi[:, q0:q0 + 128],
                                                    kT_lo[:, c:c + 512], start=False,
                                                    stop=True))
                            block.tensor(t_sc)
                            ev_s = T.ev()

                            def v_evac(vector, kc=kc, diag=diag, r=r, ev_s=ev_s,
                                       ev_sc_free=ev_sc_free):
                                _wait(vector, ev_s)
                                if kc == 0:
                                    _wait(vector, ev_sc_free)
                                if diag:
                                    vector.tensor_tensor(psA[:], psA[:],
                                                         maskA[:, 512 * r:512 * r + 512],
                                                         OP.add)
                                V.inc(vector.tensor_scalar(
                                    sc_buf[:, 512 * kc:512 * kc + 512], psA[:], 1.0,
                                    -3.0e38, OP.mult, OP.max,
                                    accum_out=mxc[:, kc:kc + 1]))
                            block.vector(v_evac)
                            evs_evc.append(V.ev())

                        def v_mx(vector, nkc=nkc, ev_last=evs_evc[-1]):
                            _wait(vector, ev_last)
                            V.inc(vector.reduce_max(mxc[:, 8:9], mxc[:, 0:nkc], axis=AX.X))
                            _wait(vector, V.ev())
                            V.inc(vector.tensor_scalar_mul(mneg[:], mxc[:, 8:9], -1.0))
                        block.vector(v_mx)
                        ev_mx = V.ev()

                        for kc in range(nkc):
                            def s_exp(scalar, kc=kc, ev_mx=ev_mx, ev_ptr=ch["T"].ev()):
                                _wait(scalar, ev_mx)
                                _wait(scalar, ev_ptr)
                                SC.inc(scalar.activation(
                                    p32e[:], sc_buf[:, 512 * kc:512 * kc + 512],
                                    ACT.Exp, bias=mneg[:, 0:1],
                                    accum_out=smc[:, kc:kc + 1]))
                            block.scalar(s_exp)
                            ev_e = SC.ev()

                            def t_tr(tensor, ev_e=ev_e, ev_vfree=ch["V"].ev()):
                                _wait(tensor, ev_e)
                                _wait(tensor, ev_vfree)
                                last = None
                                for i in range(4):
                                    last = tensor.transpose(
                                        psT_[:, 128 * i:128 * i + 128],
                                        p32e[:, 128 * i:128 * i + 128], ident[:])
                                T.inc(last)
                            block.tensor(t_tr)
                            ev_tr = T.ev()

                            def s_hi(scalar, ev_tr=ev_tr):
                                _wait(scalar, ev_tr)
                                SC.inc(scalar.activation(PT_hi[:], psT_[:, 0:512],
                                                         ACT.Copy, bias=0.0))
                            block.scalar(s_hi)
                            ev_hi = SC.ev()

                            def v_plo(vector, ev_hi=ev_hi):
                                _wait(vector, ev_hi)
                                V.inc(vector.tensor_tensor(PT_lo[:], psT_[:, 0:512],
                                                           PT_hi[:], OP.subtract))
                            block.vector(v_plo)
                            ev_lo = V.ev()

                            def t_pv(tensor, kc=kc, nkc=nkc, ev_lo=ev_lo):
                                _wait(tensor, ev_lo)
                                last = None
                                for i in range(4):
                                    kt = 4 * kc + i
                                    first = (kc == 0 and i == 0)
                                    fin = (kc == nkc - 1 and i == 3)
                                    c = 128 * kt
                                    tensor.matmul(psO[:, 0:128], v_hi[:, c:c + 128],
                                                  PT_hi[:, 128 * i:128 * i + 128],
                                                  start=first, stop=False)
                                    tensor.matmul(psO[:, 0:128], v_lo[:, c:c + 128],
                                                  PT_hi[:, 128 * i:128 * i + 128],
                                                  start=False, stop=False)
                                    last = tensor.matmul(psO[:, 0:128], v_hi[:, c:c + 128],
                                                         PT_lo[:, 128 * i:128 * i + 128],
                                                         start=False, stop=fin)
                                T.inc(last)
                            block.tensor(t_pv)
                        ev_pv_last = T.ev()

                        def v_fin(vector, nkc=nkc, ev_sc=ch["SC"].ev(),
                                  ev_pv=ev_pv_last):
                            _wait(vector, ev_sc)
                            V.inc(vector.reduce_sum(smc[:, 8:9], smc[:, 0:nkc], axis=AX.X))
                            _wait(vector, V.ev())
                            V.inc(vector.reciprocal(rT[:], smc[:, 8:9]))
                            _wait(vector, V.ev())
                            _wait(vector, ev_pv)
                            V.inc(vector.tensor_copy(OTs[:], psO[:, 0:128]))
                        block.vector(v_fin)
                        ev_ots = V.ev()

                        def t_to(tensor, ev_ots=ev_ots):
                            _wait(tensor, ev_ots)
                            T.inc(tensor.transpose(psB[:, 0:128], OTs[:], ident[:]))
                        block.tensor(t_to)
                        ev_to = T.ev()

                        def v_of(vector, ev_to=ev_to, ev_so=ch["SO"].ev()):
                            _wait(vector, ev_to)
                            _wait(vector, ev_so)
                            V.inc(vector.tensor_scalar(Ofin[:], psB[:, 0:128], rT[:, 0:1],
                                                       None, OP.mult))
                        block.vector(v_of)
                        ev_of = V.ev()

                        def g_of(gpsimd, qt=qt, hd_i=hd_i, ev_of=ev_of):
                            _wait(gpsimd, ev_of)
                            ch["SO"].inc(gpsimd.dma_start(
                                out=O_out[128 * qt:128 * qt + 128,
                                          128 * hd_i:128 * hd_i + 128],
                                in_=Ofin[:]), 16)
                        block.gpsimd(g_of)
                    ev_headdone = ch["SO"].ev()

        def g_ag1(gpsimd):
            _wait(gpsimd, ch["SO"].ev())
            CC.inc(gpsimd.collective_compute(
                "AllGather", OP.bypass, replica_groups=CC_GROUPS,
                ins=[O_out.ap().opt()], outs=[O_all.ap().opt()]), 1)
        block.gpsimd(g_ag1)
        ev_ag1 = CC.ev()

        with ExitStack() as st2:
            E = st2.enter_context
            gaB = E(nc.sbuf_tensor("gaB", [128, D], F32))
            cbB = E(nc.sbuf_tensor("cbB", [128, D], F32))
            sbB = E(nc.sbuf_tensor("sbB", [128, D], F32))
            rga = E(nc.sbuf_tensor("rga", [1, D], F32))
            rcl = E(nc.sbuf_tensor("rcl", [1, D], F32))
            rsb = E(nc.sbuf_tensor("rsb", [1, D], F32))
            rw1 = E(nc.sbuf_tensor("rw1", [1, D], F32))
            rw2 = E(nc.sbuf_tensor("rw2", [1, D], F32))
            xb2 = E(nc.sbuf_tensor("xb2", [128, D], F32))
            ob2 = E(nc.sbuf_tensor("ob2", [128, D], F32))
            x1b = E(nc.sbuf_tensor("x1b", [128, D], F32))
            xc2 = E(nc.sbuf_tensor("xc2", [128, D], F32))
            hb2 = E(nc.sbuf_tensor("hb2", [128, D], F32))
            h2s = [(E(nc.sbuf_tensor("h2sh%d" % i, [128, D], BF16)),
                    E(nc.sbuf_tensor("h2sl%d" % i, [128, D], BF16)))
                   for i in range(2)]
            psT2 = E(nc.psum_tensor("psT2", [128, 512], F32))

            def ld_r2(sync):
                _wait(sync, ev_ag1)
                ch["LD0"].inc(sync.dma_start(out=rga[0:1, 0:D],
                                             in_=mod_out[0:1, 2 * D:3 * D]), 16)
                ch["LD0"].inc(sync.dma_start(out=rcl[0:1, 0:D],
                                             in_=mod_out[0:1, 4 * D:5 * D]), 16)
                ch["LD0"].inc(sync.dma_start(out=rsb[0:1, 0:D],
                                             in_=mod_out[0:1, 3 * D:4 * D]), 16)
            block.sync(ld_r2)
            ev_r2 = ch["LD0"].ev()

            def v_w2(vector):
                _wait(vector, ev_r2)
                V.inc(vector.engine_nop())
            block.vector(v_w2)
            ev_ga2 = emit_clamp_row(nc, block, ch, rga, 0, rw1, rw2)
            ev_gaB = emit_bcast_row(nc, block, ch, onesR, rw1, 0, gaB, D, psB,
                                    ev_ga2)

            def v_cb2(vector):
                _wait(vector, ev_gaB)
                V.inc(vector.tensor_scalar_add(rw1[0:1, 0:D], rcl[0:1, 0:D], 1.0))
            block.vector(v_cb2)
            ev_cbB = emit_bcast_row(nc, block, ch, onesR, rw1, 0, cbB, D, psB,
                                    V.ev())
            ev_sbB = emit_bcast_row(nc, block, ch, onesR, rsb, 0, sbB, D, psB,
                                    ev_r2)

            evs2_cons = []
            evs2_st = []
            for tt in range(NT):
                par = tt % 2

                def ldx2(sync, tt=tt, par=par):
                    if tt >= 1:
                        _wait(sync, evs2_cons[tt - 1])
                    else:
                        _wait(sync, ev_sbB)
                    ch["LX%d" % par].inc(sync.dma_start(
                        out=xb2[:], in_=xb[128 * tt:128 * tt + 128, :]), 16)
                    for k in range(4):
                        ch["LX%d" % par].inc(sync.dma_start(
                            out=ob2[:, 512 * k:512 * k + 512],
                            in_=O_all[k, 128 * tt:128 * tt + 128, :]), 16)
                block.sync(ldx2)
                ev_x = ch["LX%d" % par].ev()

                def v_x12(vector, ev_x=ev_x, ev_so=ch["SO"].ev()):
                    _wait(vector, ev_x)
                    _wait(vector, ev_cbB)
                    _wait(vector, ev_so)
                    vector.tensor_tensor(x1b[:], ob2[:], gaB[:], OP.mult)
                    V.inc(vector.tensor_add(x1b[:], x1b[:], xb2[:]))
                block.vector(v_x12)
                ev_x1 = V.ev()

                def g_x12(gpsimd, tt=tt, ev_x1=ev_x1):
                    _wait(gpsimd, ev_x1)
                    ch["SO"].inc(gpsimd.dma_start(
                        out=x1_own[128 * tt:128 * tt + 128, :], in_=x1b[:]), 16)
                block.gpsimd(g_x12)

                ev_h = emit_ln_h(nc, block, ch, x1b, stats, xc2, hb2, hb2,
                                 cbB, sbB, SC, ev_sq=ch["T"].ev())
                evs2_cons.append(ev_h)
                if tt >= 2:
                    block.vector(lambda vector, tt=tt: _wait(vector,
                                                             evs2_st[tt - 2]))
                ev_sp = emit_transpose_split(nc, block, ch, hb2, D, ident, psT2,
                                             tmpf, h2s[par][0], h2s[par][1], 0,
                                             ev_pre=ev_h)

                def sh2(gpsimd, tt=tt, par=par, ev_sp=ev_sp):
                    _wait(gpsimd, ev_sp)
                    for dc in range(NDC):
                        ch["SH%d" % par].inc(gpsimd.dma_start(
                            out=h2hi_own[dc, :, 128 * tt:128 * tt + 128],
                            in_=h2s[par][0][:, 128 * dc:128 * dc + 128]), 16)
                        ch["SH%d" % par].inc(gpsimd.dma_start(
                            out=h2lo_own[dc, :, 128 * tt:128 * tt + 128],
                            in_=h2s[par][1][:, 128 * dc:128 * dc + 128]), 16)
                block.gpsimd(sh2)
                evs2_st.append(ch["SH%d" % par].ev())
        ev_ph2 = [ch["SH0"].ev(), ch["SH1"].ev(), ch["SO"].ev()]

        with ExitStack() as st3:
            E3o = st3.enter_context
            lutB = E3o(nc.sbuf_tensor("lutB", [128, 16], F32))
            mskj = E3o(nc.sbuf_tensor("mskj", [128, NJT], F32))
            gbB = E3o(nc.sbuf_tensor("gbB", [128, D], F32))

            def ld3(sync):
                for e in ev_ph2:
                    _wait(sync, e)
                ch["LD0"].inc(sync.dma_start(out=lutB[0:1, :], in_=lutr[:]), 16)
                ch["LD0"].inc(sync.dma_start(out=mskj[:], in_=maskj[:]), 16)
                ch["LD0"].inc(sync.dma_start(out=rw1_3[0:1, 0:D],
                                             in_=mod_out[0:1, 5 * D:6 * D]), 16)
            ev_lutB = None

            with ExitStack() as stg:
                E2 = stg.enter_context
                rw1_3 = E2(nc.sbuf_tensor("rw1_3", [1, D], F32))
                rw2_3 = E2(nc.sbuf_tensor("rw2_3", [1, D], F32))
                rw3_3 = E2(nc.sbuf_tensor("rw3_3", [1, D], F32))
                idxb = [E2(nc.sbuf_tensor("idxb%d" % i, [128, 2048], I32))
                        for i in range(2)]
                acc = E2(nc.sbuf_tensor("acc", [128, 2048], F32))
                gtmp = E2(nc.sbuf_tensor("gtmp", [128, 2048], F32))
                whi = [E2(nc.sbuf_tensor("whi%d" % i, [128, 2048], BF16))
                       for i in range(2)]
                wlo = [E2(nc.sbuf_tensor("wlo%d" % i, [128, 2048], BF16))
                       for i in range(2)]
                psT3 = E2(nc.psum_tensor("psT3g", [128, 512], F32))

                block.sync(ld3)
                ev_ld3 = ch["LD0"].ev()

                def v03(vector):
                    _wait(vector, ev_ld3)
                    V.inc(vector.engine_nop())
                block.vector(v03)
                ev_lutB = emit_bcast_row(nc, block, ch, onesR, lutB, 0, lutB, 16,
                                         psB, V.ev())
                ev_gb = emit_clamp_row(nc, block, ch, rw1_3, 0, rw2_3, rw3_3)
                ev_gbB = emit_bcast_row(nc, block, ch, onesR, rw2_3, 0, gbB, D,
                                        psB, ev_gb)

                jobs = []
                for jt in range(NJT):
                    jobs.append((gidx, 128 * jt, D, wgh_d, wgl_d, jt, jt))
                for jt in range(NJT):
                    jobs.append((uidx, 128 * jt, D, wuh_d, wul_d, jt, jt))
                for ot in range(NDC):
                    jobs.append((didx, 128 * ot, JSL, wdh_d, wdl_d, ot, None))

                evs_jdone = []
                for ji, (srcd, r0, wdt, dsth, dstl, blk, mcol) in enumerate(jobs):
                    par = ji % 2

                    def ldi(sync, ji=ji, par=par, srcd=srcd, r0=r0, wdt=wdt):
                        if ji >= 2:
                            _wait(sync, evs_jdone[ji - 2])
                        else:
                            _wait(sync, ev_gbB)
                        ch["LI%d" % par].inc(sync.dma_start(
                            out=idxb[par][:, 0:wdt], in_=srcd[r0:r0 + 128, 0:wdt]),
                            16)
                    block.sync(ldi)
                    ev_i = ch["LI%d" % par].ev()

                    def v_g(vector, ji=ji, par=par, wdt=wdt, mcol=mcol, ev_i=ev_i,
                            ev_tprev=ch["T"].ev()):
                        _wait(vector, ev_i)
                        _wait(vector, ev_tprev)
                        src = idxb[par][:, 0:wdt]
                        vector.tensor_scalar(acc[:, 0:wdt], src, 0.0, lutB[:, 0:1],
                                             OP.is_equal, OP.mult)
                        for v in range(1, 16):
                            vector.tensor_scalar(gtmp[:, 0:wdt], src, float(v),
                                                 lutB[:, v:v + 1], OP.is_equal,
                                                 OP.mult)
                            vector.tensor_tensor(acc[:, 0:wdt], acc[:, 0:wdt],
                                                 gtmp[:, 0:wdt], OP.add)
                        if mcol is not None:
                            vector.tensor_scalar(acc[:, 0:wdt], acc[:, 0:wdt],
                                                 mskj[:, mcol:mcol + 1], None,
                                                 OP.mult)
                        V.inc(vector.engine_nop())
                    block.vector(v_g)
                    ev_acc = V.ev()

                    nt_ = wdt // 128
                    for g0 in range(0, nt_, 4):
                        gn = min(4, nt_ - g0)

                        def t_g(tensor, g0=g0, gn=gn, ev_acc=ev_acc, ji=ji):
                            _wait(tensor, ev_acc)
                            if g0 > 0:
                                _wait(tensor, evs_gt[-1])
                            elif ji >= 2:
                                _wait(tensor, evs_jdone[ji - 2])
                            last = None
                            for k in range(gn):
                                last = tensor.transpose(
                                    psT3[:, 128 * k:128 * k + 128],
                                    acc[:, (g0 + k) * 128:(g0 + k) * 128 + 128],
                                    ident[:])
                            T.inc(last)
                        if g0 == 0:
                            evs_gt = []
                        block.tensor(t_g)
                        ev_t = T.ev()

                        def v_t(vector, g0=g0, gn=gn, par=par, ev_t=ev_t):
                            _wait(vector, ev_t)
                            w = 128 * gn
                            o = 128 * g0
                            vector.tensor_copy(whi[par][:, o:o + w], psT3[:, 0:w])
                            vector.tensor_tensor(tmpf[:, 0:w], psT3[:, 0:w],
                                                 whi[par][:, o:o + w], OP.subtract)
                            V.inc(vector.tensor_copy(wlo[par][:, o:o + w],
                                                     tmpf[:, 0:w]))
                        block.vector(v_t)
                        evs_gt.append(V.ev())
                    ev_sp3 = V.ev()

                    def g_w(gpsimd, ji=ji, par=par, dsth=dsth, dstl=dstl, blk=blk,
                            wdt=wdt, ev_sp3=ev_sp3):
                        _wait(gpsimd, ev_sp3)
                        ch["SW%d" % par].inc(gpsimd.dma_start(
                            out=dsth[blk, :, 0:wdt], in_=whi[par][:, 0:wdt]), 16)
                        ch["SW%d" % par].inc(gpsimd.dma_start(
                            out=dstl[blk, :, 0:wdt], in_=wlo[par][:, 0:wdt]), 16)
                    block.gpsimd(g_w)
                    evs_jdone.append(ch["SW%d" % par].ev())

            ev_gall = [ch["SW0"].ev(), ch["SW1"].ev()]

            TBW = 512
            NTB = S // TBW
            NTTL = TBW // 128
            with ExitStack() as stf:
                E3 = stf.enter_context
                h2h = E3(nc.sbuf_tensor("h2h", [128, NDC * TBW], BF16))
                h2l = E3(nc.sbuf_tensor("h2l", [128, NDC * TBW], BF16))
                wb_h = [E3(nc.sbuf_tensor("wbh%d" % i, [128, D], BF16))
                        for i in range(2)]
                wb_l = [E3(nc.sbuf_tensor("wbl%d" % i, [128, D], BF16))
                        for i in range(2)]
                wub_h = [E3(nc.sbuf_tensor("wubh%d" % i, [128, D], BF16))
                         for i in range(2)]
                wub_l = [E3(nc.sbuf_tensor("wubl%d" % i, [128, D], BF16))
                         for i in range(2)]
                gsil = E3(nc.sbuf_tensor("gsil", [128, 512], F32))
                hfh = E3(nc.sbuf_tensor("hfh", [128, NJT * TBW], BF16))
                hfl = E3(nc.sbuf_tensor("hfl", [128, NJT * TBW], BF16))
                wdhs = E3(nc.sbuf_tensor("wdhs", [128, NDC * JSL], BF16))
                wdls = E3(nc.sbuf_tensor("wdls", [128, NDC * JSL], BF16))
                partS = E3(nc.sbuf_tensor("partS", [128, D], F32))
                x1f = E3(nc.sbuf_tensor("x1f", [128, D], F32))
                psG = E3(nc.psum_tensor("psG", [128, 512], F32))
                psU = E3(nc.psum_tensor("psU", [128, 512], F32))
                psD = [E3(nc.psum_tensor("psD%d" % i, [128, 512], F32))
                       for i in range(4)]

                def ldwd(sync):
                    _wait(sync, ev_gall[0])
                    _wait(sync, ev_gall[1])
                    for ot in range(NDC):
                        ch["LH0"].inc(sync.dma_start(
                            out=wdhs[:, JSL * ot:JSL * ot + JSL],
                            in_=wdh_d[ot, :, 0:JSL]), 16)
                        ch["LH0"].inc(sync.dma_start(
                            out=wdls[:, JSL * ot:JSL * ot + JSL],
                            in_=wdl_d[ot, :, 0:JSL]), 16)
                block.sync(ldwd)
                ev_wd = ch["LH0"].ev()

                for tb in range(NTB):
                    def ldh2(sync, tb=tb):
                        if tb == 0:
                            _wait(sync, ev_gall[0])
                            _wait(sync, ev_gall[1])
                        else:
                            _wait(sync, evs_tb[-1])
                        for dc in range(NDC):
                            ch["LH0"].inc(sync.dma_start(
                                out=h2h[:, TBW * dc:TBW * dc + TBW],
                                in_=h2hi_own[dc, :, TBW * tb:TBW * tb + TBW]), 16)
                            ch["LH0"].inc(sync.dma_start(
                                out=h2l[:, TBW * dc:TBW * dc + TBW],
                                in_=h2lo_own[dc, :, TBW * tb:TBW * tb + TBW]), 16)
                    if tb == 0:
                        evs_tb = []
                    block.sync(ldh2)
                    ev_h2 = ch["LH0"].ev()

                    evs_hf = []
                    for jt in range(NJT):
                        par = jt % 2

                        def ldwg(sync, jt=jt, par=par, tb=tb):
                            if jt >= 2:
                                _wait(sync, evs_hf[jt - 2])
                            elif tb > 0:
                                _wait(sync, evs_hf_prev[-1])
                            ch["LW%d" % par].inc(sync.dma_start(
                                out=wb_h[par][:], in_=wgh_d[jt, :, :]), 16)
                            ch["LW%d" % par].inc(sync.dma_start(
                                out=wb_l[par][:], in_=wgl_d[jt, :, :]), 16)
                            ch["LW%d" % par].inc(sync.dma_start(
                                out=wub_h[par][:], in_=wuh_d[jt, :, :]), 16)
                            ch["LW%d" % par].inc(sync.dma_start(
                                out=wub_l[par][:], in_=wul_d[jt, :, :]), 16)
                        block.sync(ldwg)
                        ev_w = ch["LW%d" % par].ev()

                        def t_gu(tensor, jt=jt, par=par, ev_w=ev_w, ev_h2=ev_h2,
                                 tb=tb):
                            _wait(tensor, ev_w)
                            _wait(tensor, ev_h2)
                            if jt > 0:
                                _wait(tensor, evs_guev[-1])
                            elif tb > 0:
                                _wait(tensor, evs_guev_prev[-1])
                            last = None
                            for dc in range(NDC):
                                rh = h2h[:, TBW * dc:TBW * dc + TBW]
                                rl = h2l[:, TBW * dc:TBW * dc + TBW]
                                for (ps, wh_, wl_) in ((psG, wb_h[par], wb_l[par]),
                                                       (psU, wub_h[par],
                                                        wub_l[par])):
                                    a = wh_[:, 128 * dc:128 * dc + 128]
                                    bb = wl_[:, 128 * dc:128 * dc + 128]
                                    tensor.matmul(ps[:], a, rh, start=(dc == 0),
                                                  stop=False)
                                    tensor.matmul(ps[:], a, rl, start=False,
                                                  stop=False)
                                    last = tensor.matmul(ps[:], bb, rh, start=False,
                                                         stop=(dc == NDC - 1))
                            T.inc(last)
                        if jt == 0:
                            evs_guev_prev = evs_guev if tb > 0 else None
                            evs_guev = []
                        block.tensor(t_gu)
                        ev_mm = T.ev()

                        def s_si(scalar, ev_mm=ev_mm):
                            _wait(scalar, ev_mm)
                            SC.inc(scalar.activation(gsil[:], psG[:], ACT.Silu))
                        block.scalar(s_si)
                        ev_si = SC.ev()

                        def v_hf(vector, jt=jt, ev_si=ev_si, ev_tdn=ch["T"].ev()):
                            _wait(vector, ev_si)
                            if jt == 0:
                                _wait(vector, ev_tdn)
                            o = TBW * jt
                            vector.tensor_tensor(tmpf[:, 0:512], psU[:], gsil[:],
                                                 OP.mult)
                            vector.tensor_copy(hfh[:, o:o + 512], tmpf[:, 0:512])
                            vector.tensor_tensor(dump[:, 0:512], tmpf[:, 0:512],
                                                 hfh[:, o:o + 512], OP.subtract)
                            V.inc(vector.tensor_copy(hfl[:, o:o + 512],
                                                     dump[:, 0:512]))
                        block.vector(v_hf)
                        evs_guev.append(V.ev())
                        evs_hf.append(V.ev())
                    evs_hf_prev = evs_hf
                    ev_hf_all = V.ev()

                    for ttl in range(NTTL):
                        def ldx3(sync, tb=tb, ttl=ttl, ev0=ev_gbB):
                            if tb == 0 and ttl == 0:
                                _wait(sync, ev0)
                            else:
                                _wait(sync, evs_dn[-1])
                            r = TBW * tb + 128 * ttl
                            ch["LX0"].inc(sync.dma_start(
                                out=x1f[:], in_=x1_own[r:r + 128, :]), 16)
                        if ttl == 0 and tb == 0:
                            evs_dn = []
                        block.sync(ldx3)
                        ev_xf = ch["LX0"].ev()

                        def t_dn(tensor, ttl=ttl, tb=tb, ev_hf_all=ev_hf_all):
                            _wait(tensor, ev_hf_all)
                            _wait(tensor, ev_wd)
                            if evs_dn:
                                _wait(tensor, evs_dn[-1])
                            last = None
                            for ot in range(NDC):
                                ps = psD[ot // 4]
                                oc = 128 * (ot % 4)
                                for jt in range(NJT):
                                    sh = hfh[:, TBW * jt + 128 * ttl:
                                              TBW * jt + 128 * ttl + 128]
                                    sl = hfl[:, TBW * jt + 128 * ttl:
                                              TBW * jt + 128 * ttl + 128]
                                    rh = wdhs[:, JSL * ot + 128 * jt:
                                              JSL * ot + 128 * jt + 128]
                                    rl = wdls[:, JSL * ot + 128 * jt:
                                              JSL * ot + 128 * jt + 128]
                                    first = (jt == 0)
                                    lastf = (jt == NJT - 1)
                                    tensor.matmul(ps[:, oc:oc + 128], sh, rh,
                                                  start=first, stop=False)
                                    tensor.matmul(ps[:, oc:oc + 128], sh, rl,
                                                  start=False, stop=False)
                                    last = tensor.matmul(ps[:, oc:oc + 128], sl, rh,
                                                         start=False, stop=lastf)
                            T.inc(last)
                        block.tensor(t_dn)
                        ev_dm = T.ev()

                        def v_dn(vector, ttl=ttl, ev_dm=ev_dm, ev_xf=ev_xf,
                                 ev_so=ch["SO"].ev()):
                            _wait(vector, ev_dm)
                            _wait(vector, ev_xf)
                            _wait(vector, ev_so)
                            for i in range(4):
                                vector.tensor_tensor(
                                    partS[:, 512 * i:512 * i + 512], psD[i][:],
                                    gbB[:, 512 * i:512 * i + 512], OP.mult)
                                vector.scalar_tensor_tensor(
                                    partS[:, 512 * i:512 * i + 512],
                                    x1f[:, 512 * i:512 * i + 512], 0.25,
                                    partS[:, 512 * i:512 * i + 512],
                                    OP.mult, OP.add)
                            V.inc(vector.engine_nop())
                        block.vector(v_dn)
                        evs_dn.append(V.ev())

                        def g_dn(gpsimd, ttl=ttl, tb=tb, ev_v=evs_dn[-1]):
                            _wait(gpsimd, ev_v)
                            r = TBW * tb + 128 * ttl
                            ch["SO"].inc(gpsimd.dma_start(
                                out=c_own[r:r + 128, :], in_=partS[:]), 16)
                        block.gpsimd(g_dn)
                    evs_tb.append(V.ev())

        def g_rs(gpsimd):
            _wait(gpsimd, ch["SO"].ev())
            CC.inc(gpsimd.collective_compute(
                "ReduceScatter", OP.add, replica_groups=CC_GROUPS,
                ins=[c_own.ap().opt()], outs=[y_q.ap().opt()]), 1)
            _wait(gpsimd, CC.ev())
            ch["SO"].inc(gpsimd.dma_start(out=y_out[:, :], in_=y_q[:, :]), 16)
            _wait(gpsimd, ch["SO"].ev())
        block.gpsimd(g_rs)
    return nc


def kernel(L0_attn_hidden_states, L0_attn_w_qkv, L0_attn_b_qkv, L0_ffn_lut,
           L0_ffn_gate_idx, L0_ffn_up_idx, L0_ffn_down_idx, L0_c, L0_wmod):
    x = np.ascontiguousarray(L0_attn_hidden_states)
    wqkv = np.ascontiguousarray(L0_attn_w_qkv)
    bqkv = np.ascontiguousarray(L0_attn_b_qkv)
    lut = np.ascontiguousarray(L0_ffn_lut)
    gi = np.ascontiguousarray(L0_ffn_gate_idx)
    ui = np.ascontiguousarray(L0_ffn_up_idx)
    di = np.ascontiguousarray(L0_ffn_down_idx)
    c = np.ascontiguousarray(L0_c)
    wmod = np.ascontiguousarray(L0_wmod)
    tb = _host_tables()
    cores = list(range(8))
    jsl = _jslices()

    nc = build_fused()
    maps = []
    for cc in cores:
        b, g = cc // 4, cc % 4
        o0, osz = jsl[g]
        gip = np.zeros((JSL, D), dtype=np.int32)
        uip = np.zeros((JSL, D), dtype=np.int32)
        dip = np.zeros((D, JSL), dtype=np.int32)
        gip[:osz] = gi[o0:o0 + osz]
        uip[:osz] = ui[o0:o0 + osz]
        dip[:, :osz] = di[:, o0:o0 + osz]
        mj = np.zeros((128, NJT), dtype=np.float32)
        mjf = np.zeros(JSL, dtype=np.float32)
        mjf[:osz] = 1.0
        mj[:, :] = mjf.reshape(NJT, 128).T
        maps.append(dict(
            xb=x[b], wq=wqkv[EH * g:EH * (g + 1)],
            bq=bqkv.reshape(1, EH * 4)[:, EH * g:EH * (g + 1)],
            cCol=np.ascontiguousarray(c[b].reshape(16, 128).T),
            wmod=wmod, embT=tb["embT"], ksT=tb["ksT"], kcT=tb["kcT"],
            ident=tb["ident"], maskA=tb["maskA"],
            gidx=gip, uidx=uip, didx=dip,
            lutr=lut.reshape(1, 16), maskj=mj,
        ))
    r = run_bass_kernel_spmd(nc, maps, core_ids=cores)
    y = np.empty((B, S, D), dtype=np.float32)
    for cc in cores:
        b, g = cc // 4, cc % 4
        y[b, TQ * g:TQ * (g + 1)] = np.asarray(r.results[cc]["y_out"])
    return y


# version 06
# speedup vs baseline: 1.1106x; 1.1106x over previous
import numpy as np
from contextlib import ExitStack
import concourse.bass as bass
import concourse.mybir as mybir
from concourse.bass_utils import run_bass_kernel_spmd

F32 = mybir.dt.float32
BF16 = mybir.dt.bfloat16
I32 = mybir.dt.int32
OP = mybir.AluOpType
ACT = mybir.ActivationFunctionType
AX = mybir.AxisListType

B, S, D, H, HD = 2, 4096, 2048, 16, 128
INTER = 5504
NT = S // 128
NDC = D // 128
NQ = 4
TQ = S // NQ
NTQ = TQ // 128
JSL = 1408
NJT = JSL // 128
EH = 4 * 3 * HD
NORM32 = float(np.float32(np.sqrt(np.float32(128.0))))
C1 = float(np.float32(2 * np.pi))
C2 = float(np.float32(2 * np.pi - np.float64(np.float32(2 * np.pi))))
EPS = 1e-5
INV_D = 1.0 / float(D)

_CACHE = {}


def _host_tables():
    if "t" in _CACHE:
        return _CACHE["t"]
    import jax
    import jax.numpy as jnp
    with jax.default_device(jax.devices("cpu")[0]):
        inv_freq = 1.0 / 10000.0 ** (jnp.arange(0, HD, 2, dtype=jnp.float32) / HD)
        t = jnp.arange(S, dtype=jnp.float32)
        freqs = jnp.einsum("i,j->ij", t, inv_freq)
        emb32 = np.asarray(freqs)
    emb64 = emb32.astype(np.float64)
    ks = np.round(emb64 / (2 * np.pi))
    kc = np.round(emb64 / (2 * np.pi) + 0.25) - 0.25
    assert np.all(np.abs(ks) < 2 ** 11) and np.all(np.abs(kc * 4) % 1 == 0)

    def dev_layout(a):
        return np.ascontiguousarray(
            a.reshape(NT, 128, 64).transpose(1, 0, 2).reshape(128, NT * 64)
        ).astype(np.float32)

    ident = np.eye(128, dtype=np.float32)
    maskA = np.zeros((128, 4 * 512), dtype=np.float32)
    for r in range(4):
        q_i = np.arange(128)[:, None]
        k_i = np.arange(512)[None, :]
        maskA[:, 512 * r:512 * (r + 1)] = np.where(k_i <= 128 * r + q_i,
                                                   0.0, -1.0e30)
    _CACHE["t"] = dict(
        embT=dev_layout(emb32), ksT=dev_layout(ks), kcT=dev_layout(kc),
        ident=ident, maskA=maskA,
    )
    return _CACHE["t"]


def _wmod_relayout(wmod):
    return np.ascontiguousarray(
        wmod.reshape(16, 128, 24, 512).transpose(1, 2, 0, 3).reshape(128, 24 * 8192))


def _jslices():
    out = []
    for g in range(NQ):
        o0 = 1408 * g
        osz = 1408 if g < 3 else INTER - 1408 * 3
        out.append((o0, osz))
    return out


class Chan:

    def __init__(self, nc, stack, name):
        self.h = stack.enter_context(nc.semaphore(name))
        self.n = 0

    def inc(self, instr, k=1):
        instr.then_inc(self.h, k)
        self.n += k
        return (self.h, self.n)

    def ev(self):
        return (self.h, self.n)


def _wait(eng, ev):
    if ev is not None:
        eng.wait_ge(ev[0], ev[1])



def emit_mod_phase(nc, block, st, ch, cCol, wmod_dram, mod_out, rowM, psM,
                   ev_pre=None):
    T, V, SC = ch["T"], ch["V"], ch["SC"]
    scS = st.enter_context(nc.sbuf_tensor("scS", [128, 16], F32))
    wmW = st.enter_context(nc.sbuf_tensor("wmW", [128, 2 * 16 * 512], F32))

    def sc_fn(scalar):
        _wait(scalar, ch["LD0"].ev())
        SC.inc(scalar.activation(scS[:], cCol[:], ACT.Silu))
    block.scalar(sc_fn)
    ev_silu = SC.ev()

    njc = 12288 // 512
    evs_v = []
    evs_st = []
    for jc in range(njc):
        par = jc % 2

        def ld_fn(sync, jc=jc, par=par):
            if jc >= 2:
                _wait(sync, evs_v[jc - 2])
            else:
                _wait(sync, ev_pre)
            ch["LW%d" % par].inc(
                sync.dma_start(out=wmW[:, par * 8192:par * 8192 + 8192],
                               in_=wmod_dram[:, 8192 * jc:8192 * jc + 8192]), 16)
        block.sync(ld_fn)
        ev_ld = ch["LW%d" % par].ev()

        def t_fn(tensor, jc=jc, par=par, ev_ld=ev_ld):
            _wait(tensor, ev_ld)
            _wait(tensor, ev_silu)
            if jc >= 1:
                _wait(tensor, evs_v[jc - 1])
            last = None
            for dc in range(NDC):
                last = tensor.matmul(psM[0:1, 0:512], scS[:, dc:dc + 1],
                                     wmW[:, (par * 16 + dc) * 512:(par * 16 + dc) * 512 + 512],
                                     start=(dc == 0), stop=(dc == NDC - 1))
            T.inc(last)
        block.tensor(t_fn)
        ev_mm = T.ev()

        def v_fn(vector, jc=jc, par=par, ev_mm=ev_mm):
            _wait(vector, ev_mm)
            if jc >= 2:
                _wait(vector, evs_st[jc - 2])
            V.inc(vector.tensor_copy(rowM[0:1, 512 * par:512 * par + 512],
                                     psM[0:1, 0:512]))
        block.vector(v_fn)
        evs_v.append(V.ev())

        def g_fn(gpsimd, jc=jc, par=par, ev_v=evs_v[-1]):
            _wait(gpsimd, ev_v)
            ch["SO"].inc(gpsimd.dma_start(
                out=mod_out[0:1, 512 * jc:512 * jc + 512],
                in_=rowM[0:1, 512 * par:512 * par + 512]), 16)
        block.gpsimd(g_fn)
        evs_st.append(ch["SO"].ev())
    return evs_st


def emit_clamp_row(nc, block, ch, modrow, off, outR, tmpR):
    V = ch["V"]

    def v_fn(vector):
        g = modrow[0:1, off:off + D]
        vector.tensor_scalar(tmpR[0:1, 0:D], g, 0.0, 2.0, OP.is_ge, OP.mult)
        vector.tensor_scalar_add(tmpR[0:1, 0:D], tmpR[0:1, 0:D], -1.0)
        vector.tensor_scalar_mul(outR[0:1, 0:D], g, -1.0)
        vector.tensor_max(outR[0:1, 0:D], g, outR[0:1, 0:D])
        vector.tensor_scalar_max(outR[0:1, 0:D], outR[0:1, 0:D], 0.25)
        V.inc(vector.tensor_mul(outR[0:1, 0:D], outR[0:1, 0:D], tmpR[0:1, 0:D]))
    block.vector(v_fn)
    return V.ev()


def emit_bcast_row(nc, block, ch, onesR, srcR, s_off, dstB, width, psB, ev_pre=None):
    T, V = ch["T"], ch["V"]
    nch = (width + 511) // 512
    for k in range(nch):
        w = min(512, width - 512 * k)
        ev_vpre = ch["V"].ev()

        def t_fn(tensor, k=k, w=w, ev_vpre=ev_vpre):
            _wait(tensor, ev_pre)
            _wait(tensor, ev_vpre)
            T.inc(tensor.matmul(psB[:, 0:w], onesR[0:1, 0:128],
                                srcR[0:1, s_off + 512 * k:s_off + 512 * k + w],
                                start=True, stop=True))
        block.tensor(t_fn)
        ev_mm = T.ev()

        def v_fn(vector, k=k, w=w, ev_mm=ev_mm):
            _wait(vector, ev_mm)
            V.inc(vector.tensor_copy(dstB[:, 512 * k:512 * k + w], psB[:, 0:w]))
        block.vector(v_fn)
    return V.ev()


def emit_ln_h(nc, block, ch, xbuf, stats, xc, dump, hout, scaleB, shiftB, sc_ch,
              ev_sq=None):
    V, SC = ch["V"], ch["SC"]
    ev_xc = ch["SC"].ev()

    def v1(vector):
        _wait(vector, ev_xc)
        V.inc(vector.reduce_sum(stats[:, 0:1], xbuf[:], axis=AX.X))
        _wait(vector, V.ev())
        V.inc(vector.tensor_scalar_mul(stats[:, 1:2], stats[:, 0:1], INV_D))
        _wait(vector, V.ev())
        V.inc(vector.tensor_scalar(xc[:], xbuf[:], stats[:, 1:2], None, OP.subtract))
    block.vector(v1)
    ev1 = V.ev()

    def s1(scalar):
        _wait(scalar, ev1)
        _wait(scalar, ev_sq)
        SC.inc(scalar.activation(dump[:], xc[:], ACT.Square))
    block.scalar(s1)
    ev2 = SC.ev()

    def v2(vector):
        _wait(vector, ev2)
        V.inc(vector.reduce_sum(stats[:, 2:3], dump[:], axis=AX.X))
        _wait(vector, V.ev())
        V.inc(vector.tensor_scalar_mul(stats[:, 3:4], stats[:, 2:3], INV_D))
    block.vector(v2)
    ev3 = V.ev()

    def s2(scalar):
        _wait(scalar, ev3)
        SC.inc(scalar.activation(stats[:, 4:5], stats[:, 3:4], ACT.Sqrt,
                                 bias=stats[:, 15:16]))
    block.scalar(s2)
    ev4 = SC.ev()

    def v3(vector):
        _wait(vector, ev4)
        V.inc(vector.reciprocal(stats[:, 5:6], stats[:, 4:5]))
        _wait(vector, V.ev())
        vector.scalar_tensor_tensor(hout[:], xc[:], stats[:, 5:6], scaleB[:],
                                    OP.mult, OP.mult)
        V.inc(vector.tensor_add(hout[:], hout[:], shiftB[:]))
    block.vector(v3)
    return V.ev()


def emit_transpose_split(nc, block, ch, src_f32, ncols, ident, psT, tmpf,
                         dst_hi, dst_lo, dst_off, ev_pre=None, scale_ap=None):
    T, V = ch["T"], ch["V"]
    nt = ncols // 128
    for g0 in range(0, nt, 4):
        gn = min(4, nt - g0)

        def t_fn(tensor, g0=g0, gn=gn):
            _wait(tensor, ev_pre)
            if g0 > 0:
                _wait(tensor, ch["V"].ev() if False else evs[-1])
            last = None
            for k in range(gn):
                last = tensor.transpose(psT[:, 128 * k:128 * k + 128],
                                        src_f32[:, (g0 + k) * 128:(g0 + k) * 128 + 128],
                                        ident[:])
            T.inc(last)
        if g0 == 0:
            evs = []
        block.tensor(t_fn)
        ev_t = T.ev()

        def v_fn(vector, g0=g0, gn=gn, ev_t=ev_t):
            _wait(vector, ev_t)
            w = 128 * gn
            o = dst_off + 128 * g0
            if scale_ap is not None:
                vector.tensor_scalar(tmpf[:, 0:w], psT[:, 0:w], scale_ap, None, OP.mult)
                vector.tensor_copy(dst_hi[:, o:o + w], tmpf[:, 0:w])
                vector.tensor_tensor(tmpf[:, 0:w], tmpf[:, 0:w], dst_hi[:, o:o + w],
                                     OP.subtract)
            else:
                vector.tensor_copy(dst_hi[:, o:o + w], psT[:, 0:w])
                vector.tensor_tensor(tmpf[:, 0:w], psT[:, 0:w], dst_hi[:, o:o + w],
                                     OP.subtract)
            V.inc(vector.tensor_copy(dst_lo[:, o:o + w], tmpf[:, 0:w]))
        block.vector(v_fn)
        evs.append(V.ev())
    return V.ev()


def emit_split_hilo(nc, block, ch, src, dst_hi, dst_lo, tmpf, w, ev_pre=None):
    V = ch["V"]

    def v_fn(vector):
        _wait(vector, ev_pre)
        vector.tensor_copy(dst_hi[:, 0:w], src[:, 0:w])
        vector.tensor_tensor(tmpf[:, 0:w], src[:, 0:w], dst_hi[:, 0:w], OP.subtract)
        V.inc(vector.tensor_copy(dst_lo[:, 0:w], tmpf[:, 0:w]))
    block.vector(v_fn)
    return V.ev()



CC_GROUPS = [[0, 1, 2, 3], [4, 5, 6, 7]]


def build_fused(dbg=False):
    nc = bass.Bass("TRN2", target_bir_lowering=False)
    xb = nc.dram_tensor("xb", [S, D], F32, kind="ExternalInput")
    wq = nc.dram_tensor("wq", [EH, D], F32, kind="ExternalInput")
    bq = nc.dram_tensor("bq", [1, EH], F32, kind="ExternalInput")
    cCol_d = nc.dram_tensor("cCol", [128, 16], F32, kind="ExternalInput")
    wmod = nc.dram_tensor("wmod", [128, 24 * 8192], F32, kind="ExternalInput")
    embT_d = nc.dram_tensor("embT", [128, NT * 64], F32, kind="ExternalInput")
    ksT_d = nc.dram_tensor("ksT", [128, NT * 64], F32, kind="ExternalInput")
    kcT_d = nc.dram_tensor("kcT", [128, NT * 64], F32, kind="ExternalInput")
    ident_d = nc.dram_tensor("ident", [128, 128], F32, kind="ExternalInput")
    maskA_d = nc.dram_tensor("maskA", [128, 4 * 512], F32, kind="ExternalInput")
    gidx = nc.dram_tensor("gidx", [JSL, D], I32, kind="ExternalInput")
    uidx = nc.dram_tensor("uidx", [JSL, D], I32, kind="ExternalInput")
    didx = nc.dram_tensor("didx", [D, JSL], I32, kind="ExternalInput")
    lutr = nc.dram_tensor("lutr", [1, 16], F32, kind="ExternalInput")
    maskj = nc.dram_tensor("maskj", [128, NJT], F32, kind="ExternalInput")
    y_out = nc.dram_tensor("y_out", [TQ, D], F32, kind="ExternalOutput")
    O_out = nc.dram_tensor("O_own", [S, 512], F32)
    mod_out = nc.dram_tensor("mod_own", [1, 6 * D], F32)
    h1T_hi_d = nc.dram_tensor("h1T_hi", [NT, 128, D], BF16)
    h1T_lo_d = nc.dram_tensor("h1T_lo", [NT, 128, D], BF16)
    O_all = nc.dram_tensor("O_all", [4, S, 512], F32)
    x1_own = nc.dram_tensor("x1_own", [S, D], F32)
    h2hi_own = nc.dram_tensor("h2hi", [NDC, 128, S], BF16)
    h2lo_own = nc.dram_tensor("h2lo", [NDC, 128, S], BF16)
    wgh_d = nc.dram_tensor("wgh", [NJT, 128, D], BF16)
    wgl_d = nc.dram_tensor("wgl", [NJT, 128, D], BF16)
    wuh_d = nc.dram_tensor("wuh", [NJT, 128, D], BF16)
    wul_d = nc.dram_tensor("wul", [NJT, 128, D], BF16)
    wdh_d = nc.dram_tensor("wdh", [NDC, 128, JSL], BF16)
    wdl_d = nc.dram_tensor("wdl", [NDC, 128, JSL], BF16)
    c_own = nc.dram_tensor("c_own", [S, D], F32)
    y_q = nc.dram_tensor("y_q", [TQ, D], F32)

    with ExitStack() as st:
        E0 = st.enter_context
        ch = {}
        for name in ["T", "V", "SC", "LD0", "LW0", "LW1", "LX0", "LX1",
                     "LH0", "LH1", "SO", "SH0", "SH1", "LI0", "LI1",
                     "SW0", "SW1", "CC"]:
            ch[name] = Chan(nc, st, "c_" + name)
        ch["LD"] = ch["LD0"]
        T, V, SC, CC = ch["T"], ch["V"], ch["SC"], ch["CC"]
        block = E0(nc.Block())

        ident = E0(nc.sbuf_tensor("idents", [128, 128], F32))
        onesR = E0(nc.sbuf_tensor("onesR", [1, 128], F32))
        invB = E0(nc.sbuf_tensor("invB", [128, 1], F32))
        nrm = E0(nc.sbuf_tensor("nrm", [1, 1], F32))
        stats = E0(nc.sbuf_tensor("stats", [128, 16], F32))
        tmpf = E0(nc.sbuf_tensor("tmpf", [128, 512], F32))
        dump = E0(nc.sbuf_tensor("dump", [128, 512], F32))
        psB = E0(nc.psum_tensor("psB", [128, 512], F32))

        with ExitStack() as st1:
            E = st1.enter_context
            cCol = E(nc.sbuf_tensor("cCols", [128, 16], F32))
            maskA = E(nc.sbuf_tensor("maskAs", [128, 4 * 512], F32))
            cosT = E(nc.sbuf_tensor("cosT", [128, NT * 64], F32))
            sinT = E(nc.sbuf_tensor("sinT", [128, NT * 64], F32))
            rowA = E(nc.sbuf_tensor("rowA", [1, D], F32))
            rowS = E(nc.sbuf_tensor("rowS", [1, D], F32))
            rowM = E(nc.sbuf_tensor("rowM", [1, 1024], F32))
            caB = E(nc.sbuf_tensor("caB", [128, D], F32))
            saB = E(nc.sbuf_tensor("saB", [128, D], F32))
            bqB = E(nc.sbuf_tensor("bqB", [128, EH], F32))
            psA = E(nc.psum_tensor("psA", [128, 512], F32))
            psA2 = E(nc.psum_tensor("psA2", [128, 512], F32))
            psO = E(nc.psum_tensor("psO", [128, 512], F32))
            psT_ = E(nc.psum_tensor("psT", [128, 512], F32))
            psQ = E(nc.psum_tensor("psQ", [128, 512], F32))
            psM = E(nc.psum_tensor("psM", [128, 512], F32))
            with ExitStack() as st_rope:
                ER = st_rope.enter_context
                embT = ER(nc.sbuf_tensor("embTs", [128, NT * 64], F32))
                ksT = ER(nc.sbuf_tensor("ksTs", [128, NT * 64], F32))
                kcT = ER(nc.sbuf_tensor("kcTs", [128, NT * 64], F32))
                rtmp = ER(nc.sbuf_tensor("rtmp", [128, NT * 64], F32))

                def ld0(sync):
                    for dst, src in [(cCol, cCol_d), (ident, ident_d), (maskA, maskA_d),
                                     (embT, embT_d), (ksT, ksT_d), (kcT, kcT_d)]:
                        ch["LD0"].inc(sync.dma_start(out=dst[:], in_=src[:]), 16)
                    ch["LD0"].inc(sync.dma_start(out=bqB[0:1, :], in_=bq[:]), 16)
                block.sync(ld0)
                ev_ld0 = ch["LD0"].ev()

                def v0(vector):
                    vector.memset(onesR[:], 1.0)
                    vector.memset(stats[:, 15:16], EPS)
                    V.inc(vector.memset(nrm[:], NORM32))
                    _wait(vector, V.ev())
                    V.inc(vector.reciprocal(nrm[0:1, 0:1], nrm[0:1, 0:1]))
                block.vector(v0)
                ev_ones = V.ev()

                def v_rope(vector):
                    _wait(vector, ev_ld0)
                    vector.tensor_scalar(rtmp[:], ksT[:], C1, None, OP.mult)
                    vector.tensor_tensor(cosT[:], embT[:], rtmp[:], OP.subtract)
                    vector.tensor_scalar(rtmp[:], ksT[:], C2, None, OP.mult)
                    vector.tensor_tensor(cosT[:], cosT[:], rtmp[:], OP.subtract)
                    vector.tensor_scalar(rtmp[:], kcT[:], C1, None, OP.mult)
                    vector.tensor_tensor(sinT[:], embT[:], rtmp[:], OP.subtract)
                    vector.tensor_scalar(rtmp[:], kcT[:], C2, None, OP.mult)
                    V.inc(vector.tensor_tensor(sinT[:], sinT[:], rtmp[:], OP.subtract))
                block.vector(v_rope)
                ev_r = V.ev()

                def s_rope(scalar):
                    _wait(scalar, ev_r)
                    scalar.activation(cosT[:], cosT[:], ACT.Sin)
                    SC.inc(scalar.activation(sinT[:], sinT[:], ACT.Sin))
                block.scalar(s_rope)
                ev_tables = SC.ev()
            sinTab, cosTab = cosT, sinT

            def t_inv(tensor):
                _wait(tensor, ev_ones)
                T.inc(tensor.matmul(psB[:, 0:1], onesR[0:1, 0:128], nrm[0:1, 0:1],
                                    start=True, stop=True))
            block.tensor(t_inv)
            ev_ti = T.ev()

            def v_inv(vector):
                _wait(vector, ev_ti)
                V.inc(vector.tensor_copy(invB[:], psB[:, 0:1]))
            block.vector(v_inv)

            with ExitStack() as st_ln:
                E2 = st_ln.enter_context
                xbuf = [E2(nc.sbuf_tensor("xb%d" % i, [128, D], F32)) for i in range(2)]
                xc = E2(nc.sbuf_tensor("xc", [128, D], F32))
                hbuf = E2(nc.sbuf_tensor("hbuf", [128, D], F32))
                hTs = [(E2(nc.sbuf_tensor("hTh%d" % i, [128, D], BF16)),
                        E2(nc.sbuf_tensor("hTl%d" % i, [128, D], BF16))) for i in range(2)]

                with ExitStack() as st_mod:
                    evs_mod = emit_mod_phase(nc, block, st_mod, ch, cCol, wmod,
                                             mod_out, rowM, psM, ev_pre=ev_tables)
                    ev_mod = evs_mod[-1]

                def ld_rows(sync):
                    _wait(sync, evs_mod[7])
                    ch["LD0"].inc(sync.dma_start(out=rowA[0:1, 0:D],
                                                 in_=mod_out[0:1, D:2 * D]), 16)
                    ch["LD0"].inc(sync.dma_start(out=rowS[0:1, 0:D],
                                                 in_=mod_out[0:1, 0:D]), 16)
                block.sync(ld_rows)
                ev_rows = ch["LD0"].ev()

                def v_rows(vector):
                    _wait(vector, ev_rows)
                    V.inc(vector.tensor_scalar_add(rowA[0:1, 0:D], rowA[0:1, 0:D],
                                                   1.0))
                block.vector(v_rows)
                ev_ca1 = V.ev()
                ev_caB = emit_bcast_row(nc, block, ch, onesR, rowA, 0, caB, D,
                                        psB, ev_ca1)
                ev_saB = emit_bcast_row(nc, block, ch, onesR, rowS, 0, saB, D,
                                        psB, ev_rows)
                ev_bqB = emit_bcast_row(nc, block, ch, onesR, bqB, 0, bqB, EH,
                                        psB, ev_ld0)
                evs_cons = []
                evs_hT = []
                for tt in range(NT):
                    par = tt % 2

                    def ldx(sync, tt=tt, par=par):
                        if tt >= 2:
                            _wait(sync, evs_cons[tt - 2])
                        else:
                            _wait(sync, ev_caB)
                        ch["LX%d" % par].inc(
                            sync.dma_start(out=xbuf[par][:],
                                           in_=xb[128 * tt:128 * tt + 128, :]), 16)
                    block.sync(ldx)
                    ev_x = ch["LX%d" % par].ev()

                    def vw(vector, par=par, ev_x=ev_x, tt=tt):
                        _wait(vector, ev_x)
                        if tt == 0:
                            _wait(vector, ev_caB)
                            _wait(vector, ev_saB)
                    block.vector(vw)
                    ev_h = emit_ln_h(nc, block, ch, xbuf[par], stats, xc, hbuf, hbuf,
                                     caB, saB, SC, ev_sq=ch["T"].ev())
                    evs_cons.append(ev_h)
                    def w_guard(vector, tt=tt, par=par):
                        if tt >= 2:
                            _wait(vector, evs_st[tt - 2])
                    if tt >= 2:
                        block.vector(lambda vector, tt=tt, par=par: _wait(vector, evs_st[tt - 2]))
                    else:
                        evs_st = []
                    ev_sp = emit_transpose_split(nc, block, ch, hbuf, D, ident, psT_,
                                                 tmpf, hTs[par][0], hTs[par][1], 0,
                                                 ev_pre=ev_h)
                    def sh(gpsimd, tt=tt, par=par, ev_sp=ev_sp):
                        _wait(gpsimd, ev_sp)
                        ch["SH%d" % par].inc(
                            gpsimd.dma_start(out=h1T_hi_d[tt, :, :], in_=hTs[par][0][:]), 16)
                        ch["SH%d" % par].inc(
                            gpsimd.dma_start(out=h1T_lo_d[tt, :, :], in_=hTs[par][1][:]), 16)
                    block.gpsimd(sh)
                    evs_st.append(ch["SH%d" % par].ev())

            ev_allst = [ch["SH0"].ev(), ch["SH1"].ev()]

            with ExitStack() as st_at:
                E3 = st_at.enter_context
                wqNat = E3(nc.sbuf_tensor("wqNat", [128, D], F32))
                wqT_hi = E3(nc.sbuf_tensor("wqT_hi", [128, NDC * 384], BF16))
                wqT_lo = E3(nc.sbuf_tensor("wqT_lo", [128, NDC * 384], BF16))
                hh = [E3(nc.sbuf_tensor("hh%d" % i, [128, D], BF16)) for i in range(2)]
                hl = [E3(nc.sbuf_tensor("hl%d" % i, [128, D], BF16)) for i in range(2)]
                qkvS = E3(nc.sbuf_tensor("qkvS", [128, 384], F32))
                qrot = E3(nc.sbuf_tensor("qrot", [128, 256], F32))
                qT_hi = E3(nc.sbuf_tensor("qT_hi", [128, S], BF16))
                qT_lo = E3(nc.sbuf_tensor("qT_lo", [128, S], BF16))
                kT_hi = E3(nc.sbuf_tensor("kT_hi", [128, S], BF16))
                kT_lo = E3(nc.sbuf_tensor("kT_lo", [128, S], BF16))
                v_hi = E3(nc.sbuf_tensor("v_hi", [128, S], BF16))
                v_lo = E3(nc.sbuf_tensor("v_lo", [128, S], BF16))
                sc_buf = E3(nc.sbuf_tensor("sc_buf", [128, S], F32))
                p32e = E3(nc.sbuf_tensor("p32e", [128, 512], F32))
                PT_hi = E3(nc.sbuf_tensor("PT_hi", [128, 512], BF16))
                PT_lo = E3(nc.sbuf_tensor("PT_lo", [128, 512], BF16))
                mxc = E3(nc.sbuf_tensor("mxc", [128, 16], F32))
                smc = E3(nc.sbuf_tensor("smc", [128, 16], F32))
                mneg = E3(nc.sbuf_tensor("mneg", [128, 1], F32))
                rT = E3(nc.sbuf_tensor("rT", [128, 1], F32))
                OTs = E3(nc.sbuf_tensor("OTs", [128, 128], F32))
                Ofin = E3(nc.sbuf_tensor("Ofin", [128, 128], F32))

                ev_headdone = None
                ev_tr_last = None
                evs_evc = []
                for hd_i in range(4):
                    for et in range(3):
                        def ldw(sync, hd_i=hd_i, et=et):
                            if et == 0 and hd_i == 0:
                                _wait(sync, ev_allst[0])
                                _wait(sync, ev_allst[1])
                                _wait(sync, ev_mod)
                            elif et == 0:
                                _wait(sync, ev_headdone)
                            else:
                                _wait(sync, evs_wt[-1])
                            ch["LD0"].inc(sync.dma_start(
                                out=wqNat[:],
                                in_=wq[384 * hd_i + 128 * et:
                                       384 * hd_i + 128 * et + 128, :]), 16)
                        block.sync(ldw)
                        ev_wq = ch["LD0"].ev()
                        for g0 in range(0, NDC, 4):
                            def t_tw(tensor, g0=g0, ev_wq=ev_wq, first=(et == 0 and
                                     g0 == 0 and hd_i == 0)):
                                _wait(tensor, ev_wq)
                                if not first:
                                    _wait(tensor, evs_wt[-1])
                                last = None
                                for k in range(4):
                                    last = tensor.transpose(
                                        psT_[:, 128 * k:128 * k + 128],
                                        wqNat[:, (g0 + k) * 128:(g0 + k) * 128 + 128],
                                        ident[:])
                                T.inc(last)
                            if et == 0 and g0 == 0 and hd_i == 0:
                                evs_wt = []
                            block.tensor(t_tw)
                            ev_t = T.ev()

                            def v_tw(vector, et=et, g0=g0, ev_t=ev_t):
                                _wait(vector, ev_t)
                                for k in range(4):
                                    dc = g0 + k
                                    o = 384 * dc + 128 * et
                                    vector.tensor_copy(wqT_hi[:, o:o + 128],
                                                       psT_[:, 128 * k:128 * k + 128])
                                    vector.tensor_tensor(tmpf[:, 0:128],
                                                         psT_[:, 128 * k:128 * k + 128],
                                                         wqT_hi[:, o:o + 128], OP.subtract)
                                    vector.tensor_copy(wqT_lo[:, o:o + 128], tmpf[:, 0:128])
                                V.inc(vector.engine_nop())
                            block.vector(v_tw)
                            evs_wt.append(V.ev())
                    ev_wqT = V.ev()

                    evs_qkv_done = []
                    for tt in range(NT):
                        par = tt % 2

                        def ldh(sync, tt=tt, par=par, hd_i=hd_i):
                            if tt >= 2:
                                _wait(sync, evs_qkv_done[tt - 2])
                            elif hd_i == 0:
                                _wait(sync, ev_allst[0])
                                _wait(sync, ev_allst[1])
                                _wait(sync, ev_mod)
                            ch["LH%d" % par].inc(sync.dma_start(
                                out=hh[par][:], in_=h1T_hi_d[tt, :, :]), 16)
                            ch["LH%d" % par].inc(sync.dma_start(
                                out=hl[par][:], in_=h1T_lo_d[tt, :, :]), 16)
                        block.sync(ldh)
                        ev_h1 = ch["LH%d" % par].ev()

                        def t_qkv(tensor, tt=tt, par=par, ev_h1=ev_h1):
                            _wait(tensor, ev_h1)
                            _wait(tensor, ev_wqT)
                            if tt >= 1:
                                _wait(tensor, evs_evac[-1])
                            last = None
                            for dc in range(NDC):
                                a = hh[par][:, 128 * dc:128 * dc + 128]
                                b_ = hl[par][:, 128 * dc:128 * dc + 128]
                                wh = wqT_hi[:, 384 * dc:384 * dc + 384]
                                wl = wqT_lo[:, 384 * dc:384 * dc + 384]
                                last = tensor.matmul(psQ[:, 0:384], a, wh,
                                                     start=(dc == 0), stop=False)
                                last = tensor.matmul(psQ[:, 0:384], a, wl,
                                                     start=False, stop=False)
                                last = tensor.matmul(psQ[:, 0:384], b_, wh,
                                                     start=False,
                                                     stop=(dc == NDC - 1))
                            T.inc(last)
                        if tt == 0:
                            evs_evac = []
                        block.tensor(t_qkv)
                        ev_mm = T.ev()

                        def v_ev(vector, tt=tt, hd_i=hd_i, ev_mm=ev_mm):
                            _wait(vector, ev_mm)
                            vector.tensor_tensor(qkvS[:], psQ[:, 0:384],
                                                 bqB[:, 384 * hd_i:384 * hd_i + 384], OP.add)
                            cosL = cosTab[:, 64 * tt:64 * tt + 64]
                            sinL = sinTab[:, 64 * tt:64 * tt + 64]
                            for base in (0, 128):
                                src = qkvS[:, base:base + 128]
                                vector.tensor_mul(dump[:, 0:64], src[:, 0:64], cosL)
                                vector.tensor_mul(dump[:, 64:128], src[:, 64:128], cosL)
                                vector.tensor_mul(dump[:, 128:192], src[:, 64:128], sinL)
                                vector.tensor_mul(dump[:, 192:256], src[:, 0:64], sinL)
                                vector.tensor_tensor(qrot[:, base:base + 64],
                                                     dump[:, 0:64], dump[:, 128:192],
                                                     OP.subtract)
                                vector.tensor_tensor(qrot[:, base + 64:base + 128],
                                                     dump[:, 64:128], dump[:, 192:256],
                                                     OP.add)
                            vector.tensor_copy(v_hi[:, 128 * tt:128 * tt + 128],
                                               qkvS[:, 256:384])
                            vector.tensor_tensor(tmpf[:, 0:128], qkvS[:, 256:384],
                                                 v_hi[:, 128 * tt:128 * tt + 128],
                                                 OP.subtract)
                            V.inc(vector.tensor_copy(v_lo[:, 128 * tt:128 * tt + 128],
                                                     tmpf[:, 0:128]))
                        block.vector(v_ev)
                        ev_rope = V.ev()
                        evs_evac.append(ev_rope)

                        def t_tq(tensor, tt=tt, ev_rope=ev_rope):
                            _wait(tensor, ev_rope)
                            if tt >= 1:
                                _wait(tensor, evs_qkv_done[-1])
                            tensor.transpose(psT_[:, 0:128], qrot[:, 0:128], ident[:])
                            T.inc(tensor.transpose(psT_[:, 128:256], qrot[:, 128:256],
                                                   ident[:]))
                        block.tensor(t_tq)
                        ev_tq = T.ev()

                        def v_tq(vector, tt=tt, ev_tq=ev_tq):
                            _wait(vector, ev_tq)
                            c = 128 * tt
                            vector.tensor_scalar(tmpf[:, 0:128], psT_[:, 0:128], invB[:, 0:1],
                                                 None, OP.mult)
                            vector.tensor_copy(qT_hi[:, c:c + 128], tmpf[:, 0:128])
                            vector.tensor_tensor(tmpf[:, 128:256], tmpf[:, 0:128],
                                                 qT_hi[:, c:c + 128], OP.subtract)
                            vector.tensor_copy(qT_lo[:, c:c + 128], tmpf[:, 128:256])
                            vector.tensor_copy(kT_hi[:, c:c + 128], psT_[:, 128:256])
                            vector.tensor_tensor(tmpf[:, 128:256], psT_[:, 128:256],
                                                 kT_hi[:, c:c + 128], OP.subtract)
                            V.inc(vector.tensor_copy(kT_lo[:, c:c + 128], tmpf[:, 128:256]))
                        block.vector(v_tq)
                        evs_qkv_done.append(V.ev())
                    ev_qkvall = V.ev()

                    if dbg and hd_i == 0:
                        def g_dbg(gpsimd, ev_qkvall=ev_qkvall):
                            _wait(gpsimd, ev_qkvall)
                            ch["SO"].inc(gpsimd.dma_start(out=qT_dbg[:], in_=qT_hi[:]), 16)
                            ch["SO"].inc(gpsimd.dma_start(out=kT_dbg[:], in_=kT_hi[:]), 16)
                            ch["SO"].inc(gpsimd.dma_start(out=v_dbg[:], in_=v_hi[:]), 16)
                        block.gpsimd(g_dbg)

                    for qt in range(NT):
                        nkc = qt // 4 + 1
                        r = qt % 4
                        ev_sc_free = ch["SC"].ev()
                        for kc in range(nkc):
                            diag = (kc == nkc - 1)
                            psAb = psA if (len(evs_evc) % 2 == 0) else psA2

                            def t_sc(tensor, qt=qt, kc=kc, psAb=psAb,
                                     ev_qkvall=ev_qkvall):
                                _wait(tensor, ev_qkvall)
                                if len(evs_evc) >= 2:
                                    _wait(tensor, evs_evc[-2])
                                q0 = 128 * qt
                                c = 512 * kc
                                tensor.matmul(psAb[:], qT_hi[:, q0:q0 + 128],
                                              kT_hi[:, c:c + 512], start=True, stop=False)
                                tensor.matmul(psAb[:], qT_lo[:, q0:q0 + 128],
                                              kT_hi[:, c:c + 512], start=False, stop=False)
                                T.inc(tensor.matmul(psAb[:], qT_hi[:, q0:q0 + 128],
                                                    kT_lo[:, c:c + 512], start=False,
                                                    stop=True))
                            block.tensor(t_sc)
                            ev_s = T.ev()

                            def v_evac(vector, kc=kc, diag=diag, r=r, ev_s=ev_s,
                                       psAb=psAb, ev_sc_free=ev_sc_free):
                                _wait(vector, ev_s)
                                if kc == 0:
                                    _wait(vector, ev_sc_free)
                                if diag:
                                    vector.tensor_tensor(psAb[:], psAb[:],
                                                         maskA[:, 512 * r:512 * r + 512],
                                                         OP.add)
                                V.inc(vector.tensor_scalar(
                                    sc_buf[:, 512 * kc:512 * kc + 512], psAb[:], 1.0,
                                    -3.0e38, OP.mult, OP.max,
                                    accum_out=mxc[:, kc:kc + 1]))
                            block.vector(v_evac)
                            evs_evc.append(V.ev())

                        def v_mx(vector, nkc=nkc, ev_last=evs_evc[-1]):
                            _wait(vector, ev_last)
                            V.inc(vector.reduce_max(mxc[:, 8:9], mxc[:, 0:nkc], axis=AX.X))
                            _wait(vector, V.ev())
                            V.inc(vector.tensor_scalar_mul(mneg[:], mxc[:, 8:9], -1.0))
                        block.vector(v_mx)
                        ev_mx = V.ev()

                        for kc in range(nkc):
                            def s_exp(scalar, kc=kc, ev_mx=ev_mx, ev_ptr=ev_tr_last):
                                _wait(scalar, ev_mx)
                                _wait(scalar, ev_ptr)
                                SC.inc(scalar.activation(
                                    p32e[:], sc_buf[:, 512 * kc:512 * kc + 512],
                                    ACT.Exp, bias=mneg[:, 0:1],
                                    accum_out=smc[:, kc:kc + 1]))
                            block.scalar(s_exp)
                            ev_e = SC.ev()

                            def t_tr(tensor, ev_e=ev_e, ev_vfree=ch["V"].ev()):
                                _wait(tensor, ev_e)
                                _wait(tensor, ev_vfree)
                                last = None
                                for i in range(4):
                                    last = tensor.transpose(
                                        psT_[:, 128 * i:128 * i + 128],
                                        p32e[:, 128 * i:128 * i + 128], ident[:])
                                T.inc(last)
                            block.tensor(t_tr)
                            ev_tr = T.ev()
                            ev_tr_last = ev_tr

                            def s_hi(scalar, ev_tr=ev_tr):
                                _wait(scalar, ev_tr)
                                SC.inc(scalar.activation(PT_hi[:], psT_[:, 0:512],
                                                         ACT.Copy, bias=0.0))
                            block.scalar(s_hi)
                            ev_hi = SC.ev()

                            def v_plo(vector, ev_hi=ev_hi):
                                _wait(vector, ev_hi)
                                V.inc(vector.tensor_tensor(PT_lo[:], psT_[:, 0:512],
                                                           PT_hi[:], OP.subtract))
                            block.vector(v_plo)
                            ev_lo = V.ev()

                            def t_pv(tensor, kc=kc, nkc=nkc, ev_lo=ev_lo):
                                _wait(tensor, ev_lo)
                                last = None
                                for i in range(4):
                                    kt = 4 * kc + i
                                    first = (kc == 0 and i == 0)
                                    fin = (kc == nkc - 1 and i == 3)
                                    c = 128 * kt
                                    tensor.matmul(psO[:, 0:128], v_hi[:, c:c + 128],
                                                  PT_hi[:, 128 * i:128 * i + 128],
                                                  start=first, stop=False)
                                    tensor.matmul(psO[:, 0:128], v_lo[:, c:c + 128],
                                                  PT_hi[:, 128 * i:128 * i + 128],
                                                  start=False, stop=False)
                                    last = tensor.matmul(psO[:, 0:128], v_hi[:, c:c + 128],
                                                         PT_lo[:, 128 * i:128 * i + 128],
                                                         start=False, stop=fin)
                                T.inc(last)
                            block.tensor(t_pv)
                        ev_pv_last = T.ev()

                        def v_fin(vector, nkc=nkc, ev_sc=ch["SC"].ev(),
                                  ev_pv=ev_pv_last):
                            _wait(vector, ev_sc)
                            V.inc(vector.reduce_sum(smc[:, 8:9], smc[:, 0:nkc], axis=AX.X))
                            _wait(vector, V.ev())
                            V.inc(vector.reciprocal(rT[:], smc[:, 8:9]))
                            _wait(vector, V.ev())
                            _wait(vector, ev_pv)
                            V.inc(vector.tensor_copy(OTs[:], psO[:, 0:128]))
                        block.vector(v_fin)
                        ev_ots = V.ev()

                        def t_to(tensor, ev_ots=ev_ots):
                            _wait(tensor, ev_ots)
                            T.inc(tensor.transpose(psB[:, 0:128], OTs[:], ident[:]))
                        block.tensor(t_to)
                        ev_to = T.ev()

                        def v_of(vector, ev_to=ev_to, ev_so=ch["SO"].ev()):
                            _wait(vector, ev_to)
                            _wait(vector, ev_so)
                            V.inc(vector.tensor_scalar(Ofin[:], psB[:, 0:128], rT[:, 0:1],
                                                       None, OP.mult))
                        block.vector(v_of)
                        ev_of = V.ev()

                        def g_of(gpsimd, qt=qt, hd_i=hd_i, ev_of=ev_of):
                            _wait(gpsimd, ev_of)
                            ch["SO"].inc(gpsimd.dma_start(
                                out=O_out[128 * qt:128 * qt + 128,
                                          128 * hd_i:128 * hd_i + 128],
                                in_=Ofin[:]), 16)
                        block.gpsimd(g_of)
                    ev_headdone = ch["SO"].ev()

        def g_ag1(gpsimd):
            _wait(gpsimd, ch["SO"].ev())
            CC.inc(gpsimd.collective_compute(
                "AllGather", OP.bypass, replica_groups=CC_GROUPS,
                ins=[O_out.ap().opt()], outs=[O_all.ap().opt()]), 1)
        block.gpsimd(g_ag1)
        ev_ag1 = CC.ev()

        with ExitStack() as st2:
            E = st2.enter_context
            gaB = E(nc.sbuf_tensor("gaB", [128, D], F32))
            cbB = E(nc.sbuf_tensor("cbB", [128, D], F32))
            sbB = E(nc.sbuf_tensor("sbB", [128, D], F32))
            rga = E(nc.sbuf_tensor("rga", [1, D], F32))
            rcl = E(nc.sbuf_tensor("rcl", [1, D], F32))
            rsb = E(nc.sbuf_tensor("rsb", [1, D], F32))
            rw1 = E(nc.sbuf_tensor("rw1", [1, D], F32))
            rw2 = E(nc.sbuf_tensor("rw2", [1, D], F32))
            xb2 = [E(nc.sbuf_tensor("xb2%d" % i, [128, D], F32)) for i in range(2)]
            ob2 = [E(nc.sbuf_tensor("ob2%d" % i, [128, D], F32)) for i in range(2)]
            x1b = E(nc.sbuf_tensor("x1b", [128, D], F32))
            xc2 = E(nc.sbuf_tensor("xc2", [128, D], F32))
            hb2 = E(nc.sbuf_tensor("hb2", [128, D], F32))
            h2s = [(E(nc.sbuf_tensor("h2sh%d" % i, [128, D], BF16)),
                    E(nc.sbuf_tensor("h2sl%d" % i, [128, D], BF16)))
                   for i in range(2)]
            psT2 = E(nc.psum_tensor("psT2", [128, 512], F32))

            def ld_r2(sync):
                _wait(sync, ev_ag1)
                ch["LD0"].inc(sync.dma_start(out=rga[0:1, 0:D],
                                             in_=mod_out[0:1, 2 * D:3 * D]), 16)
                ch["LD0"].inc(sync.dma_start(out=rcl[0:1, 0:D],
                                             in_=mod_out[0:1, 4 * D:5 * D]), 16)
                ch["LD0"].inc(sync.dma_start(out=rsb[0:1, 0:D],
                                             in_=mod_out[0:1, 3 * D:4 * D]), 16)
            block.sync(ld_r2)
            ev_r2 = ch["LD0"].ev()

            def v_w2(vector):
                _wait(vector, ev_r2)
                V.inc(vector.engine_nop())
            block.vector(v_w2)
            ev_ga2 = emit_clamp_row(nc, block, ch, rga, 0, rw1, rw2)
            ev_gaB = emit_bcast_row(nc, block, ch, onesR, rw1, 0, gaB, D, psB,
                                    ev_ga2)

            def v_cb2(vector):
                _wait(vector, ev_gaB)
                V.inc(vector.tensor_scalar_add(rw1[0:1, 0:D], rcl[0:1, 0:D], 1.0))
            block.vector(v_cb2)
            ev_cbB = emit_bcast_row(nc, block, ch, onesR, rw1, 0, cbB, D, psB,
                                    V.ev())
            ev_sbB = emit_bcast_row(nc, block, ch, onesR, rsb, 0, sbB, D, psB,
                                    ev_r2)

            evs2_cons = []
            evs2_st = []
            for tt in range(NT):
                par = tt % 2

                def ldx2(sync, tt=tt, par=par):
                    if tt >= 2:
                        _wait(sync, evs2_cons[tt - 2])
                    else:
                        _wait(sync, ev_sbB)
                    ch["LX%d" % par].inc(sync.dma_start(
                        out=xb2[par][:], in_=xb[128 * tt:128 * tt + 128, :]), 16)
                    for k in range(4):
                        ch["LX%d" % par].inc(sync.dma_start(
                            out=ob2[par][:, 512 * k:512 * k + 512],
                            in_=O_all[k, 128 * tt:128 * tt + 128, :]), 16)
                block.sync(ldx2)
                ev_x = ch["LX%d" % par].ev()

                def v_x12(vector, par=par, ev_x=ev_x, ev_so=ch["SO"].ev()):
                    _wait(vector, ev_x)
                    _wait(vector, ev_cbB)
                    _wait(vector, ev_so)
                    vector.tensor_tensor(x1b[:], ob2[par][:], gaB[:], OP.mult)
                    V.inc(vector.tensor_add(x1b[:], x1b[:], xb2[par][:]))
                block.vector(v_x12)
                ev_x1 = V.ev()

                def g_x12(gpsimd, tt=tt, ev_x1=ev_x1):
                    _wait(gpsimd, ev_x1)
                    ch["SO"].inc(gpsimd.dma_start(
                        out=x1_own[128 * tt:128 * tt + 128, :], in_=x1b[:]), 16)
                block.gpsimd(g_x12)

                ev_h = emit_ln_h(nc, block, ch, x1b, stats, xc2, hb2, hb2,
                                 cbB, sbB, SC, ev_sq=ch["T"].ev())
                evs2_cons.append(ev_h)
                if tt >= 2:
                    block.vector(lambda vector, tt=tt: _wait(vector,
                                                             evs2_st[tt - 2]))
                ev_sp = emit_transpose_split(nc, block, ch, hb2, D, ident, psT2,
                                             tmpf, h2s[par][0], h2s[par][1], 0,
                                             ev_pre=ev_h)

                def sh2(gpsimd, tt=tt, par=par, ev_sp=ev_sp):
                    _wait(gpsimd, ev_sp)
                    for dc in range(NDC):
                        ch["SH%d" % par].inc(gpsimd.dma_start(
                            out=h2hi_own[dc, :, 128 * tt:128 * tt + 128],
                            in_=h2s[par][0][:, 128 * dc:128 * dc + 128]), 16)
                        ch["SH%d" % par].inc(gpsimd.dma_start(
                            out=h2lo_own[dc, :, 128 * tt:128 * tt + 128],
                            in_=h2s[par][1][:, 128 * dc:128 * dc + 128]), 16)
                block.gpsimd(sh2)
                evs2_st.append(ch["SH%d" % par].ev())
        ev_ph2 = [ch["SH0"].ev(), ch["SH1"].ev(), ch["SO"].ev()]

        with ExitStack() as st3:
            E3o = st3.enter_context
            lutB = E3o(nc.sbuf_tensor("lutB", [128, 16], F32))
            mskj = E3o(nc.sbuf_tensor("mskj", [128, NJT], F32))
            gbB = E3o(nc.sbuf_tensor("gbB", [128, D], F32))

            def ld3(sync):
                for e in ev_ph2:
                    _wait(sync, e)
                ch["LD0"].inc(sync.dma_start(out=lutB[0:1, :], in_=lutr[:]), 16)
                ch["LD0"].inc(sync.dma_start(out=mskj[:], in_=maskj[:]), 16)
                ch["LD0"].inc(sync.dma_start(out=rw1_3[0:1, 0:D],
                                             in_=mod_out[0:1, 5 * D:6 * D]), 16)
            ev_lutB = None

            with ExitStack() as stg:
                E2 = stg.enter_context
                rw1_3 = E2(nc.sbuf_tensor("rw1_3", [1, D], F32))
                rw2_3 = E2(nc.sbuf_tensor("rw2_3", [1, D], F32))
                rw3_3 = E2(nc.sbuf_tensor("rw3_3", [1, D], F32))
                idxb = [E2(nc.sbuf_tensor("idxb%d" % i, [128, 2048], I32))
                        for i in range(2)]
                acc = E2(nc.sbuf_tensor("acc", [128, 2048], F32))
                gtmp = E2(nc.sbuf_tensor("gtmp", [128, 2048], F32))
                whi = [E2(nc.sbuf_tensor("whi%d" % i, [128, 2048], BF16))
                       for i in range(2)]
                wlo = [E2(nc.sbuf_tensor("wlo%d" % i, [128, 2048], BF16))
                       for i in range(2)]
                psT3 = E2(nc.psum_tensor("psT3g", [128, 512], F32))

                block.sync(ld3)
                ev_ld3 = ch["LD0"].ev()

                def v03(vector):
                    _wait(vector, ev_ld3)
                    V.inc(vector.engine_nop())
                block.vector(v03)
                ev_lutB = emit_bcast_row(nc, block, ch, onesR, lutB, 0, lutB, 16,
                                         psB, V.ev())
                ev_gb = emit_clamp_row(nc, block, ch, rw1_3, 0, rw2_3, rw3_3)
                ev_gbB = emit_bcast_row(nc, block, ch, onesR, rw2_3, 0, gbB, D,
                                        psB, ev_gb)

                jobs = []
                for jt in range(NJT):
                    jobs.append((gidx, 128 * jt, D, wgh_d, wgl_d, jt, jt))
                for jt in range(NJT):
                    jobs.append((uidx, 128 * jt, D, wuh_d, wul_d, jt, jt))
                for ot in range(NDC):
                    jobs.append((didx, 128 * ot, JSL, wdh_d, wdl_d, ot, None))

                evs_jdone = []
                for ji, (srcd, r0, wdt, dsth, dstl, blk, mcol) in enumerate(jobs):
                    par = ji % 2

                    def ldi(sync, ji=ji, par=par, srcd=srcd, r0=r0, wdt=wdt):
                        if ji >= 2:
                            _wait(sync, evs_jdone[ji - 2])
                        else:
                            _wait(sync, ev_gbB)
                        ch["LI%d" % par].inc(sync.dma_start(
                            out=idxb[par][:, 0:wdt], in_=srcd[r0:r0 + 128, 0:wdt]),
                            16)
                    block.sync(ldi)
                    ev_i = ch["LI%d" % par].ev()

                    def v_g(vector, ji=ji, par=par, wdt=wdt, mcol=mcol, ev_i=ev_i,
                            ev_tprev=ch["T"].ev()):
                        _wait(vector, ev_i)
                        _wait(vector, ev_tprev)
                        src = idxb[par][:, 0:wdt]
                        vector.tensor_scalar(acc[:, 0:wdt], src, 0.0, lutB[:, 0:1],
                                             OP.is_equal, OP.mult)
                        for v in range(1, 16):
                            vector.tensor_scalar(gtmp[:, 0:wdt], src, float(v),
                                                 lutB[:, v:v + 1], OP.is_equal,
                                                 OP.mult)
                            vector.tensor_tensor(acc[:, 0:wdt], acc[:, 0:wdt],
                                                 gtmp[:, 0:wdt], OP.add)
                        if mcol is not None:
                            vector.tensor_scalar(acc[:, 0:wdt], acc[:, 0:wdt],
                                                 mskj[:, mcol:mcol + 1], None,
                                                 OP.mult)
                        V.inc(vector.engine_nop())
                    block.vector(v_g)
                    ev_acc = V.ev()

                    nt_ = wdt // 128
                    for g0 in range(0, nt_, 4):
                        gn = min(4, nt_ - g0)

                        def t_g(tensor, g0=g0, gn=gn, ev_acc=ev_acc, ji=ji):
                            _wait(tensor, ev_acc)
                            if g0 > 0:
                                _wait(tensor, evs_gt[-1])
                            elif ji >= 2:
                                _wait(tensor, evs_jdone[ji - 2])
                            last = None
                            for k in range(gn):
                                last = tensor.transpose(
                                    psT3[:, 128 * k:128 * k + 128],
                                    acc[:, (g0 + k) * 128:(g0 + k) * 128 + 128],
                                    ident[:])
                            T.inc(last)
                        if g0 == 0:
                            evs_gt = []
                        block.tensor(t_g)
                        ev_t = T.ev()

                        def v_t(vector, g0=g0, gn=gn, par=par, ev_t=ev_t):
                            _wait(vector, ev_t)
                            w = 128 * gn
                            o = 128 * g0
                            vector.tensor_copy(whi[par][:, o:o + w], psT3[:, 0:w])
                            vector.tensor_tensor(tmpf[:, 0:w], psT3[:, 0:w],
                                                 whi[par][:, o:o + w], OP.subtract)
                            V.inc(vector.tensor_copy(wlo[par][:, o:o + w],
                                                     tmpf[:, 0:w]))
                        block.vector(v_t)
                        evs_gt.append(V.ev())
                    ev_sp3 = V.ev()

                    def g_w(gpsimd, ji=ji, par=par, dsth=dsth, dstl=dstl, blk=blk,
                            wdt=wdt, ev_sp3=ev_sp3):
                        _wait(gpsimd, ev_sp3)
                        ch["SW%d" % par].inc(gpsimd.dma_start(
                            out=dsth[blk, :, 0:wdt], in_=whi[par][:, 0:wdt]), 16)
                        ch["SW%d" % par].inc(gpsimd.dma_start(
                            out=dstl[blk, :, 0:wdt], in_=wlo[par][:, 0:wdt]), 16)
                    block.gpsimd(g_w)
                    evs_jdone.append(ch["SW%d" % par].ev())

            ev_gall = [ch["SW0"].ev(), ch["SW1"].ev()]

            TBW = 512
            NTB = S // TBW
            NTTL = TBW // 128
            with ExitStack() as stf:
                E3 = stf.enter_context
                h2h = E3(nc.sbuf_tensor("h2h", [128, NDC * TBW], BF16))
                h2l = E3(nc.sbuf_tensor("h2l", [128, NDC * TBW], BF16))
                wb_h = [E3(nc.sbuf_tensor("wbh%d" % i, [128, D], BF16))
                        for i in range(2)]
                wb_l = [E3(nc.sbuf_tensor("wbl%d" % i, [128, D], BF16))
                        for i in range(2)]
                wub_h = [E3(nc.sbuf_tensor("wubh%d" % i, [128, D], BF16))
                         for i in range(2)]
                wub_l = [E3(nc.sbuf_tensor("wubl%d" % i, [128, D], BF16))
                         for i in range(2)]
                gsil = E3(nc.sbuf_tensor("gsil", [128, 512], F32))
                hfh = E3(nc.sbuf_tensor("hfh", [128, NJT * TBW], BF16))
                hfl = E3(nc.sbuf_tensor("hfl", [128, NJT * TBW], BF16))
                wdhs = E3(nc.sbuf_tensor("wdhs", [128, NDC * JSL], BF16))
                wdls = E3(nc.sbuf_tensor("wdls", [128, NDC * JSL], BF16))
                partS = E3(nc.sbuf_tensor("partS", [128, D], F32))
                x1f = E3(nc.sbuf_tensor("x1f", [128, D], F32))
                psG = E3(nc.psum_tensor("psG", [128, 512], F32))
                psU = E3(nc.psum_tensor("psU", [128, 512], F32))
                psD = [E3(nc.psum_tensor("psD%d" % i, [128, 512], F32))
                       for i in range(4)]

                def ldwd(sync):
                    _wait(sync, ev_gall[0])
                    _wait(sync, ev_gall[1])
                    for ot in range(NDC):
                        ch["LH0"].inc(sync.dma_start(
                            out=wdhs[:, JSL * ot:JSL * ot + JSL],
                            in_=wdh_d[ot, :, 0:JSL]), 16)
                        ch["LH0"].inc(sync.dma_start(
                            out=wdls[:, JSL * ot:JSL * ot + JSL],
                            in_=wdl_d[ot, :, 0:JSL]), 16)
                block.sync(ldwd)
                ev_wd = ch["LH0"].ev()

                for tb in range(NTB):
                    def ldh2(sync, tb=tb):
                        if tb == 0:
                            _wait(sync, ev_gall[0])
                            _wait(sync, ev_gall[1])
                        else:
                            _wait(sync, evs_tb[-1])
                        for dc in range(NDC):
                            ch["LH0"].inc(sync.dma_start(
                                out=h2h[:, TBW * dc:TBW * dc + TBW],
                                in_=h2hi_own[dc, :, TBW * tb:TBW * tb + TBW]), 16)
                            ch["LH0"].inc(sync.dma_start(
                                out=h2l[:, TBW * dc:TBW * dc + TBW],
                                in_=h2lo_own[dc, :, TBW * tb:TBW * tb + TBW]), 16)
                    if tb == 0:
                        evs_tb = []
                    block.sync(ldh2)
                    ev_h2 = ch["LH0"].ev()

                    evs_hf = []
                    for jt in range(NJT):
                        par = jt % 2

                        def ldwg(sync, jt=jt, par=par, tb=tb):
                            if jt >= 2:
                                _wait(sync, evs_hf[jt - 2])
                            elif tb > 0:
                                _wait(sync, evs_hf_prev[-1])
                            ch["LW%d" % par].inc(sync.dma_start(
                                out=wb_h[par][:], in_=wgh_d[jt, :, :]), 16)
                            ch["LW%d" % par].inc(sync.dma_start(
                                out=wb_l[par][:], in_=wgl_d[jt, :, :]), 16)
                            ch["LW%d" % par].inc(sync.dma_start(
                                out=wub_h[par][:], in_=wuh_d[jt, :, :]), 16)
                            ch["LW%d" % par].inc(sync.dma_start(
                                out=wub_l[par][:], in_=wul_d[jt, :, :]), 16)
                        block.sync(ldwg)
                        ev_w = ch["LW%d" % par].ev()

                        def t_gu(tensor, jt=jt, par=par, ev_w=ev_w, ev_h2=ev_h2,
                                 tb=tb):
                            _wait(tensor, ev_w)
                            _wait(tensor, ev_h2)
                            if jt > 0:
                                _wait(tensor, evs_guev[-1])
                            elif tb > 0:
                                _wait(tensor, evs_guev_prev[-1])
                            last = None
                            for dc in range(NDC):
                                rh = h2h[:, TBW * dc:TBW * dc + TBW]
                                rl = h2l[:, TBW * dc:TBW * dc + TBW]
                                for (ps, wh_, wl_) in ((psG, wb_h[par], wb_l[par]),
                                                       (psU, wub_h[par],
                                                        wub_l[par])):
                                    a = wh_[:, 128 * dc:128 * dc + 128]
                                    bb = wl_[:, 128 * dc:128 * dc + 128]
                                    tensor.matmul(ps[:], a, rh, start=(dc == 0),
                                                  stop=False)
                                    tensor.matmul(ps[:], a, rl, start=False,
                                                  stop=False)
                                    last = tensor.matmul(ps[:], bb, rh, start=False,
                                                         stop=(dc == NDC - 1))
                            T.inc(last)
                        if jt == 0:
                            evs_guev_prev = evs_guev if tb > 0 else None
                            evs_guev = []
                        block.tensor(t_gu)
                        ev_mm = T.ev()

                        def s_si(scalar, ev_mm=ev_mm):
                            _wait(scalar, ev_mm)
                            SC.inc(scalar.activation(gsil[:], psG[:], ACT.Silu))
                        block.scalar(s_si)
                        ev_si = SC.ev()

                        def v_hf(vector, jt=jt, ev_si=ev_si, ev_tdn=ch["T"].ev()):
                            _wait(vector, ev_si)
                            if jt == 0:
                                _wait(vector, ev_tdn)
                            o = TBW * jt
                            vector.tensor_tensor(tmpf[:, 0:512], psU[:], gsil[:],
                                                 OP.mult)
                            vector.tensor_copy(hfh[:, o:o + 512], tmpf[:, 0:512])
                            vector.tensor_tensor(dump[:, 0:512], tmpf[:, 0:512],
                                                 hfh[:, o:o + 512], OP.subtract)
                            V.inc(vector.tensor_copy(hfl[:, o:o + 512],
                                                     dump[:, 0:512]))
                        block.vector(v_hf)
                        evs_guev.append(V.ev())
                        evs_hf.append(V.ev())
                    evs_hf_prev = evs_hf
                    ev_hf_all = V.ev()

                    for ttl in range(NTTL):
                        def ldx3(sync, tb=tb, ttl=ttl, ev0=ev_gbB):
                            if tb == 0 and ttl == 0:
                                _wait(sync, ev0)
                            else:
                                _wait(sync, evs_dn[-1])
                            r = TBW * tb + 128 * ttl
                            ch["LX0"].inc(sync.dma_start(
                                out=x1f[:], in_=x1_own[r:r + 128, :]), 16)
                        if ttl == 0 and tb == 0:
                            evs_dn = []
                        block.sync(ldx3)
                        ev_xf = ch["LX0"].ev()

                        def t_dn(tensor, ttl=ttl, tb=tb, ev_hf_all=ev_hf_all):
                            _wait(tensor, ev_hf_all)
                            _wait(tensor, ev_wd)
                            if evs_dn:
                                _wait(tensor, evs_dn[-1])
                            last = None
                            for ot in range(NDC):
                                ps = psD[ot // 4]
                                oc = 128 * (ot % 4)
                                for jt in range(NJT):
                                    sh = hfh[:, TBW * jt + 128 * ttl:
                                              TBW * jt + 128 * ttl + 128]
                                    sl = hfl[:, TBW * jt + 128 * ttl:
                                              TBW * jt + 128 * ttl + 128]
                                    rh = wdhs[:, JSL * ot + 128 * jt:
                                              JSL * ot + 128 * jt + 128]
                                    rl = wdls[:, JSL * ot + 128 * jt:
                                              JSL * ot + 128 * jt + 128]
                                    first = (jt == 0)
                                    lastf = (jt == NJT - 1)
                                    tensor.matmul(ps[:, oc:oc + 128], sh, rh,
                                                  start=first, stop=False)
                                    tensor.matmul(ps[:, oc:oc + 128], sh, rl,
                                                  start=False, stop=False)
                                    last = tensor.matmul(ps[:, oc:oc + 128], sl, rh,
                                                         start=False, stop=lastf)
                            T.inc(last)
                        block.tensor(t_dn)
                        ev_dm = T.ev()

                        def v_dn(vector, ttl=ttl, ev_dm=ev_dm, ev_xf=ev_xf,
                                 ev_so=ch["SO"].ev()):
                            _wait(vector, ev_dm)
                            _wait(vector, ev_xf)
                            _wait(vector, ev_so)
                            for i in range(4):
                                vector.tensor_tensor(
                                    partS[:, 512 * i:512 * i + 512], psD[i][:],
                                    gbB[:, 512 * i:512 * i + 512], OP.mult)
                                vector.scalar_tensor_tensor(
                                    partS[:, 512 * i:512 * i + 512],
                                    x1f[:, 512 * i:512 * i + 512], 0.25,
                                    partS[:, 512 * i:512 * i + 512],
                                    OP.mult, OP.add)
                            V.inc(vector.engine_nop())
                        block.vector(v_dn)
                        evs_dn.append(V.ev())

                        def g_dn(gpsimd, ttl=ttl, tb=tb, ev_v=evs_dn[-1]):
                            _wait(gpsimd, ev_v)
                            r = TBW * tb + 128 * ttl
                            ch["SO"].inc(gpsimd.dma_start(
                                out=c_own[r:r + 128, :], in_=partS[:]), 16)
                        block.gpsimd(g_dn)
                    evs_tb.append(V.ev())

        def g_rs(gpsimd):
            _wait(gpsimd, ch["SO"].ev())
            CC.inc(gpsimd.collective_compute(
                "ReduceScatter", OP.add, replica_groups=CC_GROUPS,
                ins=[c_own.ap().opt()], outs=[y_q.ap().opt()]), 1)
            _wait(gpsimd, CC.ev())
            ch["SO"].inc(gpsimd.dma_start(out=y_out[:, :], in_=y_q[:, :]), 16)
            _wait(gpsimd, ch["SO"].ev())
        block.gpsimd(g_rs)
    return nc


def kernel(L0_attn_hidden_states, L0_attn_w_qkv, L0_attn_b_qkv, L0_ffn_lut,
           L0_ffn_gate_idx, L0_ffn_up_idx, L0_ffn_down_idx, L0_c, L0_wmod):
    x = np.ascontiguousarray(L0_attn_hidden_states)
    wqkv = np.ascontiguousarray(L0_attn_w_qkv)
    bqkv = np.ascontiguousarray(L0_attn_b_qkv)
    lut = np.ascontiguousarray(L0_ffn_lut)
    gi = np.ascontiguousarray(L0_ffn_gate_idx)
    ui = np.ascontiguousarray(L0_ffn_up_idx)
    di = np.ascontiguousarray(L0_ffn_down_idx)
    c = np.ascontiguousarray(L0_c)
    wmod = np.ascontiguousarray(L0_wmod)
    wmodR = _wmod_relayout(wmod)
    tb = _host_tables()
    cores = list(range(8))
    jsl = _jslices()

    nc = build_fused()
    maps = []
    for cc in cores:
        b, g = cc // 4, cc % 4
        o0, osz = jsl[g]
        gip = np.zeros((JSL, D), dtype=np.int32)
        uip = np.zeros((JSL, D), dtype=np.int32)
        dip = np.zeros((D, JSL), dtype=np.int32)
        gip[:osz] = gi[o0:o0 + osz]
        uip[:osz] = ui[o0:o0 + osz]
        dip[:, :osz] = di[:, o0:o0 + osz]
        mj = np.zeros((128, NJT), dtype=np.float32)
        mjf = np.zeros(JSL, dtype=np.float32)
        mjf[:osz] = 1.0
        mj[:, :] = mjf.reshape(NJT, 128).T
        maps.append(dict(
            xb=x[b], wq=wqkv[EH * g:EH * (g + 1)],
            bq=bqkv.reshape(1, EH * 4)[:, EH * g:EH * (g + 1)],
            cCol=np.ascontiguousarray(c[b].reshape(16, 128).T),
            wmod=wmodR, embT=tb["embT"], ksT=tb["ksT"], kcT=tb["kcT"],
            ident=tb["ident"], maskA=tb["maskA"],
            gidx=gip, uidx=uip, didx=dip,
            lutr=lut.reshape(1, 16), maskj=mj,
        ))
    r = run_bass_kernel_spmd(nc, maps, core_ids=cores)
    y = np.empty((B, S, D), dtype=np.float32)
    for cc in cores:
        b, g = cc // 4, cc % 4
        y[b, TQ * g:TQ * (g + 1)] = np.asarray(r.results[cc]["y_out"])
    return y


# version 08 (final)
# speedup vs baseline: 1.1124x; 1.0017x over previous
import numpy as np
from contextlib import ExitStack
import concourse.bass as bass
import concourse.mybir as mybir
from concourse.bass_utils import run_bass_kernel_spmd

F32 = mybir.dt.float32
BF16 = mybir.dt.bfloat16
I32 = mybir.dt.int32
OP = mybir.AluOpType
ACT = mybir.ActivationFunctionType
AX = mybir.AxisListType

B, S, D, H, HD = 2, 4096, 2048, 16, 128
INTER = 5504
NT = S // 128
NDC = D // 128
NQ = 4
TQ = S // NQ
NTQ = TQ // 128
JSL = 1408
NJT = JSL // 128
EH = 4 * 3 * HD
NORM32 = float(np.float32(np.sqrt(np.float32(128.0))))
C1 = float(np.float32(2 * np.pi))
C2 = float(np.float32(2 * np.pi - np.float64(np.float32(2 * np.pi))))
EPS = 1e-5
INV_D = 1.0 / float(D)

_CACHE = {}


def _host_tables():
    if "t" in _CACHE:
        return _CACHE["t"]
    import jax
    import jax.numpy as jnp
    with jax.default_device(jax.devices("cpu")[0]):
        inv_freq = 1.0 / 10000.0 ** (jnp.arange(0, HD, 2, dtype=jnp.float32) / HD)
        t = jnp.arange(S, dtype=jnp.float32)
        freqs = jnp.einsum("i,j->ij", t, inv_freq)
        emb32 = np.asarray(freqs)
    emb64 = emb32.astype(np.float64)
    ks = np.round(emb64 / (2 * np.pi))
    kc = np.round(emb64 / (2 * np.pi) + 0.25) - 0.25
    assert np.all(np.abs(ks) < 2 ** 11) and np.all(np.abs(kc * 4) % 1 == 0)

    def dev_layout(a):
        return np.ascontiguousarray(
            a.reshape(NT, 128, 64).transpose(1, 0, 2).reshape(128, NT * 64)
        ).astype(np.float32)

    ident = np.eye(128, dtype=np.float32)
    maskA = np.zeros((128, 4 * 512), dtype=np.float32)
    for r in range(4):
        q_i = np.arange(128)[:, None]
        k_i = np.arange(512)[None, :]
        maskA[:, 512 * r:512 * (r + 1)] = np.where(k_i <= 128 * r + q_i,
                                                   0.0, -1.0e30)
    _CACHE["t"] = dict(
        embT=dev_layout(emb32), ksT=dev_layout(ks), kcT=dev_layout(kc),
        ident=ident, maskA=maskA,
    )
    return _CACHE["t"]


def _wmod_relayout(wmod):
    return np.ascontiguousarray(
        wmod.reshape(16, 128, 24, 512).transpose(1, 2, 0, 3).reshape(128, 24 * 8192))


def _jslices():
    out = []
    for g in range(NQ):
        o0 = 1408 * g
        osz = 1408 if g < 3 else INTER - 1408 * 3
        out.append((o0, osz))
    return out


class Chan:

    def __init__(self, nc, stack, name):
        self.h = stack.enter_context(nc.semaphore(name))
        self.n = 0

    def inc(self, instr, k=1):
        instr.then_inc(self.h, k)
        self.n += k
        return (self.h, self.n)

    def ev(self):
        return (self.h, self.n)


def _wait(eng, ev):
    if ev is not None:
        eng.wait_ge(ev[0], ev[1])



def emit_mod_phase(nc, block, st, ch, cCol, wmod_dram, mod_out, rowM, psM,
                   ev_pre=None):
    T, V, SC = ch["T"], ch["V"], ch["SC"]
    scS = st.enter_context(nc.sbuf_tensor("scS", [128, 16], F32))
    wmW = st.enter_context(nc.sbuf_tensor("wmW", [128, 2 * 16 * 512], F32))

    def sc_fn(scalar):
        _wait(scalar, ch["LD0"].ev())
        SC.inc(scalar.activation(scS[:], cCol[:], ACT.Silu))
    block.scalar(sc_fn)
    ev_silu = SC.ev()

    njc = 12288 // 512
    evs_v = []
    evs_st = []
    for jc in range(njc):
        par = jc % 2

        def ld_fn(sync, jc=jc, par=par):
            if jc >= 2:
                _wait(sync, evs_v[jc - 2])
            else:
                _wait(sync, ev_pre)
            ch["LW%d" % par].inc(
                sync.dma_start(out=wmW[:, par * 8192:par * 8192 + 8192],
                               in_=wmod_dram[:, 8192 * jc:8192 * jc + 8192]), 16)
        block.sync(ld_fn)
        ev_ld = ch["LW%d" % par].ev()

        def t_fn(tensor, jc=jc, par=par, ev_ld=ev_ld):
            _wait(tensor, ev_ld)
            _wait(tensor, ev_silu)
            if jc >= 1:
                _wait(tensor, evs_v[jc - 1])
            last = None
            for dc in range(NDC):
                last = tensor.matmul(psM[0:1, 0:512], scS[:, dc:dc + 1],
                                     wmW[:, (par * 16 + dc) * 512:(par * 16 + dc) * 512 + 512],
                                     start=(dc == 0), stop=(dc == NDC - 1))
            T.inc(last)
        block.tensor(t_fn)
        ev_mm = T.ev()

        def v_fn(vector, jc=jc, par=par, ev_mm=ev_mm):
            _wait(vector, ev_mm)
            if jc >= 2:
                _wait(vector, evs_st[jc - 2])
            V.inc(vector.tensor_copy(rowM[0:1, 512 * par:512 * par + 512],
                                     psM[0:1, 0:512]))
        block.vector(v_fn)
        evs_v.append(V.ev())

        def g_fn(gpsimd, jc=jc, par=par, ev_v=evs_v[-1]):
            _wait(gpsimd, ev_v)
            ch["SO"].inc(gpsimd.dma_start(
                out=mod_out[0:1, 512 * jc:512 * jc + 512],
                in_=rowM[0:1, 512 * par:512 * par + 512]), 16)
        block.gpsimd(g_fn)
        evs_st.append(ch["SO"].ev())
    return evs_st


def emit_clamp_row(nc, block, ch, modrow, off, outR, tmpR):
    V = ch["V"]

    def v_fn(vector):
        g = modrow[0:1, off:off + D]
        vector.tensor_scalar(tmpR[0:1, 0:D], g, 0.0, 2.0, OP.is_ge, OP.mult)
        vector.tensor_scalar_add(tmpR[0:1, 0:D], tmpR[0:1, 0:D], -1.0)
        vector.tensor_scalar_mul(outR[0:1, 0:D], g, -1.0)
        vector.tensor_max(outR[0:1, 0:D], g, outR[0:1, 0:D])
        vector.tensor_scalar_max(outR[0:1, 0:D], outR[0:1, 0:D], 0.25)
        V.inc(vector.tensor_mul(outR[0:1, 0:D], outR[0:1, 0:D], tmpR[0:1, 0:D]))
    block.vector(v_fn)
    return V.ev()


def emit_bcast_row(nc, block, ch, onesR, srcR, s_off, dstB, width, psB, ev_pre=None):
    T, V = ch["T"], ch["V"]
    nch = (width + 511) // 512
    for k in range(nch):
        w = min(512, width - 512 * k)
        ev_vpre = ch["V"].ev()

        def t_fn(tensor, k=k, w=w, ev_vpre=ev_vpre):
            _wait(tensor, ev_pre)
            _wait(tensor, ev_vpre)
            T.inc(tensor.matmul(psB[:, 0:w], onesR[0:1, 0:128],
                                srcR[0:1, s_off + 512 * k:s_off + 512 * k + w],
                                start=True, stop=True))
        block.tensor(t_fn)
        ev_mm = T.ev()

        def v_fn(vector, k=k, w=w, ev_mm=ev_mm):
            _wait(vector, ev_mm)
            V.inc(vector.tensor_copy(dstB[:, 512 * k:512 * k + w], psB[:, 0:w]))
        block.vector(v_fn)
    return V.ev()


def emit_ln_h(nc, block, ch, xbuf, stats, xc, dump, hout, scaleB, shiftB, sc_ch,
              ev_sq=None):
    V, SC = ch["V"], ch["SC"]
    ev_xc = ch["SC"].ev()

    def v1(vector):
        _wait(vector, ev_xc)
        V.inc(vector.reduce_sum(stats[:, 0:1], xbuf[:], axis=AX.X))
        _wait(vector, V.ev())
        V.inc(vector.tensor_scalar_mul(stats[:, 1:2], stats[:, 0:1], INV_D))
        _wait(vector, V.ev())
        V.inc(vector.tensor_scalar(xc[:], xbuf[:], stats[:, 1:2], None, OP.subtract))
    block.vector(v1)
    ev1 = V.ev()

    def s1(scalar):
        _wait(scalar, ev1)
        _wait(scalar, ev_sq)
        SC.inc(scalar.activation(dump[:], xc[:], ACT.Square))
    block.scalar(s1)
    ev2 = SC.ev()

    def v2(vector):
        _wait(vector, ev2)
        V.inc(vector.reduce_sum(stats[:, 2:3], dump[:], axis=AX.X))
        _wait(vector, V.ev())
        V.inc(vector.tensor_scalar_mul(stats[:, 3:4], stats[:, 2:3], INV_D))
    block.vector(v2)
    ev3 = V.ev()

    def s2(scalar):
        _wait(scalar, ev3)
        SC.inc(scalar.activation(stats[:, 4:5], stats[:, 3:4], ACT.Sqrt,
                                 bias=stats[:, 15:16]))
    block.scalar(s2)
    ev4 = SC.ev()

    def v3(vector):
        _wait(vector, ev4)
        V.inc(vector.reciprocal(stats[:, 5:6], stats[:, 4:5]))
        _wait(vector, V.ev())
        vector.scalar_tensor_tensor(hout[:], xc[:], stats[:, 5:6], scaleB[:],
                                    OP.mult, OP.mult)
        V.inc(vector.tensor_add(hout[:], hout[:], shiftB[:]))
    block.vector(v3)
    return V.ev()


def emit_transpose_split(nc, block, ch, src_f32, ncols, ident, psT, tmpf,
                         dst_hi, dst_lo, dst_off, ev_pre=None, scale_ap=None):
    T, V = ch["T"], ch["V"]
    nt = ncols // 128
    for g0 in range(0, nt, 4):
        gn = min(4, nt - g0)

        def t_fn(tensor, g0=g0, gn=gn):
            _wait(tensor, ev_pre)
            if g0 > 0:
                _wait(tensor, ch["V"].ev() if False else evs[-1])
            last = None
            for k in range(gn):
                last = tensor.transpose(psT[:, 128 * k:128 * k + 128],
                                        src_f32[:, (g0 + k) * 128:(g0 + k) * 128 + 128],
                                        ident[:])
            T.inc(last)
        if g0 == 0:
            evs = []
        block.tensor(t_fn)
        ev_t = T.ev()

        def v_fn(vector, g0=g0, gn=gn, ev_t=ev_t):
            _wait(vector, ev_t)
            w = 128 * gn
            o = dst_off + 128 * g0
            if scale_ap is not None:
                vector.tensor_scalar(tmpf[:, 0:w], psT[:, 0:w], scale_ap, None, OP.mult)
                vector.tensor_copy(dst_hi[:, o:o + w], tmpf[:, 0:w])
                vector.tensor_tensor(tmpf[:, 0:w], tmpf[:, 0:w], dst_hi[:, o:o + w],
                                     OP.subtract)
            else:
                vector.tensor_copy(dst_hi[:, o:o + w], psT[:, 0:w])
                vector.tensor_tensor(tmpf[:, 0:w], psT[:, 0:w], dst_hi[:, o:o + w],
                                     OP.subtract)
            V.inc(vector.tensor_copy(dst_lo[:, o:o + w], tmpf[:, 0:w]))
        block.vector(v_fn)
        evs.append(V.ev())
    return V.ev()


def emit_split_hilo(nc, block, ch, src, dst_hi, dst_lo, tmpf, w, ev_pre=None):
    V = ch["V"]

    def v_fn(vector):
        _wait(vector, ev_pre)
        vector.tensor_copy(dst_hi[:, 0:w], src[:, 0:w])
        vector.tensor_tensor(tmpf[:, 0:w], src[:, 0:w], dst_hi[:, 0:w], OP.subtract)
        V.inc(vector.tensor_copy(dst_lo[:, 0:w], tmpf[:, 0:w]))
    block.vector(v_fn)
    return V.ev()



CC_GROUPS = [[0, 1, 2, 3], [4, 5, 6, 7]]


def build_fused(dbg=False):
    nc = bass.Bass("TRN2", target_bir_lowering=False)
    xb = nc.dram_tensor("xb", [S, D], F32, kind="ExternalInput")
    wq = nc.dram_tensor("wq", [EH, D], F32, kind="ExternalInput")
    bq = nc.dram_tensor("bq", [1, EH], F32, kind="ExternalInput")
    cCol_d = nc.dram_tensor("cCol", [128, 16], F32, kind="ExternalInput")
    wmod = nc.dram_tensor("wmod", [128, 24 * 8192], F32, kind="ExternalInput")
    embT_d = nc.dram_tensor("embT", [128, NT * 64], F32, kind="ExternalInput")
    ksT_d = nc.dram_tensor("ksT", [128, NT * 64], F32, kind="ExternalInput")
    kcT_d = nc.dram_tensor("kcT", [128, NT * 64], F32, kind="ExternalInput")
    ident_d = nc.dram_tensor("ident", [128, 128], F32, kind="ExternalInput")
    maskA_d = nc.dram_tensor("maskA", [128, 4 * 512], F32, kind="ExternalInput")
    gidx = nc.dram_tensor("gidx", [JSL, D], I32, kind="ExternalInput")
    uidx = nc.dram_tensor("uidx", [JSL, D], I32, kind="ExternalInput")
    didx = nc.dram_tensor("didx", [D, JSL], I32, kind="ExternalInput")
    lutr = nc.dram_tensor("lutr", [1, 16], F32, kind="ExternalInput")
    maskj = nc.dram_tensor("maskj", [128, NJT], F32, kind="ExternalInput")
    y_out = nc.dram_tensor("y_out", [TQ, D], F32, kind="ExternalOutput")
    O_out = nc.dram_tensor("O_own", [S, 512], F32)
    mod_out = nc.dram_tensor("mod_own", [1, 6 * D], F32)
    h1T_hi_d = nc.dram_tensor("h1T_hi", [NT, 128, D], BF16)
    h1T_lo_d = nc.dram_tensor("h1T_lo", [NT, 128, D], BF16)
    O_all = nc.dram_tensor("O_all", [4, S, 512], F32)
    x1_own = nc.dram_tensor("x1_own", [S, D], F32)
    h2hi_own = nc.dram_tensor("h2hi", [NDC, 128, S], BF16)
    h2lo_own = nc.dram_tensor("h2lo", [NDC, 128, S], BF16)
    wgh_d = nc.dram_tensor("wgh", [NJT, 128, D], BF16)
    wgl_d = nc.dram_tensor("wgl", [NJT, 128, D], BF16)
    wuh_d = nc.dram_tensor("wuh", [NJT, 128, D], BF16)
    wul_d = nc.dram_tensor("wul", [NJT, 128, D], BF16)
    wdh_d = nc.dram_tensor("wdh", [NDC, 128, JSL], BF16)
    wdl_d = nc.dram_tensor("wdl", [NDC, 128, JSL], BF16)
    c_own = nc.dram_tensor("c_own", [S, D], F32)
    y_q = nc.dram_tensor("y_q", [TQ, D], F32)

    with ExitStack() as st:
        E0 = st.enter_context
        ch = {}
        for name in ["T", "V", "SC", "LD0", "LW0", "LW1", "LX0", "LX1",
                     "LH0", "LH1", "SO", "SH0", "SH1", "LI0", "LI1",
                     "SW0", "SW1", "CC"]:
            ch[name] = Chan(nc, st, "c_" + name)
        ch["LD"] = ch["LD0"]
        T, V, SC, CC = ch["T"], ch["V"], ch["SC"], ch["CC"]
        block = E0(nc.Block())

        ident = E0(nc.sbuf_tensor("idents", [128, 128], F32))
        onesR = E0(nc.sbuf_tensor("onesR", [1, 128], F32))
        invB = E0(nc.sbuf_tensor("invB", [128, 1], F32))
        nrm = E0(nc.sbuf_tensor("nrm", [1, 1], F32))
        stats = E0(nc.sbuf_tensor("stats", [128, 16], F32))
        tmpf = E0(nc.sbuf_tensor("tmpf", [128, 512], F32))
        dump = E0(nc.sbuf_tensor("dump", [128, 512], F32))
        psB = E0(nc.psum_tensor("psB", [128, 512], F32))

        with ExitStack() as st1:
            E = st1.enter_context
            cCol = E(nc.sbuf_tensor("cCols", [128, 16], F32))
            maskA = E(nc.sbuf_tensor("maskAs", [128, 4 * 512], F32))
            cosT = E(nc.sbuf_tensor("cosT", [128, NT * 64], F32))
            sinT = E(nc.sbuf_tensor("sinT", [128, NT * 64], F32))
            rowA = E(nc.sbuf_tensor("rowA", [1, D], F32))
            rowS = E(nc.sbuf_tensor("rowS", [1, D], F32))
            rowM = E(nc.sbuf_tensor("rowM", [1, 1024], F32))
            caB = E(nc.sbuf_tensor("caB", [128, D], F32))
            saB = E(nc.sbuf_tensor("saB", [128, D], F32))
            bqB = E(nc.sbuf_tensor("bqB", [128, EH], F32))
            psA = E(nc.psum_tensor("psA", [128, 512], F32))
            psA2 = E(nc.psum_tensor("psA2", [128, 512], F32))
            psO = E(nc.psum_tensor("psO", [128, 512], F32))
            psT_ = E(nc.psum_tensor("psT", [128, 512], F32))
            psT2_ = E(nc.psum_tensor("psT2a", [128, 512], F32))
            psQ = E(nc.psum_tensor("psQ", [128, 512], F32))
            psM = E(nc.psum_tensor("psM", [128, 512], F32))
            with ExitStack() as st_rope:
                ER = st_rope.enter_context
                embT = ER(nc.sbuf_tensor("embTs", [128, NT * 64], F32))
                ksT = ER(nc.sbuf_tensor("ksTs", [128, NT * 64], F32))
                kcT = ER(nc.sbuf_tensor("kcTs", [128, NT * 64], F32))
                rtmp = ER(nc.sbuf_tensor("rtmp", [128, NT * 64], F32))

                def ld0(sync):
                    for dst, src in [(cCol, cCol_d), (ident, ident_d), (maskA, maskA_d),
                                     (embT, embT_d), (ksT, ksT_d), (kcT, kcT_d)]:
                        ch["LD0"].inc(sync.dma_start(out=dst[:], in_=src[:]), 16)
                    ch["LD0"].inc(sync.dma_start(out=bqB[0:1, :], in_=bq[:]), 16)
                block.sync(ld0)
                ev_ld0 = ch["LD0"].ev()

                def v0(vector):
                    vector.memset(onesR[:], 1.0)
                    vector.memset(stats[:, 15:16], EPS)
                    V.inc(vector.memset(nrm[:], NORM32))
                    _wait(vector, V.ev())
                    V.inc(vector.reciprocal(nrm[0:1, 0:1], nrm[0:1, 0:1]))
                block.vector(v0)
                ev_ones = V.ev()

                def v_rope(vector):
                    _wait(vector, ev_ld0)
                    vector.tensor_scalar(rtmp[:], ksT[:], C1, None, OP.mult)
                    vector.tensor_tensor(cosT[:], embT[:], rtmp[:], OP.subtract)
                    vector.tensor_scalar(rtmp[:], ksT[:], C2, None, OP.mult)
                    vector.tensor_tensor(cosT[:], cosT[:], rtmp[:], OP.subtract)
                    vector.tensor_scalar(rtmp[:], kcT[:], C1, None, OP.mult)
                    vector.tensor_tensor(sinT[:], embT[:], rtmp[:], OP.subtract)
                    vector.tensor_scalar(rtmp[:], kcT[:], C2, None, OP.mult)
                    V.inc(vector.tensor_tensor(sinT[:], sinT[:], rtmp[:], OP.subtract))
                block.vector(v_rope)
                ev_r = V.ev()

                def s_rope(scalar):
                    _wait(scalar, ev_r)
                    scalar.activation(cosT[:], cosT[:], ACT.Sin)
                    SC.inc(scalar.activation(sinT[:], sinT[:], ACT.Sin))
                block.scalar(s_rope)
                ev_tables = SC.ev()
            sinTab, cosTab = cosT, sinT

            def t_inv(tensor):
                _wait(tensor, ev_ones)
                T.inc(tensor.matmul(psB[:, 0:1], onesR[0:1, 0:128], nrm[0:1, 0:1],
                                    start=True, stop=True))
            block.tensor(t_inv)
            ev_ti = T.ev()

            def v_inv(vector):
                _wait(vector, ev_ti)
                V.inc(vector.tensor_copy(invB[:], psB[:, 0:1]))
            block.vector(v_inv)

            with ExitStack() as st_ln:
                E2 = st_ln.enter_context
                xbuf = [E2(nc.sbuf_tensor("xb%d" % i, [128, D], F32)) for i in range(2)]
                xc = E2(nc.sbuf_tensor("xc", [128, D], F32))
                hbuf = E2(nc.sbuf_tensor("hbuf", [128, D], F32))
                hTs = [(E2(nc.sbuf_tensor("hTh%d" % i, [128, D], BF16)),
                        E2(nc.sbuf_tensor("hTl%d" % i, [128, D], BF16))) for i in range(2)]

                with ExitStack() as st_mod:
                    evs_mod = emit_mod_phase(nc, block, st_mod, ch, cCol, wmod,
                                             mod_out, rowM, psM, ev_pre=ev_tables)
                    ev_mod = evs_mod[-1]

                def ld_rows(sync):
                    _wait(sync, evs_mod[7])
                    ch["LD0"].inc(sync.dma_start(out=rowA[0:1, 0:D],
                                                 in_=mod_out[0:1, D:2 * D]), 16)
                    ch["LD0"].inc(sync.dma_start(out=rowS[0:1, 0:D],
                                                 in_=mod_out[0:1, 0:D]), 16)
                block.sync(ld_rows)
                ev_rows = ch["LD0"].ev()

                def v_rows(vector):
                    _wait(vector, ev_rows)
                    V.inc(vector.tensor_scalar_add(rowA[0:1, 0:D], rowA[0:1, 0:D],
                                                   1.0))
                block.vector(v_rows)
                ev_ca1 = V.ev()
                ev_caB = emit_bcast_row(nc, block, ch, onesR, rowA, 0, caB, D,
                                        psB, ev_ca1)
                ev_saB = emit_bcast_row(nc, block, ch, onesR, rowS, 0, saB, D,
                                        psB, ev_rows)
                ev_bqB = emit_bcast_row(nc, block, ch, onesR, bqB, 0, bqB, EH,
                                        psB, ev_ld0)
                evs_cons = []
                evs_hT = []
                for tt in range(NT):
                    par = tt % 2

                    def ldx(sync, tt=tt, par=par):
                        if tt >= 2:
                            _wait(sync, evs_cons[tt - 2])
                        else:
                            _wait(sync, ev_caB)
                        ch["LX%d" % par].inc(
                            sync.dma_start(out=xbuf[par][:],
                                           in_=xb[128 * tt:128 * tt + 128, :]), 16)
                    block.sync(ldx)
                    ev_x = ch["LX%d" % par].ev()

                    def vw(vector, par=par, ev_x=ev_x, tt=tt):
                        _wait(vector, ev_x)
                        if tt == 0:
                            _wait(vector, ev_caB)
                            _wait(vector, ev_saB)
                    block.vector(vw)
                    ev_h = emit_ln_h(nc, block, ch, xbuf[par], stats, xc, hbuf, hbuf,
                                     caB, saB, SC, ev_sq=ch["T"].ev())
                    evs_cons.append(ev_h)
                    def w_guard(vector, tt=tt, par=par):
                        if tt >= 2:
                            _wait(vector, evs_st[tt - 2])
                    if tt >= 2:
                        block.vector(lambda vector, tt=tt, par=par: _wait(vector, evs_st[tt - 2]))
                    else:
                        evs_st = []
                    ev_sp = emit_transpose_split(nc, block, ch, hbuf, D, ident, psT_,
                                                 tmpf, hTs[par][0], hTs[par][1], 0,
                                                 ev_pre=ev_h)
                    def sh(gpsimd, tt=tt, par=par, ev_sp=ev_sp):
                        _wait(gpsimd, ev_sp)
                        ch["SH%d" % par].inc(
                            gpsimd.dma_start(out=h1T_hi_d[tt, :, :], in_=hTs[par][0][:]), 16)
                        ch["SH%d" % par].inc(
                            gpsimd.dma_start(out=h1T_lo_d[tt, :, :], in_=hTs[par][1][:]), 16)
                    block.gpsimd(sh)
                    evs_st.append(ch["SH%d" % par].ev())

            ev_allst = [ch["SH0"].ev(), ch["SH1"].ev()]

            with ExitStack() as st_at:
                E3 = st_at.enter_context
                wqNat = E3(nc.sbuf_tensor("wqNat", [128, D], F32))
                wqT_hi = E3(nc.sbuf_tensor("wqT_hi", [128, NDC * 384], BF16))
                wqT_lo = E3(nc.sbuf_tensor("wqT_lo", [128, NDC * 384], BF16))
                hh = [E3(nc.sbuf_tensor("hh%d" % i, [128, D], BF16)) for i in range(2)]
                hl = [E3(nc.sbuf_tensor("hl%d" % i, [128, D], BF16)) for i in range(2)]
                qkvS = E3(nc.sbuf_tensor("qkvS", [128, 384], F32))
                qrot = E3(nc.sbuf_tensor("qrot", [128, 256], F32))
                qT_hi = E3(nc.sbuf_tensor("qT_hi", [128, S], BF16))
                qT_lo = E3(nc.sbuf_tensor("qT_lo", [128, S], BF16))
                kT_hi = E3(nc.sbuf_tensor("kT_hi", [128, S], BF16))
                kT_lo = E3(nc.sbuf_tensor("kT_lo", [128, S], BF16))
                v_hi = E3(nc.sbuf_tensor("v_hi", [128, S], BF16))
                v_lo = E3(nc.sbuf_tensor("v_lo", [128, S], BF16))
                sc_buf = E3(nc.sbuf_tensor("sc_buf", [128, S], F32))
                p32e = E3(nc.sbuf_tensor("p32e", [128, 512], F32))
                p32e2 = E3(nc.sbuf_tensor("p32e2", [128, 512], F32))
                PT_hi = E3(nc.sbuf_tensor("PT_hi", [128, 512], BF16))
                PT_hi2 = E3(nc.sbuf_tensor("PT_hi2", [128, 512], BF16))
                PT_lo = E3(nc.sbuf_tensor("PT_lo", [128, 512], BF16))
                PT_lo2 = E3(nc.sbuf_tensor("PT_lo2", [128, 512], BF16))
                mxc = E3(nc.sbuf_tensor("mxc", [128, 16], F32))
                smc = E3(nc.sbuf_tensor("smc", [128, 16], F32))
                mneg = E3(nc.sbuf_tensor("mneg", [128, 1], F32))
                rT = E3(nc.sbuf_tensor("rT", [128, 1], F32))
                OTs = E3(nc.sbuf_tensor("OTs", [128, 128], F32))
                Ofin = E3(nc.sbuf_tensor("Ofin", [128, 128], F32))

                ev_headdone = None
                evs_evc = []
                trE = []
                ploE = []
                pvE = []
                for hd_i in range(4):
                    for et in range(3):
                        def ldw(sync, hd_i=hd_i, et=et):
                            if et == 0 and hd_i == 0:
                                _wait(sync, ev_allst[0])
                                _wait(sync, ev_allst[1])
                                _wait(sync, ev_mod)
                            elif et == 0:
                                _wait(sync, ev_headdone)
                            else:
                                _wait(sync, evs_wt[-1])
                            ch["LD0"].inc(sync.dma_start(
                                out=wqNat[:],
                                in_=wq[384 * hd_i + 128 * et:
                                       384 * hd_i + 128 * et + 128, :]), 16)
                        block.sync(ldw)
                        ev_wq = ch["LD0"].ev()
                        for g0 in range(0, NDC, 4):
                            def t_tw(tensor, g0=g0, ev_wq=ev_wq, first=(et == 0 and
                                     g0 == 0 and hd_i == 0)):
                                _wait(tensor, ev_wq)
                                if not first:
                                    _wait(tensor, evs_wt[-1])
                                last = None
                                for k in range(4):
                                    last = tensor.transpose(
                                        psT_[:, 128 * k:128 * k + 128],
                                        wqNat[:, (g0 + k) * 128:(g0 + k) * 128 + 128],
                                        ident[:])
                                T.inc(last)
                            if et == 0 and g0 == 0 and hd_i == 0:
                                evs_wt = []
                            block.tensor(t_tw)
                            ev_t = T.ev()

                            def v_tw(vector, et=et, g0=g0, ev_t=ev_t):
                                _wait(vector, ev_t)
                                for k in range(4):
                                    dc = g0 + k
                                    o = 384 * dc + 128 * et
                                    vector.tensor_copy(wqT_hi[:, o:o + 128],
                                                       psT_[:, 128 * k:128 * k + 128])
                                    vector.tensor_tensor(tmpf[:, 0:128],
                                                         psT_[:, 128 * k:128 * k + 128],
                                                         wqT_hi[:, o:o + 128], OP.subtract)
                                    vector.tensor_copy(wqT_lo[:, o:o + 128], tmpf[:, 0:128])
                                V.inc(vector.engine_nop())
                            block.vector(v_tw)
                            evs_wt.append(V.ev())
                    ev_wqT = V.ev()

                    evs_qkv_done = []
                    for tt in range(NT):
                        par = tt % 2

                        def ldh(sync, tt=tt, par=par, hd_i=hd_i):
                            if tt >= 2:
                                _wait(sync, evs_qkv_done[tt - 2])
                            elif hd_i == 0:
                                _wait(sync, ev_allst[0])
                                _wait(sync, ev_allst[1])
                                _wait(sync, ev_mod)
                            ch["LH%d" % par].inc(sync.dma_start(
                                out=hh[par][:], in_=h1T_hi_d[tt, :, :]), 16)
                            ch["LH%d" % par].inc(sync.dma_start(
                                out=hl[par][:], in_=h1T_lo_d[tt, :, :]), 16)
                        block.sync(ldh)
                        ev_h1 = ch["LH%d" % par].ev()

                        psQb = psQ if tt % 2 == 0 else psM

                        def t_qkv(tensor, tt=tt, par=par, ev_h1=ev_h1, psQb=psQb):
                            _wait(tensor, ev_h1)
                            _wait(tensor, ev_wqT)
                            if tt >= 2:
                                _wait(tensor, evs_evac[tt - 2])
                            last = None
                            for dc in range(NDC):
                                a = hh[par][:, 128 * dc:128 * dc + 128]
                                b_ = hl[par][:, 128 * dc:128 * dc + 128]
                                wh = wqT_hi[:, 384 * dc:384 * dc + 384]
                                wl = wqT_lo[:, 384 * dc:384 * dc + 384]
                                last = tensor.matmul(psQb[:, 0:384], a, wh,
                                                     start=(dc == 0), stop=False)
                                last = tensor.matmul(psQb[:, 0:384], a, wl,
                                                     start=False, stop=False)
                                last = tensor.matmul(psQb[:, 0:384], b_, wh,
                                                     start=False,
                                                     stop=(dc == NDC - 1))
                            T.inc(last)
                        if tt == 0:
                            evs_evac = []
                        block.tensor(t_qkv)
                        ev_mm = T.ev()

                        def v_ev(vector, tt=tt, hd_i=hd_i, ev_mm=ev_mm, psQb=psQb):
                            _wait(vector, ev_mm)
                            vector.tensor_tensor(qkvS[:], psQb[:, 0:384],
                                                 bqB[:, 384 * hd_i:384 * hd_i + 384], OP.add)
                            cosL = cosTab[:, 64 * tt:64 * tt + 64]
                            sinL = sinTab[:, 64 * tt:64 * tt + 64]
                            for base in (0, 128):
                                src = qkvS[:, base:base + 128]
                                vector.tensor_mul(dump[:, 0:64], src[:, 0:64], cosL)
                                vector.tensor_mul(dump[:, 64:128], src[:, 64:128], cosL)
                                vector.tensor_mul(dump[:, 128:192], src[:, 64:128], sinL)
                                vector.tensor_mul(dump[:, 192:256], src[:, 0:64], sinL)
                                vector.tensor_tensor(qrot[:, base:base + 64],
                                                     dump[:, 0:64], dump[:, 128:192],
                                                     OP.subtract)
                                vector.tensor_tensor(qrot[:, base + 64:base + 128],
                                                     dump[:, 64:128], dump[:, 192:256],
                                                     OP.add)
                            vector.tensor_copy(v_hi[:, 128 * tt:128 * tt + 128],
                                               qkvS[:, 256:384])
                            vector.tensor_tensor(tmpf[:, 0:128], qkvS[:, 256:384],
                                                 v_hi[:, 128 * tt:128 * tt + 128],
                                                 OP.subtract)
                            V.inc(vector.tensor_copy(v_lo[:, 128 * tt:128 * tt + 128],
                                                     tmpf[:, 0:128]))
                        block.vector(v_ev)
                        ev_rope = V.ev()
                        evs_evac.append(ev_rope)

                        def t_tq(tensor, tt=tt, ev_rope=ev_rope):
                            _wait(tensor, ev_rope)
                            if tt >= 1:
                                _wait(tensor, evs_qkv_done[-1])
                            tensor.transpose(psT_[:, 0:128], qrot[:, 0:128], ident[:])
                            T.inc(tensor.transpose(psT_[:, 128:256], qrot[:, 128:256],
                                                   ident[:]))
                        block.tensor(t_tq)
                        ev_tq = T.ev()

                        def v_tq(vector, tt=tt, ev_tq=ev_tq):
                            _wait(vector, ev_tq)
                            c = 128 * tt
                            vector.tensor_scalar(tmpf[:, 0:128], psT_[:, 0:128], invB[:, 0:1],
                                                 None, OP.mult)
                            vector.tensor_copy(qT_hi[:, c:c + 128], tmpf[:, 0:128])
                            vector.tensor_tensor(tmpf[:, 128:256], tmpf[:, 0:128],
                                                 qT_hi[:, c:c + 128], OP.subtract)
                            vector.tensor_copy(qT_lo[:, c:c + 128], tmpf[:, 128:256])
                            vector.tensor_copy(kT_hi[:, c:c + 128], psT_[:, 128:256])
                            vector.tensor_tensor(tmpf[:, 128:256], psT_[:, 128:256],
                                                 kT_hi[:, c:c + 128], OP.subtract)
                            V.inc(vector.tensor_copy(kT_lo[:, c:c + 128], tmpf[:, 128:256]))
                        block.vector(v_tq)
                        evs_qkv_done.append(V.ev())
                    ev_qkvall = V.ev()

                    if dbg and hd_i == 0:
                        def g_dbg(gpsimd, ev_qkvall=ev_qkvall):
                            _wait(gpsimd, ev_qkvall)
                            ch["SO"].inc(gpsimd.dma_start(out=qT_dbg[:], in_=qT_hi[:]), 16)
                            ch["SO"].inc(gpsimd.dma_start(out=kT_dbg[:], in_=kT_hi[:]), 16)
                            ch["SO"].inc(gpsimd.dma_start(out=v_dbg[:], in_=v_hi[:]), 16)
                        block.gpsimd(g_dbg)

                    for qt in range(NT):
                        nkc = qt // 4 + 1
                        r = qt % 4
                        ev_sc_free = ch["SC"].ev()
                        for kc in range(nkc):
                            diag = (kc == nkc - 1)
                            psAb = psA if (len(evs_evc) % 2 == 0) else psA2

                            def t_sc(tensor, qt=qt, kc=kc, psAb=psAb,
                                     ev_qkvall=ev_qkvall):
                                _wait(tensor, ev_qkvall)
                                if len(evs_evc) >= 2:
                                    _wait(tensor, evs_evc[-2])
                                q0 = 128 * qt
                                c = 512 * kc
                                tensor.matmul(psAb[:], qT_hi[:, q0:q0 + 128],
                                              kT_hi[:, c:c + 512], start=True, stop=False)
                                tensor.matmul(psAb[:], qT_lo[:, q0:q0 + 128],
                                              kT_hi[:, c:c + 512], start=False, stop=False)
                                T.inc(tensor.matmul(psAb[:], qT_hi[:, q0:q0 + 128],
                                                    kT_lo[:, c:c + 512], start=False,
                                                    stop=True))
                            block.tensor(t_sc)
                            ev_s = T.ev()

                            def v_evac(vector, kc=kc, diag=diag, r=r, ev_s=ev_s,
                                       psAb=psAb, ev_sc_free=ev_sc_free):
                                _wait(vector, ev_s)
                                if kc == 0:
                                    _wait(vector, ev_sc_free)
                                if diag:
                                    vector.tensor_tensor(psAb[:], psAb[:],
                                                         maskA[:, 512 * r:512 * r + 512],
                                                         OP.add)
                                V.inc(vector.tensor_scalar(
                                    sc_buf[:, 512 * kc:512 * kc + 512], psAb[:], 1.0,
                                    -3.0e38, OP.mult, OP.max,
                                    accum_out=mxc[:, kc:kc + 1]))
                            block.vector(v_evac)
                            evs_evc.append(V.ev())

                        def v_mx(vector, nkc=nkc, ev_last=evs_evc[-1]):
                            _wait(vector, ev_last)
                            V.inc(vector.reduce_max(mxc[:, 8:9], mxc[:, 0:nkc], axis=AX.X))
                            _wait(vector, V.ev())
                            V.inc(vector.tensor_scalar_mul(mneg[:], mxc[:, 8:9], -1.0))
                        block.vector(v_mx)
                        ev_mx = V.ev()

                        def _emit_pv(job, nkc=nkc):
                            kc, pth, ptl, ev_lo = job

                            def t_pv(tensor, kc=kc, nkc=nkc, pth=pth, ptl=ptl,
                                     ev_lo=ev_lo):
                                _wait(tensor, ev_lo)
                                last = None
                                for i in range(4):
                                    kt = 4 * kc + i
                                    first = (kc == 0 and i == 0)
                                    fin = (kc == nkc - 1 and i == 3)
                                    c = 128 * kt
                                    tensor.matmul(psO[:, 0:128], v_hi[:, c:c + 128],
                                                  pth[:, 128 * i:128 * i + 128],
                                                  start=first, stop=False)
                                    tensor.matmul(psO[:, 0:128], v_lo[:, c:c + 128],
                                                  pth[:, 128 * i:128 * i + 128],
                                                  start=False, stop=False)
                                    last = tensor.matmul(psO[:, 0:128], v_hi[:, c:c + 128],
                                                         ptl[:, 128 * i:128 * i + 128],
                                                         start=False, stop=fin)
                                T.inc(last)
                            block.tensor(t_pv)
                            pvE.append(T.ev())

                        pend = None
                        for kc in range(nkc):
                            ci = len(trE)
                            par2 = ci % 2
                            peb = p32e if par2 == 0 else p32e2
                            pth = PT_hi if par2 == 0 else PT_hi2
                            ptl = PT_lo if par2 == 0 else PT_lo2
                            psTb = psT_ if par2 == 0 else psT2_

                            def s_exp(scalar, kc=kc, ci=ci, peb=peb, ev_mx=ev_mx):
                                _wait(scalar, ev_mx)
                                if ci >= 2:
                                    _wait(scalar, trE[ci - 2])
                                SC.inc(scalar.activation(
                                    peb[:], sc_buf[:, 512 * kc:512 * kc + 512],
                                    ACT.Exp, bias=mneg[:, 0:1],
                                    accum_out=smc[:, kc:kc + 1]))
                            block.scalar(s_exp)
                            ev_e = SC.ev()

                            def t_tr(tensor, ci=ci, peb=peb, psTb=psTb, ev_e=ev_e):
                                _wait(tensor, ev_e)
                                if ci >= 2:
                                    _wait(tensor, ploE[ci - 2])
                                last = None
                                for i in range(4):
                                    last = tensor.transpose(
                                        psTb[:, 128 * i:128 * i + 128],
                                        peb[:, 128 * i:128 * i + 128], ident[:])
                                T.inc(last)
                            block.tensor(t_tr)
                            trE.append(T.ev())
                            ev_tr = trE[-1]

                            def s_hi(scalar, ci=ci, psTb=psTb, pth=pth, ev_tr=ev_tr):
                                _wait(scalar, ev_tr)
                                if ci >= 2:
                                    _wait(scalar, pvE[ci - 2])
                                SC.inc(scalar.activation(pth[:], psTb[:, 0:512],
                                                         ACT.Copy, bias=0.0))
                            block.scalar(s_hi)
                            ev_hi = SC.ev()

                            def v_plo(vector, psTb=psTb, pth=pth, ptl=ptl,
                                      ev_hi=ev_hi):
                                _wait(vector, ev_hi)
                                V.inc(vector.tensor_tensor(ptl[:], psTb[:, 0:512],
                                                           pth[:], OP.subtract))
                            block.vector(v_plo)
                            ploE.append(V.ev())
                            ev_lo = ploE[-1]

                            if pend is not None:
                                _emit_pv(pend)
                            pend = (kc, pth, ptl, ev_lo)
                        _emit_pv(pend)
                        ev_pv_last = pvE[-1]

                        def v_fin(vector, nkc=nkc, ev_sc=ch["SC"].ev(),
                                  ev_pv=ev_pv_last):
                            _wait(vector, ev_sc)
                            V.inc(vector.reduce_sum(smc[:, 8:9], smc[:, 0:nkc], axis=AX.X))
                            _wait(vector, V.ev())
                            V.inc(vector.reciprocal(rT[:], smc[:, 8:9]))
                            _wait(vector, V.ev())
                            _wait(vector, ev_pv)
                            V.inc(vector.tensor_copy(OTs[:], psO[:, 0:128]))
                        block.vector(v_fin)
                        ev_ots = V.ev()

                        def t_to(tensor, ev_ots=ev_ots):
                            _wait(tensor, ev_ots)
                            T.inc(tensor.transpose(psB[:, 0:128], OTs[:], ident[:]))
                        block.tensor(t_to)
                        ev_to = T.ev()

                        def v_of(vector, ev_to=ev_to, ev_so=ch["SO"].ev()):
                            _wait(vector, ev_to)
                            _wait(vector, ev_so)
                            V.inc(vector.tensor_scalar(Ofin[:], psB[:, 0:128], rT[:, 0:1],
                                                       None, OP.mult))
                        block.vector(v_of)
                        ev_of = V.ev()

                        def g_of(gpsimd, qt=qt, hd_i=hd_i, ev_of=ev_of):
                            _wait(gpsimd, ev_of)
                            ch["SO"].inc(gpsimd.dma_start(
                                out=O_out[128 * qt:128 * qt + 128,
                                          128 * hd_i:128 * hd_i + 128],
                                in_=Ofin[:]), 16)
                        block.gpsimd(g_of)
                    ev_headdone = ch["SO"].ev()

        def g_ag1(gpsimd):
            _wait(gpsimd, ch["SO"].ev())
            CC.inc(gpsimd.collective_compute(
                "AllGather", OP.bypass, replica_groups=CC_GROUPS,
                ins=[O_out.ap().opt()], outs=[O_all.ap().opt()]), 1)
        block.gpsimd(g_ag1)
        ev_ag1 = CC.ev()

        with ExitStack() as st2:
            E = st2.enter_context
            gaB = E(nc.sbuf_tensor("gaB", [128, D], F32))
            cbB = E(nc.sbuf_tensor("cbB", [128, D], F32))
            sbB = E(nc.sbuf_tensor("sbB", [128, D], F32))
            rga = E(nc.sbuf_tensor("rga", [1, D], F32))
            rcl = E(nc.sbuf_tensor("rcl", [1, D], F32))
            rsb = E(nc.sbuf_tensor("rsb", [1, D], F32))
            rw1 = E(nc.sbuf_tensor("rw1", [1, D], F32))
            rw2 = E(nc.sbuf_tensor("rw2", [1, D], F32))
            xb2 = [E(nc.sbuf_tensor("xb2%d" % i, [128, D], F32)) for i in range(2)]
            ob2 = [E(nc.sbuf_tensor("ob2%d" % i, [128, D], F32)) for i in range(2)]
            x1b = E(nc.sbuf_tensor("x1b", [128, D], F32))
            xc2 = E(nc.sbuf_tensor("xc2", [128, D], F32))
            hb2 = E(nc.sbuf_tensor("hb2", [128, D], F32))
            h2s = [(E(nc.sbuf_tensor("h2sh%d" % i, [128, D], BF16)),
                    E(nc.sbuf_tensor("h2sl%d" % i, [128, D], BF16)))
                   for i in range(2)]
            psT2 = E(nc.psum_tensor("psT2", [128, 512], F32))

            def ld_r2(sync):
                _wait(sync, ev_ag1)
                ch["LD0"].inc(sync.dma_start(out=rga[0:1, 0:D],
                                             in_=mod_out[0:1, 2 * D:3 * D]), 16)
                ch["LD0"].inc(sync.dma_start(out=rcl[0:1, 0:D],
                                             in_=mod_out[0:1, 4 * D:5 * D]), 16)
                ch["LD0"].inc(sync.dma_start(out=rsb[0:1, 0:D],
                                             in_=mod_out[0:1, 3 * D:4 * D]), 16)
            block.sync(ld_r2)
            ev_r2 = ch["LD0"].ev()

            def v_w2(vector):
                _wait(vector, ev_r2)
                V.inc(vector.engine_nop())
            block.vector(v_w2)
            ev_ga2 = emit_clamp_row(nc, block, ch, rga, 0, rw1, rw2)
            ev_gaB = emit_bcast_row(nc, block, ch, onesR, rw1, 0, gaB, D, psB,
                                    ev_ga2)

            def v_cb2(vector):
                _wait(vector, ev_gaB)
                V.inc(vector.tensor_scalar_add(rw1[0:1, 0:D], rcl[0:1, 0:D], 1.0))
            block.vector(v_cb2)
            ev_cbB = emit_bcast_row(nc, block, ch, onesR, rw1, 0, cbB, D, psB,
                                    V.ev())
            ev_sbB = emit_bcast_row(nc, block, ch, onesR, rsb, 0, sbB, D, psB,
                                    ev_r2)

            evs2_cons = []
            evs2_st = []
            for tt in range(NT):
                par = tt % 2

                def ldx2(sync, tt=tt, par=par):
                    if tt >= 2:
                        _wait(sync, evs2_cons[tt - 2])
                    else:
                        _wait(sync, ev_sbB)
                    ch["LX%d" % par].inc(sync.dma_start(
                        out=xb2[par][:], in_=xb[128 * tt:128 * tt + 128, :]), 16)
                    for k in range(4):
                        ch["LX%d" % par].inc(sync.dma_start(
                            out=ob2[par][:, 512 * k:512 * k + 512],
                            in_=O_all[k, 128 * tt:128 * tt + 128, :]), 16)
                block.sync(ldx2)
                ev_x = ch["LX%d" % par].ev()

                def v_x12(vector, par=par, ev_x=ev_x, ev_so=ch["SO"].ev()):
                    _wait(vector, ev_x)
                    _wait(vector, ev_cbB)
                    _wait(vector, ev_so)
                    vector.tensor_tensor(x1b[:], ob2[par][:], gaB[:], OP.mult)
                    V.inc(vector.tensor_add(x1b[:], x1b[:], xb2[par][:]))
                block.vector(v_x12)
                ev_x1 = V.ev()

                def g_x12(gpsimd, tt=tt, ev_x1=ev_x1):
                    _wait(gpsimd, ev_x1)
                    ch["SO"].inc(gpsimd.dma_start(
                        out=x1_own[128 * tt:128 * tt + 128, :], in_=x1b[:]), 16)
                block.gpsimd(g_x12)

                ev_h = emit_ln_h(nc, block, ch, x1b, stats, xc2, hb2, hb2,
                                 cbB, sbB, SC, ev_sq=ch["T"].ev())
                evs2_cons.append(ev_h)
                if tt >= 2:
                    block.vector(lambda vector, tt=tt: _wait(vector,
                                                             evs2_st[tt - 2]))
                ev_sp = emit_transpose_split(nc, block, ch, hb2, D, ident, psT2,
                                             tmpf, h2s[par][0], h2s[par][1], 0,
                                             ev_pre=ev_h)

                def sh2(gpsimd, tt=tt, par=par, ev_sp=ev_sp):
                    _wait(gpsimd, ev_sp)
                    for dc in range(NDC):
                        ch["SH%d" % par].inc(gpsimd.dma_start(
                            out=h2hi_own[dc, :, 128 * tt:128 * tt + 128],
                            in_=h2s[par][0][:, 128 * dc:128 * dc + 128]), 16)
                        ch["SH%d" % par].inc(gpsimd.dma_start(
                            out=h2lo_own[dc, :, 128 * tt:128 * tt + 128],
                            in_=h2s[par][1][:, 128 * dc:128 * dc + 128]), 16)
                block.gpsimd(sh2)
                evs2_st.append(ch["SH%d" % par].ev())
        ev_ph2 = [ch["SH0"].ev(), ch["SH1"].ev(), ch["SO"].ev()]

        with ExitStack() as st3:
            E3o = st3.enter_context
            lutB = E3o(nc.sbuf_tensor("lutB", [128, 16], F32))
            mskj = E3o(nc.sbuf_tensor("mskj", [128, NJT], F32))
            gbB = E3o(nc.sbuf_tensor("gbB", [128, D], F32))

            def ld3(sync):
                for e in ev_ph2:
                    _wait(sync, e)
                ch["LD0"].inc(sync.dma_start(out=lutB[0:1, :], in_=lutr[:]), 16)
                ch["LD0"].inc(sync.dma_start(out=mskj[:], in_=maskj[:]), 16)
                ch["LD0"].inc(sync.dma_start(out=rw1_3[0:1, 0:D],
                                             in_=mod_out[0:1, 5 * D:6 * D]), 16)
            ev_lutB = None

            with ExitStack() as stg:
                E2 = stg.enter_context
                rw1_3 = E2(nc.sbuf_tensor("rw1_3", [1, D], F32))
                rw2_3 = E2(nc.sbuf_tensor("rw2_3", [1, D], F32))
                rw3_3 = E2(nc.sbuf_tensor("rw3_3", [1, D], F32))
                idxb = [E2(nc.sbuf_tensor("idxb%d" % i, [128, 2048], I32))
                        for i in range(2)]
                acc = E2(nc.sbuf_tensor("acc", [128, 2048], F32))
                gtmp = E2(nc.sbuf_tensor("gtmp", [128, 2048], F32))
                whi = [E2(nc.sbuf_tensor("whi%d" % i, [128, 2048], BF16))
                       for i in range(2)]
                wlo = [E2(nc.sbuf_tensor("wlo%d" % i, [128, 2048], BF16))
                       for i in range(2)]
                psT3 = E2(nc.psum_tensor("psT3g", [128, 512], F32))

                block.sync(ld3)
                ev_ld3 = ch["LD0"].ev()

                def v03(vector):
                    _wait(vector, ev_ld3)
                    V.inc(vector.engine_nop())
                block.vector(v03)
                ev_lutB = emit_bcast_row(nc, block, ch, onesR, lutB, 0, lutB, 16,
                                         psB, V.ev())
                ev_gb = emit_clamp_row(nc, block, ch, rw1_3, 0, rw2_3, rw3_3)
                ev_gbB = emit_bcast_row(nc, block, ch, onesR, rw2_3, 0, gbB, D,
                                        psB, ev_gb)

                jobs = []
                for jt in range(NJT):
                    jobs.append((gidx, 128 * jt, D, wgh_d, wgl_d, jt, jt))
                for jt in range(NJT):
                    jobs.append((uidx, 128 * jt, D, wuh_d, wul_d, jt, jt))
                for ot in range(NDC):
                    jobs.append((didx, 128 * ot, JSL, wdh_d, wdl_d, ot, None))

                evs_jdone = []
                for ji, (srcd, r0, wdt, dsth, dstl, blk, mcol) in enumerate(jobs):
                    par = ji % 2

                    def ldi(sync, ji=ji, par=par, srcd=srcd, r0=r0, wdt=wdt):
                        if ji >= 2:
                            _wait(sync, evs_jdone[ji - 2])
                        else:
                            _wait(sync, ev_gbB)
                        ch["LI%d" % par].inc(sync.dma_start(
                            out=idxb[par][:, 0:wdt], in_=srcd[r0:r0 + 128, 0:wdt]),
                            16)
                    block.sync(ldi)
                    ev_i = ch["LI%d" % par].ev()

                    def v_g(vector, ji=ji, par=par, wdt=wdt, mcol=mcol, ev_i=ev_i,
                            ev_tprev=ch["T"].ev()):
                        _wait(vector, ev_i)
                        _wait(vector, ev_tprev)
                        src = idxb[par][:, 0:wdt]
                        vector.tensor_scalar(acc[:, 0:wdt], src, 0.0, lutB[:, 0:1],
                                             OP.is_equal, OP.mult)
                        for v in range(1, 16):
                            vector.tensor_scalar(gtmp[:, 0:wdt], src, float(v),
                                                 lutB[:, v:v + 1], OP.is_equal,
                                                 OP.mult)
                            vector.tensor_tensor(acc[:, 0:wdt], acc[:, 0:wdt],
                                                 gtmp[:, 0:wdt], OP.add)
                        if mcol is not None:
                            vector.tensor_scalar(acc[:, 0:wdt], acc[:, 0:wdt],
                                                 mskj[:, mcol:mcol + 1], None,
                                                 OP.mult)
                        V.inc(vector.engine_nop())
                    block.vector(v_g)
                    ev_acc = V.ev()

                    nt_ = wdt // 128
                    for g0 in range(0, nt_, 4):
                        gn = min(4, nt_ - g0)

                        def t_g(tensor, g0=g0, gn=gn, ev_acc=ev_acc, ji=ji):
                            _wait(tensor, ev_acc)
                            if g0 > 0:
                                _wait(tensor, evs_gt[-1])
                            elif ji >= 2:
                                _wait(tensor, evs_jdone[ji - 2])
                            last = None
                            for k in range(gn):
                                last = tensor.transpose(
                                    psT3[:, 128 * k:128 * k + 128],
                                    acc[:, (g0 + k) * 128:(g0 + k) * 128 + 128],
                                    ident[:])
                            T.inc(last)
                        if g0 == 0:
                            evs_gt = []
                        block.tensor(t_g)
                        ev_t = T.ev()

                        def v_t(vector, g0=g0, gn=gn, par=par, ev_t=ev_t):
                            _wait(vector, ev_t)
                            w = 128 * gn
                            o = 128 * g0
                            vector.tensor_copy(whi[par][:, o:o + w], psT3[:, 0:w])
                            vector.tensor_tensor(tmpf[:, 0:w], psT3[:, 0:w],
                                                 whi[par][:, o:o + w], OP.subtract)
                            V.inc(vector.tensor_copy(wlo[par][:, o:o + w],
                                                     tmpf[:, 0:w]))
                        block.vector(v_t)
                        evs_gt.append(V.ev())
                    ev_sp3 = V.ev()

                    def g_w(gpsimd, ji=ji, par=par, dsth=dsth, dstl=dstl, blk=blk,
                            wdt=wdt, ev_sp3=ev_sp3):
                        _wait(gpsimd, ev_sp3)
                        ch["SW%d" % par].inc(gpsimd.dma_start(
                            out=dsth[blk, :, 0:wdt], in_=whi[par][:, 0:wdt]), 16)
                        ch["SW%d" % par].inc(gpsimd.dma_start(
                            out=dstl[blk, :, 0:wdt], in_=wlo[par][:, 0:wdt]), 16)
                    block.gpsimd(g_w)
                    evs_jdone.append(ch["SW%d" % par].ev())

            ev_gall = [ch["SW0"].ev(), ch["SW1"].ev()]

            TBW = 512
            NTB = S // TBW
            NTTL = TBW // 128
            with ExitStack() as stf:
                E3 = stf.enter_context
                h2h = E3(nc.sbuf_tensor("h2h", [128, NDC * TBW], BF16))
                h2l = E3(nc.sbuf_tensor("h2l", [128, NDC * TBW], BF16))
                wb_h = [E3(nc.sbuf_tensor("wbh%d" % i, [128, D], BF16))
                        for i in range(2)]
                wb_l = [E3(nc.sbuf_tensor("wbl%d" % i, [128, D], BF16))
                        for i in range(2)]
                wub_h = [E3(nc.sbuf_tensor("wubh%d" % i, [128, D], BF16))
                         for i in range(2)]
                wub_l = [E3(nc.sbuf_tensor("wubl%d" % i, [128, D], BF16))
                         for i in range(2)]
                gsil = E3(nc.sbuf_tensor("gsil", [128, 512], F32))
                hfh = E3(nc.sbuf_tensor("hfh", [128, NJT * TBW], BF16))
                hfl = E3(nc.sbuf_tensor("hfl", [128, NJT * TBW], BF16))
                wdhs = E3(nc.sbuf_tensor("wdhs", [128, NDC * JSL], BF16))
                wdls = E3(nc.sbuf_tensor("wdls", [128, NDC * JSL], BF16))
                partS = E3(nc.sbuf_tensor("partS", [128, D], F32))
                x1f = E3(nc.sbuf_tensor("x1f", [128, D], F32))
                psG = E3(nc.psum_tensor("psG", [128, 512], F32))
                psU = E3(nc.psum_tensor("psU", [128, 512], F32))
                psD = [E3(nc.psum_tensor("psD%d" % i, [128, 512], F32))
                       for i in range(4)]

                def ldwd(sync):
                    _wait(sync, ev_gall[0])
                    _wait(sync, ev_gall[1])
                    for ot in range(NDC):
                        ch["LH0"].inc(sync.dma_start(
                            out=wdhs[:, JSL * ot:JSL * ot + JSL],
                            in_=wdh_d[ot, :, 0:JSL]), 16)
                        ch["LH0"].inc(sync.dma_start(
                            out=wdls[:, JSL * ot:JSL * ot + JSL],
                            in_=wdl_d[ot, :, 0:JSL]), 16)
                block.sync(ldwd)
                ev_wd = ch["LH0"].ev()

                for tb in range(NTB):
                    def ldh2(sync, tb=tb):
                        if tb == 0:
                            _wait(sync, ev_gall[0])
                            _wait(sync, ev_gall[1])
                        else:
                            _wait(sync, evs_tb[-1])
                        for dc in range(NDC):
                            ch["LH0"].inc(sync.dma_start(
                                out=h2h[:, TBW * dc:TBW * dc + TBW],
                                in_=h2hi_own[dc, :, TBW * tb:TBW * tb + TBW]), 16)
                            ch["LH0"].inc(sync.dma_start(
                                out=h2l[:, TBW * dc:TBW * dc + TBW],
                                in_=h2lo_own[dc, :, TBW * tb:TBW * tb + TBW]), 16)
                    if tb == 0:
                        evs_tb = []
                    block.sync(ldh2)
                    ev_h2 = ch["LH0"].ev()

                    evs_hf = []
                    for jt in range(NJT):
                        par = jt % 2

                        def ldwg(sync, jt=jt, par=par, tb=tb):
                            if jt >= 2:
                                _wait(sync, evs_hf[jt - 2])
                            elif tb > 0:
                                _wait(sync, evs_hf_prev[-1])
                            ch["LW%d" % par].inc(sync.dma_start(
                                out=wb_h[par][:], in_=wgh_d[jt, :, :]), 16)
                            ch["LW%d" % par].inc(sync.dma_start(
                                out=wb_l[par][:], in_=wgl_d[jt, :, :]), 16)
                            ch["LW%d" % par].inc(sync.dma_start(
                                out=wub_h[par][:], in_=wuh_d[jt, :, :]), 16)
                            ch["LW%d" % par].inc(sync.dma_start(
                                out=wub_l[par][:], in_=wul_d[jt, :, :]), 16)
                        block.sync(ldwg)
                        ev_w = ch["LW%d" % par].ev()

                        def t_gu(tensor, jt=jt, par=par, ev_w=ev_w, ev_h2=ev_h2,
                                 tb=tb):
                            _wait(tensor, ev_w)
                            _wait(tensor, ev_h2)
                            if jt > 0:
                                _wait(tensor, evs_guev[-1])
                            elif tb > 0:
                                _wait(tensor, evs_guev_prev[-1])
                            last = None
                            for dc in range(NDC):
                                rh = h2h[:, TBW * dc:TBW * dc + TBW]
                                rl = h2l[:, TBW * dc:TBW * dc + TBW]
                                for (ps, wh_, wl_) in ((psG, wb_h[par], wb_l[par]),
                                                       (psU, wub_h[par],
                                                        wub_l[par])):
                                    a = wh_[:, 128 * dc:128 * dc + 128]
                                    bb = wl_[:, 128 * dc:128 * dc + 128]
                                    tensor.matmul(ps[:], a, rh, start=(dc == 0),
                                                  stop=False)
                                    tensor.matmul(ps[:], a, rl, start=False,
                                                  stop=False)
                                    last = tensor.matmul(ps[:], bb, rh, start=False,
                                                         stop=(dc == NDC - 1))
                            T.inc(last)
                        if jt == 0:
                            evs_guev_prev = evs_guev if tb > 0 else None
                            evs_guev = []
                        block.tensor(t_gu)
                        ev_mm = T.ev()

                        def s_si(scalar, ev_mm=ev_mm):
                            _wait(scalar, ev_mm)
                            SC.inc(scalar.activation(gsil[:], psG[:], ACT.Silu))
                        block.scalar(s_si)
                        ev_si = SC.ev()

                        def v_hf(vector, jt=jt, ev_si=ev_si, ev_tdn=ch["T"].ev()):
                            _wait(vector, ev_si)
                            if jt == 0:
                                _wait(vector, ev_tdn)
                            o = TBW * jt
                            vector.tensor_tensor(tmpf[:, 0:512], psU[:], gsil[:],
                                                 OP.mult)
                            vector.tensor_copy(hfh[:, o:o + 512], tmpf[:, 0:512])
                            vector.tensor_tensor(dump[:, 0:512], tmpf[:, 0:512],
                                                 hfh[:, o:o + 512], OP.subtract)
                            V.inc(vector.tensor_copy(hfl[:, o:o + 512],
                                                     dump[:, 0:512]))
                        block.vector(v_hf)
                        evs_guev.append(V.ev())
                        evs_hf.append(V.ev())
                    evs_hf_prev = evs_hf
                    ev_hf_all = V.ev()

                    for ttl in range(NTTL):
                        def ldx3(sync, tb=tb, ttl=ttl, ev0=ev_gbB):
                            if tb == 0 and ttl == 0:
                                _wait(sync, ev0)
                            else:
                                _wait(sync, evs_dn[-1])
                            r = TBW * tb + 128 * ttl
                            ch["LX0"].inc(sync.dma_start(
                                out=x1f[:], in_=x1_own[r:r + 128, :]), 16)
                        if ttl == 0 and tb == 0:
                            evs_dn = []
                        block.sync(ldx3)
                        ev_xf = ch["LX0"].ev()

                        def t_dn(tensor, ttl=ttl, tb=tb, ev_hf_all=ev_hf_all):
                            _wait(tensor, ev_hf_all)
                            _wait(tensor, ev_wd)
                            if evs_dn:
                                _wait(tensor, evs_dn[-1])
                            last = None
                            for ot in range(NDC):
                                ps = psD[ot // 4]
                                oc = 128 * (ot % 4)
                                for jt in range(NJT):
                                    sh = hfh[:, TBW * jt + 128 * ttl:
                                              TBW * jt + 128 * ttl + 128]
                                    sl = hfl[:, TBW * jt + 128 * ttl:
                                              TBW * jt + 128 * ttl + 128]
                                    rh = wdhs[:, JSL * ot + 128 * jt:
                                              JSL * ot + 128 * jt + 128]
                                    rl = wdls[:, JSL * ot + 128 * jt:
                                              JSL * ot + 128 * jt + 128]
                                    first = (jt == 0)
                                    lastf = (jt == NJT - 1)
                                    tensor.matmul(ps[:, oc:oc + 128], sh, rh,
                                                  start=first, stop=False)
                                    tensor.matmul(ps[:, oc:oc + 128], sh, rl,
                                                  start=False, stop=False)
                                    last = tensor.matmul(ps[:, oc:oc + 128], sl, rh,
                                                         start=False, stop=lastf)
                            T.inc(last)
                        block.tensor(t_dn)
                        ev_dm = T.ev()

                        def v_dn(vector, ttl=ttl, ev_dm=ev_dm, ev_xf=ev_xf,
                                 ev_so=ch["SO"].ev()):
                            _wait(vector, ev_dm)
                            _wait(vector, ev_xf)
                            _wait(vector, ev_so)
                            for i in range(4):
                                vector.tensor_tensor(
                                    partS[:, 512 * i:512 * i + 512], psD[i][:],
                                    gbB[:, 512 * i:512 * i + 512], OP.mult)
                                vector.scalar_tensor_tensor(
                                    partS[:, 512 * i:512 * i + 512],
                                    x1f[:, 512 * i:512 * i + 512], 0.25,
                                    partS[:, 512 * i:512 * i + 512],
                                    OP.mult, OP.add)
                            V.inc(vector.engine_nop())
                        block.vector(v_dn)
                        evs_dn.append(V.ev())

                        def g_dn(gpsimd, ttl=ttl, tb=tb, ev_v=evs_dn[-1]):
                            _wait(gpsimd, ev_v)
                            r = TBW * tb + 128 * ttl
                            ch["SO"].inc(gpsimd.dma_start(
                                out=c_own[r:r + 128, :], in_=partS[:]), 16)
                        block.gpsimd(g_dn)
                    evs_tb.append(V.ev())

        def g_rs(gpsimd):
            _wait(gpsimd, ch["SO"].ev())
            CC.inc(gpsimd.collective_compute(
                "ReduceScatter", OP.add, replica_groups=CC_GROUPS,
                ins=[c_own.ap().opt()], outs=[y_q.ap().opt()]), 1)
            _wait(gpsimd, CC.ev())
            ch["SO"].inc(gpsimd.dma_start(out=y_out[:, :], in_=y_q[:, :]), 16)
            _wait(gpsimd, ch["SO"].ev())
        block.gpsimd(g_rs)
    return nc


def kernel(L0_attn_hidden_states, L0_attn_w_qkv, L0_attn_b_qkv, L0_ffn_lut,
           L0_ffn_gate_idx, L0_ffn_up_idx, L0_ffn_down_idx, L0_c, L0_wmod):
    x = np.ascontiguousarray(L0_attn_hidden_states)
    wqkv = np.ascontiguousarray(L0_attn_w_qkv)
    bqkv = np.ascontiguousarray(L0_attn_b_qkv)
    lut = np.ascontiguousarray(L0_ffn_lut)
    gi = np.ascontiguousarray(L0_ffn_gate_idx)
    ui = np.ascontiguousarray(L0_ffn_up_idx)
    di = np.ascontiguousarray(L0_ffn_down_idx)
    c = np.ascontiguousarray(L0_c)
    wmod = np.ascontiguousarray(L0_wmod)
    wmodR = _wmod_relayout(wmod)
    tb = _host_tables()
    cores = list(range(8))
    jsl = _jslices()

    nc = build_fused()
    maps = []
    for cc in cores:
        b, g = cc // 4, cc % 4
        o0, osz = jsl[g]
        gip = np.zeros((JSL, D), dtype=np.int32)
        uip = np.zeros((JSL, D), dtype=np.int32)
        dip = np.zeros((D, JSL), dtype=np.int32)
        gip[:osz] = gi[o0:o0 + osz]
        uip[:osz] = ui[o0:o0 + osz]
        dip[:, :osz] = di[:, o0:o0 + osz]
        mj = np.zeros((128, NJT), dtype=np.float32)
        mjf = np.zeros(JSL, dtype=np.float32)
        mjf[:osz] = 1.0
        mj[:, :] = mjf.reshape(NJT, 128).T
        maps.append(dict(
            xb=x[b], wq=wqkv[EH * g:EH * (g + 1)],
            bq=bqkv.reshape(1, EH * 4)[:, EH * g:EH * (g + 1)],
            cCol=np.ascontiguousarray(c[b].reshape(16, 128).T),
            wmod=wmodR, embT=tb["embT"], ksT=tb["ksT"], kcT=tb["kcT"],
            ident=tb["ident"], maskA=tb["maskA"],
            gidx=gip, uidx=uip, didx=dip,
            lutr=lut.reshape(1, 16), maskj=mj,
        ))
    r = run_bass_kernel_spmd(nc, maps, core_ids=cores)
    y = np.empty((B, S, D), dtype=np.float32)
    for cc in cores:
        b, g = cc // 4, cc % 4
        y[b, TQ * g:TQ * (g + 1)] = np.asarray(r.results[cc]["y_out"])
    return y
